# version 18
# baseline (speedup 1.0000x reference)
import os
import numpy as np
import ml_dtypes
from contextlib import ExitStack

import concourse.bass as bass
import concourse.mybir as mybir
from concourse.bass_utils import run_bass_kernel_spmd

F32 = mybir.dt.float32
BF16 = mybir.dt.bfloat16
AF = mybir.ActivationFunctionType
ALU = mybir.AluOpType
AX = mybir.AxisListType
NPBF = ml_dtypes.bfloat16

D = 4096
T = 2048
NK = 4096
H, E = 8, 64
NDC = D // 128
EPS = 1e-5
SQRT_DH = float(np.sqrt(512.0))

ENGS = ("PE", "ACT", "DVE", "POOL", "SP")
DMA_STREAMS = {"ld": 3, "sc": 3, "sca": 2, "out": 2}
DMAS = tuple("%s%d" % (st, i) for st, n in DMA_STREAMS.items() for i in range(n))


class Res(object):
    __slots__ = ("w", "r")

    def __init__(self):
        self.w = None
        self.r = []

    def rd(self):
        return [self.w] if self.w is not None else []

    def wr(self):
        out = list(self.r)
        if self.w is not None:
            out.append(self.w)
        return out

    def did_read(self, tok):
        if tok is not None:
            self.r.append(tok)

    def did_write(self, tok):
        if tok is not None:
            self.w = tok
            self.r = []


class Sched(object):
    def __init__(self, nc):
        self.nc = nc
        self.sems = {}
        for name in ENGS + DMAS:
            self.sems[name] = nc.alloc_semaphore("sem_" + name)
        self.cnt = dict((n, 0) for n in self.sems)
        self.seen = dict((e, {}) for e in ENGS)
        self.blocks = [dict((e, []) for e in ENGS)]
        self.rr = dict((st, 0) for st in DMA_STREAMS)
        self.pending = dict((e, []) for e in ENGS)

    def new_block(self):
        self.blocks.append(dict((e, []) for e in ENGS))
        for e in ENGS:
            self.seen[e] = dict((k, v) for k, v in self.cnt.items() if k in ENGS)

    def phase_barrier(self):
        self.new_block()
        fence = self.dma_fence(*DMA_STREAMS.keys())
        for e in ENGS:
            self.pending[e] = list(fence)

    def dma_fence(self, *streams):
        out = []
        for st in streams:
            for i in range(DMA_STREAMS[st]):
                sn = "%s%d" % (st, i)
                if self.cnt[sn]:
                    out.append((sn, self.cnt[sn]))
        return out

    def op(self, eng, fn, waits=(), dma=None, tok=True, fwaits=()):
        if dma is not None:
            i = self.rr[dma]
            self.rr[dma] = (i + 1) % DMA_STREAMS[dma]
            semname = "%s%d" % (dma, i)
            waits = list(waits) + [(semname, self.cnt[semname])]
        if self.pending[eng]:
            waits = list(waits) + self.pending[eng]
            self.pending[eng] = []
        wl = [wv for wv in fwaits if wv is not None]
        seen = self.seen[eng]
        for wv in waits:
            if wv is None:
                continue
            sn, v = wv
            if v > seen.get(sn, 0) and v > 0:
                wl.append((sn, v))
                seen[sn] = v
        if dma is None:
            semname = eng
            inc = 1
            make_tok = tok
        else:
            inc = 16
            make_tok = True
        post = None
        if make_tok:
            self.cnt[semname] += inc
            post = self.cnt[semname]
        sems = self.sems

        def emit(e, fn=fn, wl=wl, semname=semname, inc=inc, post=post):
            for sn, v in wl:
                e.wait_ge(sems[sn], v)
            ins = fn(e)
            if post is not None:
                ins.then_inc(sems[semname], inc)
        self.blocks[-1][eng].append(emit)
        return (semname, post) if make_tok else None

    def finish(self):
        sems = self.sems
        fence = self.dma_fence("out")

        def emit(e):
            for sn, v in fence:
                e.wait_ge(sems[sn], v)
        self.blocks[-1]["SP"].append(emit)

    def run(self):
        nc = self.nc
        for blk_ops in self.blocks:
            with nc.Block() as blk:
                for eng, meth in (("PE", blk.tensor), ("ACT", blk.scalar),
                                  ("DVE", blk.vector), ("POOL", blk.gpsimd),
                                  ("SP", blk.sync)):
                    ops = blk_ops[eng]
                    if not ops:
                        continue

                    def body(engine, ops=ops):
                        for f in ops:
                            f(engine)
                    meth(body)


def mm(o, l, r, st, sp):
    return lambda e: e.matmul(o, l, r, start=st, stop=sp, skip_group_check=True)


def build_program():
    KPH = int(os.environ.get("KPH", "9"))

    nc = bass.Bass("TRN2", target_bir_lowering=False)

    di = lambda n, sh, dt: nc.dram_tensor(n, sh, dt, kind="ExternalInput")
    x_in = di("xrot", [D, NK], F32)
    wqT = di("wqT", [D, 512], F32)
    wkT = di("wkT", [D, 512], F32)
    wvT = di("wvT", [D, 512], F32)
    woT = di("woT", [512, D], F32)
    wbT = di("wbT", [D, D], F32)
    wrT = di("wrT", [D, 8], F32)
    aallT = di("aallT", [D, 128], F32)
    ballT = di("ballT", [128, D], F32)
    c_bq = di("c_bq", [128, 4], F32)
    c_bk = di("c_bk", [128, 4], F32)
    c_bo = di("c_bo", [128, 32], F32)
    c_bB = di("c_bB", [128, 32], F32)
    c_bv = di("c_bv", [1, 512], F32)
    c_id32 = di("c_id32", [128, 128], F32)
    c_ones65 = di("c_ones65", [65, 128], BF16)
    c_ones1f = di("c_ones1f", [1, 128], F32)
    c_o65f = di("c_o65f", [65, 64], F32)
    c_onescol = di("c_onescol", [128, 2], BF16)
    c_e16 = di("c_e16", [8, 128], BF16)
    c_eps = di("c_eps", [1, 1], F32)

    outT = nc.dram_tensor("outT", [D, T], F32, kind="ExternalOutput")
    xscr = nc.dram_tensor("xscr", [D, NK], BF16, kind="Internal")
    x2scr = nc.dram_tensor("x2scr", [D, T], BF16, kind="Internal")
    r2scr = nc.dram_tensor("r2scr", [D, T], BF16, kind="Internal")
    xscr_r = xscr.ap().rearrange("(c p) t -> p c t", p=128)
    x2scr_r = x2scr.ap().rearrange("(c p) t -> p c t", p=128)
    r2scr_r = r2scr.ap().rearrange("(c p) t -> p c t", p=128)

    s = Sched(nc)
    xscrR = [Res() for _ in range(16)]
    x2scrR = [Res() for _ in range(4)]
    r2scrR = [[Res() for _ in range(NDC)] for _ in range(2)]

    def wload(dst_bf, dstR_list, src_dram, stg, stgR, n_chunks, cw):
        src_r = src_dram.ap().rearrange("(c p) n -> p c n", p=128)
        per = stg[0].shape[1] // cw
        for g in range(n_chunks // per):
            st_i = g % 2
            src = src_r[:, per * g:per * g + per, :]
            tl = s.op("SP", lambda e, d=stg[st_i], src=src, per=per:
                      e.dma_start(d.ap().rearrange("p (c n) -> p c n", c=per),
                                  src),
                      waits=stgR[st_i].wr(), dma="ld")
            stgR[st_i].did_write(tl)
            tc = s.op("POOL",
                      lambda e, d=dst_bf[:, per * g * cw:(per * g + per) * cw],
                      srd=stg[st_i][:, 0:per * cw]: e.tensor_copy(d, srd),
                      waits=[tl])
            stgR[st_i].did_read(tc)
            for j in range(per):
                dstR_list[per * g + j].did_write(tc)

    with ExitStack() as st0:
        sb = lambda n, sh, dt: st0.enter_context(nc.sbuf_tensor(n, sh, dt))
        id32 = sb("id32", [128, 128], F32)
        ones65 = sb("ones65", [65, 128], BF16)
        ones1f = sb("ones1f", [1, 128], F32)
        o65f = sb("o65f", [65, 64], F32)
        onescol = sb("onescol", [128, 2], BF16)
        e16 = sb("e16", [8, 128], BF16)
        bq_sb = sb("bq_sb", [128, 4], F32)
        bk_sb = sb("bk_sb", [128, 4], F32)
        bo_sb = sb("bo_sb", [128, 32], F32)
        bB_sb = sb("bB_sb", [128, 32], F32)
        rows = sb("rows", [1, 4 * 1024], F32)
        epsc = sb("epsc", [1, 1], F32)
        rowsRes = [Res() for _ in range(4)]

        for ap_s, ap_d in ((id32, c_id32), (ones65, c_ones65),
                           (ones1f, c_ones1f), (o65f, c_o65f),
                           (onescol, c_onescol), (e16, c_e16),
                           (bq_sb, c_bq), (bk_sb, c_bk), (bo_sb, c_bo),
                           (bB_sb, c_bB),
                           (epsc, c_eps)):
            s.op("SP", lambda e, d=ap_s, src=ap_d: e.dma_start(d[:], src.ap()),
                 dma="ld")
        consts_fence = s.dma_fence("ld")

        def ln_rows(stat_s, stat_q, statSR, statQR, width, coff):
            ro = lambda i: rows[0:1, i * 1024 + coff: i * 1024 + coff + width]

            def rowop(eng, fn, rds, wrt, extra=()):
                waits = list(extra)
                for i in rds:
                    waits += rowsRes[i].rd()
                waits += rowsRes[wrt].wr()
                t = s.op(eng, fn, waits)
                for i in rds:
                    rowsRes[i].did_read(t)
                rowsRes[wrt].did_write(t)
                return t
            tmu = rowop("DVE", lambda e, d=ro(0), srd=stat_s:
                        e.tensor_scalar(d, srd, 1.0 / D, None, ALU.mult),
                        [], 0, statSR.rd())
            statSR.did_read(tmu)
            tms = rowop("DVE", lambda e, d=ro(1), srd=stat_q:
                        e.tensor_scalar(d, srd, 1.0 / D, None, ALU.mult),
                        [], 1, statQR.rd())
            statQR.did_read(tms)
            rowop("DVE", lambda e, d=ro(2), a=ro(0), b=ro(0):
                  e.tensor_tensor(d, a, b, ALU.mult), [0], 2)
            rowop("DVE", lambda e, d=ro(3), a=ro(1), b=ro(2):
                  e.tensor_tensor(d, a, b, ALU.subtract), [1, 2], 3)
            rowop("ACT", lambda e, d=ro(2), srd=ro(3):
                  e.activation(d, srd, AF.Ln, bias=epsc[:]), [3], 2)
            trs = rowop("ACT", lambda e, d=ro(1), srd=ro(2):
                        e.activation(d, srd, AF.Exp, scale=-0.5), [2], 1)
            rowop("DVE", lambda e, d=ro(3), a=ro(0), b=ro(1):
                  e.tensor_tensor(d, a, b, ALU.mult), [0, 1], 3)
            tb2 = rowop("DVE", lambda e, d=ro(0), srd=ro(3):
                        e.tensor_scalar(d, srd, -1.0, None, ALU.mult), [3], 0)
            return ro, trs, tb2

        with ExitStack() as stA:
            sbA = lambda n, sh, dt: stA.enter_context(nc.sbuf_tensor(n, sh, dt))
            OT = [sbA("OT%d" % i, [128, T], BF16) for i in range(4)]
            OTr = [[Res() for _ in range(4)] for _ in range(4)]
            OTtmp = sbA("OTtmp", [64, T], BF16)
            OTtmpr = [Res() for _ in range(4)]
            wo_bf = sbA("wo_bf", [128, 4 * D], BF16)
            woR = [[Res() for _ in range(4)] for _ in range(4)]

            with ExitStack() as stI:
                sbI = lambda n, sh, dt: stI.enter_context(nc.sbuf_tensor(n, sh, dt))
                kT = sbI("kT", [128, 4 * NK], BF16)
                kTr = [[Res() for _ in range(16)] for _ in range(4)]
                V = sbI("V", [128, 32 * 520], BF16)
                Vr = [Res() for _ in range(32)]
                qT = sbI("qT", [128, 4 * T], BF16)
                qTr = [[Res() for _ in range(4)] for _ in range(4)]

                vones = V.ap().rearrange("p (kc h x) -> p kc h x",
                                         kc=32, h=8)[:, :, :, 64:65]
                t = s.op("POOL", lambda e: e.memset(vones, 1.0),
                         waits=consts_fence)
                for r in Vr:
                    r.did_write(t)

                with ExitStack() as stP:
                    sbP = lambda n, sh, dt: stP.enter_context(nc.sbuf_tensor(n, sh, dt))
                    wk_bf = sbP("wk_bf", [128, 32 * 512], BF16)
                    wkR = [Res() for _ in range(32)]
                    wstg = [sbP("wstg%d" % i, [128, 1024], F32) for i in range(2)]
                    wstgR = [Res(), Res()]
                    xstg = [sbP("xstg%d" % i, [128, 1024], F32) for i in range(3)]
                    xstgR = [Res() for _ in range(3)]
                    xbf = [sbP("xbf%d" % i, [128, 1024], BF16) for i in range(2)]
                    xbfR = [Res() for _ in range(2)]
                    wload(wk_bf, wkR, wkT, wstg, wstgR, 32, 512)
                    with nc.psum_tensor("pk0", [128, 512], F32) as pk0, \
                         nc.psum_tensor("pk1", [128, 512], F32) as pk1, \
                         nc.psum_tensor("pk2", [128, 512], F32) as pk2, \
                         nc.psum_tensor("pk3", [128, 512], F32) as pk3:
                        psK = [pk0, pk1, pk2, pk3]
                        psKR = [Res() for _ in range(4)]
                        for kg in range(16):
                            for g in range(8):
                                bi = (kg * 8 + g) % 3
                                bj = (kg * 8 + g) % 2
                                src = x_in.ap().rearrange("(c p) n -> p c n", p=128)[
                                    :, 4 * g:4 * g + 4, 256 * kg:256 * kg + 256]
                                tl = s.op("SP", lambda e, d=xstg[bi], src=src:
                                          e.dma_start(d.ap().rearrange(
                                              "p (c n) -> p c n", c=4), src),
                                          waits=xstgR[bi].wr(), dma="ld")
                                xstgR[bi].did_write(tl)
                                if g % 2:
                                    tc = s.op("ACT", lambda e, d=xbf[bj][:],
                                              srd=xstg[bi][:]:
                                              e.activation(d, srd, AF.Copy),
                                              waits=[tl] + xbfR[bj].wr())
                                else:
                                    tc = s.op("POOL", lambda e, d=xbf[bj][:],
                                              srd=xstg[bi][:]:
                                              e.tensor_copy(d, srd),
                                              waits=[tl] + xbfR[bj].wr())
                                xstgR[bi].did_read(tc)
                                xbfR[bj].did_write(tc)
                                tw = s.op("ACT", lambda e, kg=kg, g=g, bj=bj:
                                          e.dma_start(
                                              xscr_r[:, 4 * g:4 * g + 4,
                                                     256 * kg:256 * kg + 256],
                                              xbf[bj].ap().rearrange(
                                                  "p (c n) -> p c n", c=4)),
                                          waits=[tc], dma="sca")
                                xbfR[bj].did_read(tw)
                                xscrR[kg].did_write(tw)
                                for j in range(4):
                                    dc = 4 * g + j
                                    for mc in range(4):
                                        tk = s.op("PE", mm(
                                            psK[mc][:, 0:256],
                                            wk_bf[:, dc * 512 + mc * 128:
                                                  dc * 512 + mc * 128 + 128],
                                            xbf[bj][:, j * 256: j * 256 + 256],
                                            dc == 0, dc == 31),
                                            waits=([tc] if mc == 0 else []) +
                                            wkR[dc].rd() +
                                            (psKR[mc].wr() if dc == 0 else []),
                                            tok=(j == 3))
                                        if j == 3:
                                            xbfR[bj].did_read(tk)
                                            if dc == 31:
                                                psKR[mc].did_write(tk)
                                        wkR[dc].did_read(tk)
                            for mc in range(4):
                                te = s.op("ACT",
                                          lambda e,
                                          d=kT[:, mc * NK + kg * 256:
                                               mc * NK + kg * 256 + 256],
                                          srd=psK[mc][:, 0:256],
                                          b=bk_sb[:, mc:mc + 1]:
                                          e.activation(d, srd, AF.Identity, bias=b),
                                          waits=psKR[mc].rd())
                                psKR[mc].did_read(te)
                                kTr[mc][kg].did_write(te)

                s.phase_barrier()
                if KPH < 1:
                    s.finish(); s.run(); return nc
                with ExitStack() as stP:
                    sbP = lambda n, sh, dt: stP.enter_context(nc.sbuf_tensor(n, sh, dt))
                    bv_sb = sbP("bv_sb", [1, 512], F32)
                    bvbf = sbP("bvbf", [1, 512], BF16)
                    t_bvl = s.op("SP", lambda e: e.dma_start(bv_sb[:],
                                                             c_bv.ap()),
                                 dma="ld")
                    t_bv = s.op("POOL", lambda e: e.tensor_copy(bvbf[:],
                                                                bv_sb[:]),
                                waits=[t_bvl])
                    wv_bf = sbP("wv_bf", [128, 32 * 512], BF16)
                    wvR = [Res() for _ in range(32)]
                    wstg = [sbP("wstgv%d" % i, [128, 1024], F32) for i in range(2)]
                    wstgR = [Res(), Res()]
                    xvb = [sbP("xvb%d" % i, [128, 1024], BF16) for i in range(4)]
                    xvbR = [Res() for _ in range(4)]
                    wload(wv_bf, wvR, wvT, wstg, wstgR, 32, 512)
                    with nc.psum_tensor("pv0", [128, 512], F32) as pv0, \
                         nc.psum_tensor("pv1", [128, 512], F32) as pv1:
                        psV = [pv0, pv1]
                        psVR = [Res(), Res()]
                        for kg in range(16):
                            for g in range(8):
                                bi = (kg * 8 + g) % 3
                                bj = (kg * 8 + g) % 2
                                tl = s.op("SP", lambda e, kg=kg, g=g, bi=bi:
                                          e.dma_start(
                                              xvb[bi].ap().rearrange(
                                                  "p (c n) -> p c n", c=4),
                                              xscr_r[:, 4 * g:4 * g + 4,
                                                     256 * kg:256 * kg + 256]),
                                          waits=xvbR[bi].wr() + xscrR[kg].rd(),
                                          dma="sc")
                                xvbR[bi].did_write(tl)
                                for j in range(4):
                                    dc = 4 * g + j
                                    for ti in range(2):
                                        tk = s.op("PE", mm(
                                            psV[ti][:],
                                            xvb[bi][:, j * 256 + ti * 128:
                                                    j * 256 + ti * 128 + 128],
                                            wv_bf[:, dc * 512: dc * 512 + 512],
                                            dc == 0, False),
                                            waits=([tl] if ti == 0 else []) +
                                            wvR[dc].rd() +
                                            (psVR[ti].wr() if dc == 0 else []),
                                            tok=(j == 3))
                                        if j == 3:
                                            xvbR[bi].did_read(tk)
                                        wvR[dc].did_read(tk)
                            for ti in range(2):
                                kc = kg * 2 + ti
                                tb = s.op("PE", mm(psV[ti][:], ones65[0:1, :],
                                                   bvbf[:], False, True),
                                          waits=[t_bv])
                                psVR[ti].did_write(tb)
                                dst = V.ap().rearrange("p (kc h x) -> p kc h x",
                                                       kc=32, h=8)[:, kc, :, 0:64]
                                te = s.op("ACT",
                                          lambda e, d=dst,
                                          srd=psV[ti].ap().rearrange(
                                              "p (h x) -> p h x", h=8):
                                          e.activation(d, srd, AF.Copy),
                                          waits=[tb])
                                psVR[ti].did_read(te)
                                Vr[kc].did_write(te)

                s.phase_barrier()
                if KPH < 2:
                    s.finish(); s.run(); return nc
                with ExitStack() as stP:
                    sbP = lambda n, sh, dt: stP.enter_context(nc.sbuf_tensor(n, sh, dt))
                    wq_bf = sbP("wq_bf", [128, 32 * 512], BF16)
                    wqR = [Res() for _ in range(32)]
                    wstg = [sbP("wstgq%d" % i, [128, 1024], F32) for i in range(2)]
                    wstgR = [Res(), Res()]
                    xqb = [sbP("xqb%d" % i, [128, 2048], BF16) for i in range(3)]
                    xqbR = [Res() for _ in range(3)]
                    wload(wq_bf, wqR, wqT, wstg, wstgR, 32, 512)
                    with nc.psum_tensor("pq0", [128, 512], F32) as pq0, \
                         nc.psum_tensor("pq1", [128, 512], F32) as pq1, \
                         nc.psum_tensor("pq2", [128, 512], F32) as pq2, \
                         nc.psum_tensor("pq3", [128, 512], F32) as pq3:
                        psQ = [pq0, pq1, pq2, pq3]
                        psQR = [Res() for _ in range(4)]
                        for qg in range(4):
                            for g in range(8):
                                bi = g % 3
                                tl = s.op("SP", lambda e, qg=qg, g=g, bi=bi:
                                          e.dma_start(
                                              xqb[bi].ap().rearrange(
                                                  "p (c n) -> p c n", c=4),
                                              xscr_r[:, 4 * g:4 * g + 4,
                                                     512 * qg:512 * qg + 512]),
                                          waits=xqbR[bi].wr() +
                                          xscrR[2 * qg].rd() + xscrR[2 * qg + 1].rd(),
                                          dma="sc")
                                xqbR[bi].did_write(tl)
                                for j in range(4):
                                    dc = 4 * g + j
                                    for mc in range(4):
                                        tk = s.op("PE", mm(
                                            psQ[mc][:],
                                            wq_bf[:, dc * 512 + mc * 128:
                                                  dc * 512 + mc * 128 + 128],
                                            xqb[bi][:, j * 512: j * 512 + 512],
                                            dc == 0, dc == 31),
                                            waits=([tl] if mc == 0 else []) +
                                            wqR[dc].rd() +
                                            (psQR[mc].wr() if dc == 0 else []),
                                            tok=(j == 3))
                                        if j == 3:
                                            xqbR[bi].did_read(tk)
                                            if dc == 31:
                                                psQR[mc].did_write(tk)
                                        wqR[dc].did_read(tk)
                            for mc in range(4):
                                te = s.op("ACT",
                                          lambda e,
                                          d=qT[:, mc * T + qg * 512:
                                               mc * T + qg * 512 + 512],
                                          srd=psQ[mc][:], b=bq_sb[:, mc:mc + 1]:
                                          e.activation(d, srd, AF.Identity, bias=b),
                                          waits=psQR[mc].rd())
                                psQR[mc].did_read(te)
                                qTr[mc][qg].did_write(te)

                s.phase_barrier()
                if KPH < 3:
                    s.finish(); s.run(); return nc
                with ExitStack() as stP:
                    sbP = lambda n, sh, dt: stP.enter_context(nc.sbuf_tensor(n, sh, dt))
                    wstgo = [sbP("wstgo%d" % i, [128, 1024], F32)
                             for i in range(2)]
                    wstgoR = [Res(), Res()]
                    probs = sbP("probs", [128, 8192], BF16)
                    probsR = [[Res() for _ in range(8)] for _ in range(2)]
                    racc = sbP("racc", [128, 16], F32)
                    raccR = [Res(), Res()]
                    mcol = sbP("mcol", [128, 2], F32)
                    mcolR = [Res(), Res()]
                    mrow = [sbP("mrow%d" % i, [65, T], BF16) for i in range(2)]
                    mrowR = [[[Res() for _ in range(4)] for _ in range(2)]
                             for _ in range(2)]
                    mtmp = sbP("mtmp", [1, T], BF16)
                    mtmpR = [Res() for _ in range(4)]
                    rrec = sbP("rrec", [65, 2048], F32)
                    rrecR = [Res(), Res()]
                    rrecR2 = [Res(), Res()]
                    rbc = sbP("rbc", [64, 1024], F32)
                    rbcR = [Res(), Res()]

                    for hh in range(4):
                        for c in range(4):
                            st_i = c % 2
                            src_ = woT.ap()[c * 128:(c + 1) * 128,
                                            hh * 1024:(hh + 1) * 1024]
                            tl = s.op("SP", lambda e, d=wstgo[st_i], src=src_:
                                      e.dma_start(d[:], src),
                                      waits=wstgoR[st_i].wr(), dma="ld")
                            wstgoR[st_i].did_write(tl)
                            tc = s.op("POOL",
                                      lambda e,
                                      d=wo_bf[:, c * D + hh * 1024:
                                              c * D + hh * 1024 + 1024],
                                      srd=wstgo[st_i][:]: e.tensor_copy(d, srd),
                                      waits=[tl])
                            wstgoR[st_i].did_read(tc)
                            woR[c][hh].did_write(tc)

                    with nc.psum_tensor("sS", [128, 1024], F32) as Sg, \
                         nc.psum_tensor("rb0", [128, 1024], F32) as Rb0, \
                         nc.psum_tensor("rb1", [128, 1024], F32) as Rb1, \
                         nc.psum_tensor("a0", [65, 512], F32) as A0, \
                         nc.psum_tensor("a1", [65, 512], F32) as A1:
                        SgR = [Res(), Res()]
                        Rbig = [Rb0, Rb1]
                        RbR = [[Res(), Res()], [Res(), Res()]]
                        Ab = [A0, A1]
                        AbR = [Res(), Res()]

                        def mk_pair(hp, hx, qc, kg):
                            def run():
                                base = hx * 64
                                half = kg % 2
                                tmm = s.op("PE", mm(
                                    Sg[:, half * 512: half * 512 + 512],
                                    qT[base:base + 64,
                                       hp * T + qc * 128:
                                       hp * T + qc * 128 + 128],
                                    kT[base:base + 64,
                                       hp * NK + kg * 512:
                                       hp * NK + kg * 512 + 512],
                                    True, True),
                                    waits=SgR[half].wr() +
                                    qTr[hp][qc // 4].rd() +
                                    kTr[hp][2 * kg].rd() +
                                    kTr[hp][2 * kg + 1].rd())
                                SgR[half].did_write(tmm)
                                tr = s.op("DVE",
                                          lambda e,
                                          d=racc[:, hx * 8 + kg: hx * 8 + kg + 1],
                                          srd=Sg[:, half * 512: half * 512 + 512]:
                                          e.reduce_max(d, srd, AX.X, negate=True),
                                          waits=[tmm] +
                                          (raccR[hx].wr() if kg == 0 else []))
                                SgR[half].did_read(tr)
                                raccR[hx].did_write(tr)
                            return run

                        def mk_epia(hp, hx, qc):
                            def run():
                                tm = s.op("DVE",
                                          lambda e, d=mcol[:, hx:hx + 1],
                                          srd=racc[:, hx * 8: hx * 8 + 8]:
                                          e.tensor_reduce(d, srd, AX.X, ALU.min),
                                          waits=raccR[hx].rd() + mcolR[hx].wr())
                                raccR[hx].did_read(tm)
                                mcolR[hx].did_write(tm)
                            return run

                        def mk_epi(hp, hx, qc):
                            def run():
                                mr = mrow[hp % 2]
                                tm = mcolR[hx].rd()[0] if mcolR[hx].rd() else None
                                pt = Sg.ap()[0:1, 0:128]
                                tt = s.op("PE",
                                          lambda e, o=pt, i=mcol[:, hx:hx + 1],
                                          idn=id32[:]: e.transpose(o, i, idn),
                                          waits=([tm] if tm else []) +
                                          SgR[0].wr() + consts_fence)
                                SgR[0].did_write(tt)
                                mcolR[hx].did_read(tt)
                                if hx == 0:
                                    dmr = mr[0:1, qc * 128: qc * 128 + 128]
                                    wmr = mrowR[hp % 2][0][qc // 4].wr()
                                else:
                                    dmr = mtmp[0:1, qc * 128: qc * 128 + 128]
                                    wmr = mtmpR[qc // 4].wr()
                                tc2 = s.op("DVE",
                                           lambda e, d=dmr, srd=pt:
                                           e.tensor_copy(d, srd),
                                           waits=[tt] + wmr)
                                SgR[0].did_read(tc2)
                                if hx == 0:
                                    mrowR[hp % 2][0][qc // 4].did_write(tc2)
                                else:
                                    mtmpR[qc // 4].did_write(tc2)
                            return run

                        def mk_tdm(hp, qg):
                            def run():
                                mr = mrow[hp % 2]
                                tdm = s.op("SP",
                                           lambda e,
                                           d=mr[64:65, qg * 512: qg * 512 + 512],
                                           srd=mtmp[0:1, qg * 512: qg * 512 + 512]:
                                           e.dma_start(d, srd),
                                           waits=mtmpR[qg].rd() +
                                           mrowR[hp % 2][1][qg].wr(),
                                           dma="sc")
                                mtmpR[qg].did_read(tdm)
                                mrowR[hp % 2][1][qg].did_write(tdm)
                            return run

                        units = []
                        for hp in range(4):
                            for qg in range(4):
                                gid = hp * 4 + qg
                                for hx in range(2):
                                    for qc in range(4 * qg, 4 * qg + 4):
                                        for kg in range(8):
                                            units.append((gid, mk_pair(hp, hx, qc, kg)))
                                        units.append((gid, mk_epia(hp, hx, qc)))
                                        units.append((gid, mk_epi(hp, hx, qc)))
                                    if hx == 1:
                                        units.append((gid, mk_tdm(hp, qg)))
                        upos = [0]

                        def drain_until(gid):
                            while upos[0] < len(units) and units[upos[0]][0] <= gid:
                                units[upos[0]][1]()
                                upos[0] += 1

                        def drain_n(n):
                            for _ in range(n):
                                if upos[0] < len(units):
                                    units[upos[0]][1]()
                                    upos[0] += 1

                        evac_pend = []
                        for hp in range(4):
                            mr = mrow[hp % 2]
                            for qg in range(4):
                                drain_until(hp * 4 + qg)
                                pend = [[], []]

                                def emit_av(hx, kc, te, col, hp=hp):
                                    h = 2 * hp + hx
                                    ta = s.op("PE", mm(
                                        Ab[hx][:],
                                        V[:, kc * 520 + h * 65:
                                          kc * 520 + h * 65 + 65],
                                        probs[:, col: col + 512],
                                        kc == 0, kc == 31),
                                        waits=[te] + Vr[kc].rd() +
                                        (AbR[hx].wr() if kc == 0 else []),
                                        tok=(kc == 31))
                                    probsR[hx][kc % 8].did_read(ta)
                                    if kc == 31:
                                        AbR[hx].did_write(ta)
                                for kp in range(16):
                                    if kp in (1, 2, 3) and evac_pend:
                                        evac_pend.pop(0)()
                                    for k2 in range(2):
                                        kc = 2 * kp + k2
                                        for hx in range(2):
                                            s.op("PE", mm(
                                                Rbig[hx][:, k2 * 512:
                                                         k2 * 512 + 512],
                                                kT[hx * 64: hx * 64 + 64,
                                                   hp * NK + kc * 128:
                                                   hp * NK + kc * 128 + 128],
                                                qT[hx * 64: hx * 64 + 64,
                                                   hp * T + qg * 512:
                                                   hp * T + qg * 512 + 512],
                                                True, False),
                                                waits=RbR[hx][k2].wr(),
                                                tok=False)
                                        t2s = []
                                        for hx in range(2):
                                            t2 = s.op("PE", mm(
                                                Rbig[hx][:, k2 * 512:
                                                         k2 * 512 + 512],
                                                ones65[hx * 64: hx * 64 + 1, :],
                                                mr[hx * 64: hx * 64 + 1,
                                                   qg * 512: qg * 512 + 512],
                                                False, True),
                                                waits=mrowR[hp % 2][hx][qg].rd())
                                            mrowR[hp % 2][hx][qg].did_read(t2)
                                            RbR[hx][k2].did_write(t2)
                                            t2s.append(t2)
                                        for hx in range(2):
                                            col = hx * 4096 + (kc % 8) * 512
                                            te = s.op("ACT",
                                                      lambda e,
                                                      d=probs[:, col: col + 512],
                                                      srd=Rbig[hx][:, k2 * 512:
                                                                  k2 * 512 + 512]:
                                                      e.activation(d, srd, AF.Exp),
                                                      waits=[t2s[hx]] +
                                                      probsR[hx][kc % 8].wr())
                                            RbR[hx][k2].did_read(te)
                                            probsR[hx][kc % 8].did_write(te)
                                            pend[hx].append((hx, kc, te, col))
                                        drain_n(1)
                                    drain_n(1)
                                    for hx in range(2):
                                        if kp >= 3:
                                            while len(pend[hx]) > 4:
                                                emit_av(*pend[hx].pop(0))
                                        drain_n(1)
                                for hx in range(2):
                                    for item in pend[hx]:
                                        emit_av(*item)

                                def emit_evac_a(hx, st, hp=hp, qg=qg):
                                    c0 = hx * 1024
                                    t1 = s.op("DVE",
                                              lambda e,
                                              d=rrec[64:65, c0: c0 + 512],
                                              srd=Ab[hx].ap()[64:65, 0:512]:
                                              e.tensor_scalar(d, srd, SQRT_DH,
                                                              None, ALU.mult),
                                              waits=AbR[hx].rd(),
                                              fwaits=rrecR[hx].wr())
                                    AbR[hx].did_read(t1)
                                    rrecR[hx].did_write(t1)
                                    t2 = s.op("DVE",
                                              lambda e,
                                              d=rrec[64:65, c0 + 512: c0 + 1024],
                                              srd=rrec[64:65, c0: c0 + 512]:
                                              e.reciprocal(d, srd),
                                              waits=[t1], fwaits=rrecR2[hx].wr())
                                    rrecR[hx].did_read(t2)
                                    rrecR2[hx].did_write(t2)
                                    st["t2"] = t2

                                def emit_evac_b(hx, st, hp=hp, qg=qg):
                                    t2 = st["t2"]
                                    c0 = hx * 1024
                                    t3 = s.op("PE", mm(Sg.ap()[0:64, 512:1024],
                                                       o65f[64:65, :],
                                                       rrec[64:65,
                                                            c0 + 512: c0 + 1024],
                                                       True, True),
                                              waits=[t2] + SgR[1].wr())
                                    rrecR2[hx].did_read(t3)
                                    SgR[1].did_write(t3)
                                    t4 = s.op("DVE",
                                              lambda e,
                                              d=rbc[:, hx * 512: hx * 512 + 512],
                                              srd=Sg.ap()[0:64, 512:1024]:
                                              e.tensor_copy(d, srd),
                                              waits=[t3], fwaits=rbcR[hx].wr())
                                    SgR[1].did_read(t4)
                                    rbcR[hx].did_write(t4)
                                    st["t4"] = t4

                                def emit_evac_c(hx, st, hp=hp, qg=qg):
                                    t4 = st["t4"]
                                    if hx == 0:
                                        dst = OT[hp][0:64, qg * 512: qg * 512 + 512]
                                        wx = OTr[hp][qg].wr()
                                    else:
                                        dst = OTtmp[0:64, qg * 512: qg * 512 + 512]
                                        wx = OTtmpr[qg].wr()
                                    t5 = s.op("DVE",
                                              lambda e, d=dst,
                                              a=Ab[hx].ap()[0:64, 0:512],
                                              b=rbc[:, hx * 512: hx * 512 + 512]:
                                              e.tensor_tensor(d, a, b, ALU.mult),
                                              waits=[t4] + AbR[hx].rd() + wx)
                                    AbR[hx].did_read(t5)
                                    rbcR[hx].did_read(t5)
                                    if hx == 0:
                                        OTr[hp][qg].did_write(t5)
                                    else:
                                        OTtmpr[qg].did_write(t5)
                                st0_, st1_ = {}, {}
                                evac_pend.append(
                                    lambda a=emit_evac_a, s0=st0_, s1=st1_:
                                    (a(0, s0), a(1, s1)))
                                evac_pend.append(
                                    lambda b=emit_evac_b, s0=st0_, s1=st1_:
                                    (b(0, s0), b(1, s1)))
                                evac_pend.append(
                                    lambda c=emit_evac_c, s0=st0_, s1=st1_:
                                    (c(0, s0), c(1, s1)))
                            for f in evac_pend:
                                f()
                            del evac_pend[:]
                            tmv = s.op("SP", lambda e, hp=hp:
                                       e.dma_start(OT[hp][64:128, :], OTtmp[:]),
                                       waits=[tk for r in OTtmpr
                                              for tk in r.rd()] +
                                       [tk for qg in range(4)
                                        for tk in OTr[hp][qg].wr()],
                                       dma="sc")
                            for qg in range(4):
                                OTtmpr[qg].did_read(tmv)
                                OTr[hp][qg].did_write(tmv)

            s.phase_barrier()
            if KPH < 4:
                s.finish(); s.run(); return nc
            with ExitStack() as stP:
                sbP = lambda n, sh, dt: stP.enter_context(nc.sbuf_tensor(n, sh, dt))
                xq = sbP("xq2", [128, 32 * 512], BF16)
                xqR = [Res() for _ in range(32)]
                res = sbP("res", [128, 32 * 512], BF16)
                resR = [Res() for _ in range(32)]
                sq = sbP("sq", [128, 4 * 512], BF16)
                sqR = [Res() for _ in range(4)]
                x2g = sbP("x2g", [128, 32 * 512], BF16)
                x2gR = [Res() for _ in range(32)]
                abc = sbP("abc", [128, 512], F32)
                abcR = Res()
                bbc = sbP("bbc", [128, 512], F32)
                bbcR = Res()
                tmpf = [sbP("tmpf%d" % i, [128, 512], F32) for i in range(2)]
                tmpfR = [Res(), Res()]

                with nc.psum_tensor("po0", [128, 512], F32) as po0, \
                     nc.psum_tensor("po1", [128, 512], F32) as po1, \
                     nc.psum_tensor("po2", [128, 512], F32) as po2, \
                     nc.psum_tensor("pstS", [1, 512], F32) as pstS, \
                     nc.psum_tensor("pstQ", [1, 512], F32) as pstQ, \
                     nc.psum_tensor("pbc0", [128, 512], F32) as pbc0, \
                     nc.psum_tensor("pbc1", [128, 512], F32) as pbc1:
                    po = [po0, po1, po2]
                    poR = [Res(), Res(), Res()]
                    pstSR = Res()
                    pstQR = Res()
                    pbcR_ = [Res(), Res()]
                    opend = []
                    for qg in range(4):
                        for g in range(8):
                            tl = s.op("SP", lambda e, qg=qg, g=g: e.dma_start(
                                xq.ap().rearrange("p (c n) -> p c n", c=32)[
                                    :, 4 * g:4 * g + 4, :],
                                xscr_r[:, 4 * g:4 * g + 4,
                                       qg * 512: qg * 512 + 512]),
                                waits=[tk for j in range(4)
                                       for tk in xqR[4 * g + j].wr()] +
                                xscrR[2 * qg].rd() + xscrR[2 * qg + 1].rd(),
                                dma="sc")
                            for j in range(4):
                                xqR[4 * g + j].did_write(tl)
                        ppend = []
                        def emit_stats(mc, te, ts, sqi):
                            t_s = s.op("PE", mm(pstS[:], onescol[:, 0:1],
                                                res[:, mc * 512: mc * 512 + 512],
                                                mc == 0, mc == NDC - 1),
                                       waits=[te] +
                                       (pstSR.wr() if mc == 0 else []))
                            t_q = s.op("PE", mm(pstQ[:], onescol[:, 1:2],
                                                sq[:, sqi * 512: sqi * 512 + 512],
                                                mc == 0, mc == NDC - 1),
                                       waits=[ts] +
                                       (pstQR.wr() if mc == 0 else []))
                            resR[mc].did_read(t_s)
                            sqR[sqi].did_read(t_q)
                            if mc == NDC - 1:
                                pstSR.did_write(t_s)
                                pstQR.did_write(t_q)
                        for mc in range(NDC):
                            for _ in range(2):
                                if opend:
                                    opend.pop(0)()
                            pi = mc % 3
                            w0 = poR[pi].wr()
                            last = None
                            for hp_ in range(4):
                                last = s.op("PE", mm(
                                    po[pi][:],
                                    wo_bf[:, hp_ * D + mc * 128:
                                          hp_ * D + mc * 128 + 128],
                                    OT[hp_][:, qg * 512: qg * 512 + 512],
                                    hp_ == 0, hp_ == 3),
                                    waits=(w0 if hp_ == 0 else []) +
                                    woR[hp_][mc // 8].rd() + OTr[hp_][qg].rd(),
                                    tok=(hp_ == 3))
                            poR[pi].did_write(last)
                            for hp_ in range(4):
                                woR[hp_][mc // 8].did_read(last)
                                OTr[hp_][qg].did_read(last)
                            if len(ppend) > 2:
                                emit_stats(*ppend.pop(0))
                            te = s.op("DVE",
                                      lambda e,
                                      d=res[:, mc * 512: mc * 512 + 512],
                                      a=po[pi][:], b=bo_sb[:, mc:mc + 1],
                                      c2=xq[:, mc * 512: mc * 512 + 512]:
                                      e.scalar_tensor_tensor(d, a, b, c2,
                                                             ALU.add, ALU.add),
                                      waits=[last] + xqR[mc].rd() + resR[mc].wr())
                            poR[pi].did_read(te)
                            xqR[mc].did_read(te)
                            resR[mc].did_write(te)
                            sqi = mc % 4
                            ts = s.op("ACT",
                                      lambda e, d=sq[:, sqi * 512: sqi * 512 + 512],
                                      a=res[:, mc * 512: mc * 512 + 512]:
                                      e.activation(d, a, AF.Square),
                                      waits=[te] + sqR[sqi].wr())
                            resR[mc].did_read(ts)
                            sqR[sqi].did_write(ts)
                            ppend.append((mc, te, ts, sqi))
                        while ppend:
                            emit_stats(*ppend.pop(0))
                        ro, trs, tb2 = ln_rows(pstS[:], pstQ[:], pstSR, pstQR,
                                               512, 0)

                        def mk_bcast(ro=ro, trs=trs, tb2=tb2):
                            def f():
                                tba = s.op("PE", mm(pbc0[:], ones1f[:], ro(1),
                                                    True, True),
                                           waits=[trs] + pbcR_[0].wr() +
                                           consts_fence)
                                rowsRes[1].did_read(tba)
                                pbcR_[0].did_write(tba)
                                tbb = s.op("PE", mm(pbc1[:], ones1f[:], ro(0),
                                                    True, True),
                                           waits=[tb2] + pbcR_[1].wr())
                                rowsRes[0].did_read(tbb)
                                pbcR_[1].did_write(tbb)
                                tca = s.op("DVE", lambda e, d=abc[:], srd=pbc0[:]:
                                           e.tensor_copy(d, srd),
                                           waits=[tba] + abcR.wr())
                                pbcR_[0].did_read(tca)
                                abcR.did_write(tca)
                                tcb = s.op("DVE", lambda e, d=bbc[:], srd=pbc1[:]:
                                           e.tensor_copy(d, srd),
                                           waits=[tbb] + bbcR.wr())
                                pbcR_[1].did_read(tcb)
                                bbcR.did_write(tcb)
                            return f
                        opend.append(mk_bcast())
                        for mc in range(NDC):
                            def mk_apply(mc=mc):
                                def f():
                                    ti = mc % 2
                                    t1 = s.op("POOL",
                                              lambda e, d=tmpf[ti][:],
                                              a=res[:, mc * 512: mc * 512 + 512],
                                              b=abc[:]:
                                              e.tensor_tensor(d, a, b, ALU.mult),
                                              waits=resR[mc].rd() + abcR.rd() +
                                              tmpfR[ti].wr())
                                    resR[mc].did_read(t1)
                                    abcR.did_read(t1)
                                    tmpfR[ti].did_write(t1)
                                    t2 = s.op("DVE",
                                              lambda e,
                                              d=x2g[:, mc * 512: mc * 512 + 512],
                                              a=tmpf[ti][:], b=bbc[:]:
                                              e.tensor_tensor(d, a, b, ALU.add),
                                              waits=[t1] + bbcR.rd() +
                                              x2gR[mc].wr())
                                    tmpfR[ti].did_read(t2)
                                    bbcR.did_read(t2)
                                    x2gR[mc].did_write(t2)
                                return f
                            opend.append(mk_apply())
                            if mc % 4 == 3:
                                def mk_spill(qg=qg, g=mc // 4):
                                    def f():
                                        tw = s.op("ACT",
                                                  lambda e, qg=qg, g=g:
                                                  e.dma_start(
                                                      x2scr_r[
                                                          :, 4 * g:4 * g + 4,
                                                          qg * 512:
                                                          qg * 512 + 512],
                                                      x2g.ap().rearrange(
                                                          "p (c n) -> p c n",
                                                          c=32)[
                                                          :, 4 * g:4 * g + 4, :]),
                                                  waits=[tk for j in range(4)
                                                         for tk in
                                                         x2gR[4 * g + j].rd()],
                                                  dma="sca")
                                        for j in range(4):
                                            x2gR[4 * g + j].did_read(tw)
                                        x2scrR[qg].did_write(tw)
                                    return f
                                opend.append(mk_spill())
                    while opend:
                        opend.pop(0)()

        s.phase_barrier()
        if KPH < 5:
            s.finish(); s.run(); return nc
        with ExitStack() as stP:
            sbP = lambda n, sh, dt: stP.enter_context(nc.sbuf_tensor(n, sh, dt))
            x2h = sbP("x2h", [128, 32 * 1024], BF16)
            x2hR = [Res() for _ in range(32)]
            wq_bf = [sbP("wqf%d" % i, [128, 32 * 256], BF16) for i in range(2)]
            wqR = [[Res() for _ in range(32)] for _ in range(2)]
            wstg = [sbP("wstgf%d" % i, [128, 1024], F32) for i in range(2)]
            wstgR = [Res(), Res()]
            ar_bf = sbP("ar_bf", [128, 32 * 128], BF16)
            arR = Res()
            br_bf = sbP("br_bf", [128, 32 * 128], BF16)
            brR = Res()
            wr_bf = sbP("wr_bf", [128, 32 * 8], BF16)
            wrR = Res()
            hs = sbP("hs", [128, 1024], BF16)
            hsR = Res()
            ce = sbP("ce", [128, 1024], BF16)
            ceR = Res()
            lgt = sbP("lgt", [128, 64], F32)
            lgtR = Res()
            lg2 = sbP("lg2", [128, 64], F32)
            lg2R = Res()
            xpb = sbP("xpb", [128, 64], F32)
            xpbR = Res()
            prb = sbP("prb", [128, 64], F32)
            prbR = Res()
            mskb = sbP("mskb", [128, 64], F32)
            mskbR = Res()
            stt = sbP("stt", [128, 64], F32)
            sttR = Res()
            cmb = sbP("cmb", [128, 64], F32)
            cmbR = Res()
            cmT = sbP("cmT", [8, 1024], BF16)
            cmTR = Res()
            lg8 = sbP("lg8", [8, 1024], F32)
            lg8R = Res()
            scc = sbP("scc", [128, 8], F32)
            sccR = Res()
            res2c = [sbP("res2c%d" % i, [128, 1024], BF16) for i in range(2)]
            res2cR = [Res(), Res()]
            res2d = [sbP("res2d%d" % i, [128, 1024], BF16) for i in range(2)]
            res2dR = [Res(), Res()]
            sqc = [sbP("sqc%d" % i, [128, 1024], BF16) for i in range(2)]
            sqcR = [Res(), Res()]
            a2 = sbP("a2", [128, 1024], F32)
            a2R = Res()
            b2 = sbP("b2", [128, 1024], F32)
            b2R = Res()
            tmpo = [sbP("tmpo%d" % i, [128, 1024], F32) for i in range(2)]
            tmpoR = [Res(), Res()]
            outst = [sbP("outst%d" % i, [128, 1024], F32) for i in range(2)]
            outstR = [Res(), Res()]

            def small_load(dst, dstR, src3, n):
                per = max(1, 1024 // n)
                toks = []
                for p in range((32 + per - 1) // per):
                    c0 = p * per
                    cw = min(per, 32 - c0)
                    w = cw * n
                    st_i = p % 2
                    tl = s.op("SP",
                              lambda e, d=wstg[st_i].ap().rearrange(
                                  "q (c n) -> q c n", n=n)[:, 0:cw, :],
                              src=src3[:, c0:c0 + cw, :]: e.dma_start(d, src),
                              waits=wstgR[st_i].wr(), dma="ld")
                    wstgR[st_i].did_write(tl)
                    tc = s.op("POOL",
                              lambda e, d=dst[:, c0 * n:c0 * n + w],
                              srd=wstg[st_i][:, 0:w]: e.tensor_copy(d, srd),
                              waits=[tl])
                    wstgR[st_i].did_read(tc)
                    toks.append(tc)
                dstR.did_write(toks[-1])

            small_load(ar_bf, arR,
                       aallT.ap().rearrange("(c p) n -> p c n", p=128), 128)
            small_load(br_bf, brR,
                       ballT.ap().rearrange("p (c n) -> p c n", n=128), 128)
            small_load(wr_bf, wrR,
                       wrT.ap().rearrange("(c p) n -> p c n", p=128), 8)

            with nc.psum_tensor("pf0", [128, 1024], F32) as pf0, \
                 nc.psum_tensor("pf1", [128, 1024], F32) as pf1, \
                 nc.psum_tensor("pss", [1, 1024], F32) as pss, \
                 nc.psum_tensor("psq", [1, 1024], F32) as psq:
                pf = [pf0, pf1]
                pfR = [Res(), Res()]
                pssR = Res()
                psqR = Res()
                apply_pend = []

                def emit_apply(ath, mc, ta2, tb2c):
                    ri = mc % 2
                    trl = s.op("SP", lambda e, mc=mc, ath=ath, ri=ri:
                               e.dma_start(
                                   res2d[ri][:],
                                   r2scr_r[:, mc, ath * 1024: ath * 1024 + 1024]),
                               waits=res2dR[ri].wr() + r2scrR[ath][mc].rd(),
                               dma="sc")
                    res2dR[ri].did_write(trl)
                    r2scrR[ath][mc].did_read(trl)
                    t1 = s.op("POOL", lambda e, d=tmpo[ri][:], a=res2d[ri][:],
                              b=a2[:]: e.tensor_tensor(d, a, b, ALU.mult),
                              waits=[trl, ta2] + tmpoR[ri].wr())
                    res2dR[ri].did_read(t1)
                    a2R.did_read(t1)
                    tmpoR[ri].did_write(t1)
                    t2 = s.op("DVE", lambda e, d=outst[ri][:], a=tmpo[ri][:],
                              b=b2[:]: e.tensor_tensor(d, a, b, ALU.add),
                              waits=[t1, tb2c] + outstR[ri].wr())
                    tmpoR[ri].did_read(t2)
                    b2R.did_read(t2)
                    outstR[ri].did_write(t2)
                    two = s.op("ACT", lambda e, mc=mc, ath=ath, ri=ri:
                               e.dma_start(
                                   outT.ap()[mc * 128: mc * 128 + 128,
                                             ath * 1024: ath * 1024 + 1024],
                                   outst[ri][:]),
                               waits=[t2], dma="out")
                    outstR[ri].did_read(two)
                for th in range(2):
                    for g in range(8):
                        tl = s.op("SP", lambda e, th=th, g=g: e.dma_start(
                            x2h.ap().rearrange("p (c n) -> p c n", c=32)[
                                :, 4 * g:4 * g + 4, :],
                            x2scr_r[:, 4 * g:4 * g + 4,
                                    th * 1024: th * 1024 + 1024]),
                            waits=[tk for j in range(4)
                                   for tk in x2hR[4 * g + j].wr()] +
                            x2scrR[2 * th].rd() + x2scrR[2 * th + 1].rd(),
                            dma=("sc" if g % 2 else "ld"))
                        for j in range(4):
                            x2hR[4 * g + j].did_write(tl)
                    w0 = pfR[0].wr()
                    last = None
                    for dc in range(NDC):
                        for sub in range(2):
                            last = s.op("PE", mm(
                                pf[0].ap()[0:8, sub * 512: sub * 512 + 512],
                                wr_bf[:, dc * 8: dc * 8 + 8],
                                x2h[:, dc * 1024 + sub * 512:
                                    dc * 1024 + sub * 512 + 512],
                                dc == 0, dc == NDC - 1),
                                waits=(w0 if dc == 0 and sub == 0 else []) +
                                wrR.rd() + x2hR[dc].rd(),
                                tok=(dc == NDC - 1 and sub == 1))
                    pfR[0].did_write(last)
                    wrR.did_read(last)
                    for dc in range(NDC):
                        x2hR[dc].did_read(last)
                    tlg = s.op("DVE", lambda e, d=lg8[:],
                               srd=pf[0].ap()[0:8, 0:1024]:
                               e.tensor_copy(d, srd), waits=[last] + lg8R.wr())
                    pfR[0].did_read(tlg)
                    lg8R.did_write(tlg)
                    w0 = pfR[0].wr()
                    last = None
                    for dc in range(NDC):
                        for sub in range(2):
                            last = s.op("PE", mm(
                                pf[0][:, sub * 512: sub * 512 + 512],
                                ar_bf[:, dc * 128: dc * 128 + 128],
                                x2h[:, dc * 1024 + sub * 512:
                                    dc * 1024 + sub * 512 + 512],
                                dc == 0, dc == NDC - 1),
                                waits=(w0 if dc == 0 and sub == 0 else []) +
                                arR.rd() + x2hR[dc].rd(),
                                tok=(dc == NDC - 1 and sub == 1))
                    pfR[0].did_write(last)
                    arR.did_read(last)
                    for dc in range(NDC):
                        x2hR[dc].did_read(last)
                    t_loraA = last
                    for qc in range(8):
                        tt = s.op("PE",
                                  lambda e, o=pf[1].ap()[0:128, 0:8],
                                  i=lg8[:, qc * 128: qc * 128 + 128],
                                  idn=id32[0:8, 0:8]: e.transpose(o, i, idn),
                                  waits=lg8R.rd() + pfR[1].wr())
                        lg8R.did_read(tt)
                        pfR[1].did_write(tt)
                        tc = s.op("DVE", lambda e, d=lgt[:, qc * 8: qc * 8 + 8],
                                  srd=pf[1].ap()[0:128, 0:8]: e.tensor_copy(d, srd),
                                  waits=[tt] + (lgtR.wr() if qc == 0 else []))
                        pfR[1].did_read(tc)
                        lgtR.did_write(tc)
                    l3 = lgt.ap().rearrange("p (q x) -> p q x", x=8)
                    l23 = lg2.ap().rearrange("p (q x) -> p q x", x=8)
                    xp3 = xpb.ap().rearrange("p (q x) -> p q x", x=8)
                    pr3 = prb.ap().rearrange("p (q x) -> p q x", x=8)
                    mk3 = mskb.ap().rearrange("p (q x) -> p q x", x=8)
                    cm3 = cmb.ap().rearrange("p (q x) -> p q x", x=8)
                    stc = lambda i: stt[:, i * 8:(i + 1) * 8]
                    bc = lambda i: stt.ap().rearrange(
                        "p (c x) -> p c x", x=8)[:, i:i + 1, :].transpose(
                        [0, 2, 1]).to_broadcast([128, 8, 8])
                    t1 = s.op("DVE", lambda e: e.reduce_max(
                        stc(0), l3, AX.X, negate=True),
                        waits=lgtR.rd() + sttR.wr())
                    lgtR.did_read(t1)
                    sttR.did_write(t1)
                    t2 = s.op("DVE", lambda e: e.tensor_tensor(
                        l23, l3, bc(0), ALU.add),
                        waits=[t1] + lg2R.wr())
                    lgtR.did_read(t2)
                    sttR.did_read(t2)
                    lg2R.did_write(t2)
                    t3 = s.op("ACT", lambda e: e.activation(
                        xpb[:], lg2[:], AF.Exp), waits=[t2] + xpbR.wr())
                    lg2R.did_read(t3)
                    xpbR.did_write(t3)
                    t4 = s.op("DVE", lambda e: e.reduce_sum(
                        stc(1), xp3, AX.X), waits=[t3])
                    xpbR.did_read(t4)
                    sttR.did_write(t4)
                    t5 = s.op("DVE", lambda e: e.reciprocal(
                        stc(2), stc(1)), waits=[t4])
                    sttR.did_write(t5)
                    t6 = s.op("DVE", lambda e: e.tensor_tensor(
                        pr3, xp3, bc(2), ALU.mult), waits=[t5] + prbR.wr())
                    xpbR.did_read(t6)
                    sttR.did_read(t6)
                    prbR.did_write(t6)
                    t7 = s.op("DVE", lambda e: e.reduce_max(
                        stc(3), pr3, AX.X), waits=[t6])
                    prbR.did_read(t7)
                    sttR.did_write(t7)
                    t8 = s.op("DVE", lambda e: e.tensor_tensor(
                        mk3, pr3, bc(3), ALU.is_ge), waits=[t7] + mskbR.wr())
                    prbR.did_read(t8)
                    sttR.did_read(t8)
                    mskbR.did_write(t8)
                    t9 = s.op("DVE", lambda e: e.scalar_tensor_tensor(
                        l23, mskb[:], -1e30, prb[:], ALU.mult, ALU.add),
                        waits=[t8] + lg2R.wr())
                    mskbR.did_read(t9)
                    prbR.did_read(t9)
                    lg2R.did_write(t9)
                    t10 = s.op("DVE", lambda e: e.reduce_max(
                        stc(4), l23, AX.X), waits=[t9])
                    lg2R.did_read(t10)
                    sttR.did_write(t10)
                    t11 = s.op("DVE", lambda e: e.tensor_tensor(
                        mk3, pr3, bc(4), ALU.is_ge), waits=[t10] + mskbR.wr())
                    prbR.did_read(t11)
                    sttR.did_read(t11)
                    mskbR.did_write(t11)
                    t12 = s.op("DVE", lambda e: e.tensor_tensor(
                        stc(5), stc(3), stc(4), ALU.add), waits=[t10, t7])
                    sttR.did_write(t12)
                    t13 = s.op("DVE", lambda e: e.reciprocal(
                        stc(6), stc(5)), waits=[t12])
                    sttR.did_write(t13)
                    t14 = s.op("DVE", lambda e: e.tensor_scalar(
                        stc(5), stc(6), 2.0, None, ALU.mult), waits=[t13])
                    sttR.did_write(t14)
                    t15 = s.op("DVE", lambda e: e.tensor_tensor(
                        xp3, mk3, pr3, ALU.mult), waits=[t11] + xpbR.wr())
                    mskbR.did_read(t15)
                    prbR.did_read(t15)
                    xpbR.did_write(t15)
                    t16 = s.op("DVE", lambda e: e.tensor_tensor(
                        cm3, xp3, bc(5), ALU.mult),
                        waits=[t15, t14] + cmbR.wr())
                    xpbR.did_read(t16)
                    sttR.did_read(t16)
                    cmbR.did_write(t16)
                    for qc in range(8):
                        tt = s.op("PE", lambda e, o=pf[1].ap()[0:8, 0:128],
                                  i=cmb[:, qc * 8: qc * 8 + 8], idn=id32[:]:
                                  e.transpose(o, i, idn),
                                  waits=[t16] + pfR[1].wr())
                        cmbR.did_read(tt)
                        pfR[1].did_write(tt)
                        tc3 = s.op("DVE",
                                   lambda e, d=cmT[:, qc * 128: qc * 128 + 128],
                                   srd=pf[1].ap()[0:8, 0:128]:
                                   e.tensor_copy(d, srd),
                                   waits=[tt] + (cmTR.wr() if qc == 0 else []))
                        pfR[1].did_read(tc3)
                        cmTR.did_write(tc3)
                    tex = None
                    for sub in range(2):
                        tex = s.op("PE", mm(
                            pf[1].ap()[0:128, sub * 512: sub * 512 + 512],
                            e16[:], cmT[:, sub * 512: sub * 512 + 512],
                            True, True),
                            waits=cmTR.rd() +
                            (pfR[1].wr() if sub == 0 else []) + consts_fence,
                            tok=(sub == 1))
                    pfR[1].did_write(tex)
                    cmTR.did_read(tex)
                    tce = s.op("DVE", lambda e, d=ce[:],
                               srd=pf[1].ap()[0:128, 0:1024]:
                               e.tensor_copy(d, srd), waits=[tex] + ceR.wr())
                    pfR[1].did_read(tce)
                    ceR.did_write(tce)
                    ths = s.op("DVE", lambda e, d=hs[:], a=pf[0][:], b=ce[:]:
                               e.tensor_tensor(d, a, b, ALU.mult),
                               waits=[t_loraA] + ceR.rd() + hsR.wr())
                    pfR[0].did_read(ths)
                    ceR.did_read(ths)
                    hsR.did_write(ths)
                    wbT_r = wbT.ap().rearrange("(c p) n -> p c n", p=128)
                    for wq in range(16):
                        wb = wq % 2
                        for _ in range(2):
                            if apply_pend:
                                emit_apply(*apply_pend.pop(0))
                        for g4 in range(8):
                            st_i = g4 % 2
                            src = wbT_r[:, 4 * g4:4 * g4 + 4,
                                        wq * 256: wq * 256 + 256]
                            tl = s.op("SP",
                                      lambda e, d=wstg[st_i].ap().rearrange(
                                          "p (c n) -> p c n", n=256)[:, 0:4, :],
                                      src=src: e.dma_start(d, src),
                                      waits=wstgR[st_i].wr(), dma="ld")
                            wstgR[st_i].did_write(tl)
                            tc = s.op("POOL",
                                      lambda e,
                                      d=wq_bf[wb][:, g4 * 1024: g4 * 1024 + 1024],
                                      srd=wstg[st_i][:, 0:1024]:
                                      e.tensor_copy(d, srd),
                                      waits=[tl] +
                                      [tk for j in range(4)
                                       for tk in wqR[wb][4 * g4 + j].wr()])
                            wstgR[st_i].did_read(tc)
                            for j in range(4):
                                wqR[wb][4 * g4 + j].did_write(tc)
                        if wq == 0:
                            fpend = []
                            def emit_fstats(mc, te, ts, ri):
                                for sub in range(2):
                                    t_s = s.op("PE", mm(
                                        pss[0:1, sub * 512: sub * 512 + 512],
                                        onescol[:, 0:1],
                                        res2c[ri][:, sub * 512: sub * 512 + 512],
                                        mc == 0, mc == NDC - 1),
                                        waits=[te] +
                                        (pssR.wr() if mc == 0 and sub == 0 else []))
                                    t_q = s.op("PE", mm(
                                        psq[0:1, sub * 512: sub * 512 + 512],
                                        onescol[:, 1:2],
                                        sqc[ri][:, sub * 512: sub * 512 + 512],
                                        mc == 0, mc == NDC - 1),
                                        waits=[ts] +
                                        (psqR.wr() if mc == 0 and sub == 0 else []))
                                    res2cR[ri].did_read(t_s)
                                    sqcR[ri].did_read(t_q)
                                    if mc == NDC - 1 and sub == 1:
                                        pssR.did_write(t_s)
                                        psqR.did_write(t_q)
                        for m4 in range(2):
                            mc = wq * 2 + m4
                            pi = m4 % 2
                            w0 = pfR[pi].wr()
                            last = None
                            for sub in range(2):
                                for dc in range(NDC):
                                    s.op("PE", mm(
                                        pf[pi][:, sub * 512: sub * 512 + 512],
                                        wq_bf[wb][:, dc * 256 + m4 * 128:
                                              dc * 256 + m4 * 128 + 128],
                                        x2h[:, dc * 1024 + sub * 512:
                                            dc * 1024 + sub * 512 + 512],
                                        dc == 0, False),
                                        waits=(w0 if dc == 0 and sub == 0
                                               else []) +
                                        wqR[wb][dc].rd() + x2hR[dc].rd(),
                                        tok=False)
                                last = s.op("PE", mm(
                                    pf[pi][:, sub * 512: sub * 512 + 512],
                                    br_bf[:, mc * 128: mc * 128 + 128],
                                    hs[:, sub * 512: sub * 512 + 512],
                                    False, True),
                                    waits=brR.rd() + hsR.rd(),
                                    tok=(sub == 1))
                            pfR[pi].did_write(last)
                            for dc in range(NDC):
                                wqR[wb][dc].did_read(last)
                                x2hR[dc].did_read(last)
                            brR.did_read(last)
                            hsR.did_read(last)
                            ri = mc % 2
                            te = s.op("DVE",
                                      lambda e, d=res2c[ri][:], a=pf[pi][:],
                                      b=bB_sb[:, mc:mc + 1],
                                      c2=x2h[:, mc * 1024: mc * 1024 + 1024]:
                                      e.scalar_tensor_tensor(d, a, b, c2,
                                                             ALU.add, ALU.add),
                                      waits=[last] + x2hR[mc].rd() +
                                      res2cR[ri].wr())
                            pfR[pi].did_read(te)
                            x2hR[mc].did_read(te)
                            res2cR[ri].did_write(te)
                            twr = s.op("ACT", lambda e, mc=mc, th=th, ri=ri:
                                       e.dma_start(
                                           r2scr_r[:, mc,
                                                   th * 1024: th * 1024 + 1024],
                                           res2c[ri][:]),
                                       waits=[te] + r2scrR[th][mc].wr(),
                                       dma="sca")
                            res2cR[ri].did_read(twr)
                            r2scrR[th][mc].did_write(twr)
                            ts = s.op("ACT", lambda e, d=sqc[ri][:],
                                      a=res2c[ri][:]:
                                      e.activation(d, a, AF.Square),
                                      waits=[te] + sqcR[ri].wr())
                            res2cR[ri].did_read(ts)
                            sqcR[ri].did_write(ts)
                            fpend.append((mc, te, ts, ri))
                            if len(fpend) > 1:
                                emit_fstats(*fpend.pop(0))
                            if mc == NDC - 1:
                                while fpend:
                                    emit_fstats(*fpend.pop(0))
                    ro2, trs, tb2 = ln_rows(pss[0:1, :], psq[0:1, :],
                                            pssR, psqR, 1024, 0)
                    tba = None
                    for sub in range(2):
                        tba = s.op("PE", mm(
                            pf[0][:, sub * 512: sub * 512 + 512], ones1f[:],
                            rows[0:1, 1024 + sub * 512: 1024 + sub * 512 + 512],
                            True, True),
                            waits=[trs] + (pfR[0].wr() if sub == 0 else []),
                            tok=(sub == 1))
                    rowsRes[1].did_read(tba)
                    pfR[0].did_write(tba)
                    tbb = None
                    for sub in range(2):
                        tbb = s.op("PE", mm(
                            pf[1][:, sub * 512: sub * 512 + 512], ones1f[:],
                            rows[0:1, sub * 512: sub * 512 + 512],
                            True, True),
                            waits=[tb2] + (pfR[1].wr() if sub == 0 else []),
                            tok=(sub == 1))
                    rowsRes[0].did_read(tbb)
                    pfR[1].did_write(tbb)
                    tca = s.op("DVE", lambda e, d=a2[:], srd=pf[0][:]:
                               e.tensor_copy(d, srd), waits=[tba] + a2R.wr())
                    pfR[0].did_read(tca)
                    a2R.did_write(tca)
                    tcb = s.op("DVE", lambda e, d=b2[:], srd=pf[1][:]:
                               e.tensor_copy(d, srd), waits=[tbb] + b2R.wr())
                    pfR[1].did_read(tcb)
                    b2R.did_write(tcb)
                    ta2, tb2c = tca, tcb
                    for mc in range(NDC):
                        apply_pend.append((th, mc, ta2, tb2c))
                    if th == 1:
                        while apply_pend:
                            emit_apply(*apply_pend.pop(0))

        s.finish()
        s.run()
    return nc


_NC_CACHE = {}


def kernel(L0_attn_x, L0_attn_Wqk_w, L0_attn_Wqk_b, L0_attn_Wv_w, L0_attn_Wv_b,
           L0_attn_Wo_w, L0_attn_Wo_b, L0_ffn_W_base, L0_ffn_b_base,
           L0_ffn_W_router, L0_ffn_A_stack, L0_ffn_B_stack):
    x = np.asarray(L0_attn_x, np.float32)
    Wqk = np.asarray(L0_attn_Wqk_w, np.float32)
    bqk = np.asarray(L0_attn_Wqk_b, np.float32)
    Wv = np.asarray(L0_attn_Wv_w, np.float32)
    bv = np.asarray(L0_attn_Wv_b, np.float32)
    Wo = np.asarray(L0_attn_Wo_w, np.float32)
    bo = np.asarray(L0_attn_Wo_b, np.float32)
    WB = np.asarray(L0_ffn_W_base, np.float32)
    bB = np.asarray(L0_ffn_b_base, np.float32)
    Wr = np.asarray(L0_ffn_W_router, np.float32)
    A = np.asarray(L0_ffn_A_stack, np.float32)
    Bs = np.asarray(L0_ffn_B_stack, np.float32)

    C = np.ascontiguousarray
    Wqk_r = Wqk.reshape(H, E, 2, D)
    bqk_r = bqk.reshape(H, E, 2)
    consts = {
        "wqT": C(Wqk_r[:, :, 0, :].reshape(H * E, D).T),
        "wkT": C(Wqk_r[:, :, 1, :].reshape(H * E, D).T),
        "wvT": C(Wv.T),
        "woT": C(Wo.T),
        "wbT": C(WB.T),
        "wrT": C(Wr.T),
        "aallT": C(A.reshape(128, D).T),
        "ballT": C(Bs.transpose(0, 2, 1).reshape(128, D)),
        "c_bq": C(bqk_r[:, :, 0].reshape(4, 128).T),
        "c_bk": C(bqk_r[:, :, 1].reshape(4, 128).T),
        "c_bo": C(bo.reshape(32, 128).T),
        "c_bB": C(bB.reshape(32, 128).T),
        "c_bv": C(bv.reshape(1, 512)),
        "c_id32": np.eye(128, dtype=np.float32),
        "c_ones65": np.ones((65, 128), NPBF),
        "c_ones1f": np.ones((1, 128), np.float32),
        "c_o65f": np.ones((65, 64), np.float32),
        "c_onescol": np.ones((128, 2), NPBF),
        "c_e16": np.repeat(np.eye(8, dtype=np.float32), 16, axis=1).astype(NPBF),
        "c_eps": np.full((1, 1), 1e-5, np.float32),
    }

    in_maps = []
    for c in range(8):
        b, half = c // 2, c % 2
        xT = C(x[b].T)
        if half:
            xrot = C(np.concatenate([xT[:, 2048:], xT[:, :2048]], axis=1))
        else:
            xrot = xT
        m = dict(consts)
        m["xrot"] = xrot
        in_maps.append(m)

    if "prog" not in _NC_CACHE:
        _NC_CACHE["prog"] = build_program()
    nc = _NC_CACHE["prog"]
    res = run_bass_kernel_spmd(nc, in_maps, core_ids=list(range(8)))
    out = np.empty_like(x)
    for c in range(8):
        b, half = c // 2, c % 2
        outT_np = np.asarray(res.results[c]["outT"], np.float32)
        out[b, half * T:(half + 1) * T, :] = outT_np.T
    return out


# version 19
# speedup vs baseline: 1.0399x; 1.0399x over previous
import os
import numpy as np
import ml_dtypes
from contextlib import ExitStack

import concourse.bass as bass
import concourse.mybir as mybir
from concourse.bass_utils import run_bass_kernel_spmd

F32 = mybir.dt.float32
BF16 = mybir.dt.bfloat16
AF = mybir.ActivationFunctionType
ALU = mybir.AluOpType
AX = mybir.AxisListType
NPBF = ml_dtypes.bfloat16

D = 4096
T = 2048
NK = 4096
H, E = 8, 64
NDC = D // 128
EPS = 1e-5
SQRT_DH = float(np.sqrt(512.0))

ENGS = ("PE", "ACT", "DVE", "POOL", "SP")
DMA_STREAMS = {"ld": 3, "sc": 3, "sca": 2, "out": 2}
DMAS = tuple("%s%d" % (st, i) for st, n in DMA_STREAMS.items() for i in range(n))


class Res(object):
    __slots__ = ("w", "r")

    def __init__(self):
        self.w = None
        self.r = []

    def rd(self):
        return [self.w] if self.w is not None else []

    def wr(self):
        out = list(self.r)
        if self.w is not None:
            out.append(self.w)
        return out

    def did_read(self, tok):
        if tok is not None:
            self.r.append(tok)

    def did_write(self, tok):
        if tok is not None:
            self.w = tok
            self.r = []


class Sched(object):
    def __init__(self, nc):
        self.nc = nc
        self.sems = {}
        for name in ENGS + DMAS:
            self.sems[name] = nc.alloc_semaphore("sem_" + name)
        self.cnt = dict((n, 0) for n in self.sems)
        self.seen = dict((e, {}) for e in ENGS)
        self.blocks = [dict((e, []) for e in ENGS)]
        self.rr = dict((st, 0) for st in DMA_STREAMS)
        self.pending = dict((e, []) for e in ENGS)

    def new_block(self):
        self.blocks.append(dict((e, []) for e in ENGS))
        for e in ENGS:
            self.seen[e] = dict((k, v) for k, v in self.cnt.items() if k in ENGS)

    def phase_barrier(self):
        self.new_block()
        fence = self.dma_fence(*DMA_STREAMS.keys())
        for e in ENGS:
            self.pending[e] = list(fence)

    def dma_fence(self, *streams):
        out = []
        for st in streams:
            for i in range(DMA_STREAMS[st]):
                sn = "%s%d" % (st, i)
                if self.cnt[sn]:
                    out.append((sn, self.cnt[sn]))
        return out

    def op(self, eng, fn, waits=(), dma=None, tok=True, fwaits=()):
        if dma is not None:
            i = self.rr[dma]
            self.rr[dma] = (i + 1) % DMA_STREAMS[dma]
            semname = "%s%d" % (dma, i)
            waits = list(waits) + [(semname, self.cnt[semname])]
        if self.pending[eng]:
            waits = list(waits) + self.pending[eng]
            self.pending[eng] = []
        wl = [wv for wv in fwaits if wv is not None]
        seen = self.seen[eng]
        for wv in waits:
            if wv is None:
                continue
            sn, v = wv
            if v > seen.get(sn, 0) and v > 0:
                wl.append((sn, v))
                seen[sn] = v
        if dma is None:
            semname = eng
            inc = 1
            make_tok = tok
        else:
            inc = 16
            make_tok = True
        post = None
        if make_tok:
            self.cnt[semname] += inc
            post = self.cnt[semname]
        sems = self.sems

        def emit(e, fn=fn, wl=wl, semname=semname, inc=inc, post=post):
            for sn, v in wl:
                e.wait_ge(sems[sn], v)
            ins = fn(e)
            if post is not None:
                ins.then_inc(sems[semname], inc)
        self.blocks[-1][eng].append(emit)
        return (semname, post) if make_tok else None

    def finish(self):
        sems = self.sems
        fence = self.dma_fence("out")

        def emit(e):
            for sn, v in fence:
                e.wait_ge(sems[sn], v)
        self.blocks[-1]["SP"].append(emit)

    def run(self):
        nc = self.nc
        for blk_ops in self.blocks:
            with nc.Block() as blk:
                for eng, meth in (("PE", blk.tensor), ("ACT", blk.scalar),
                                  ("DVE", blk.vector), ("POOL", blk.gpsimd),
                                  ("SP", blk.sync)):
                    ops = blk_ops[eng]
                    if not ops:
                        continue

                    def body(engine, ops=ops):
                        for f in ops:
                            f(engine)
                    meth(body)


def mm(o, l, r, st, sp):
    return lambda e: e.matmul(o, l, r, start=st, stop=sp, skip_group_check=True)


def build_program():
    KPH = int(os.environ.get("KPH", "9"))

    nc = bass.Bass("TRN2", target_bir_lowering=False)

    di = lambda n, sh, dt: nc.dram_tensor(n, sh, dt, kind="ExternalInput")
    x_in = di("xrot", [D, NK], F32)
    wqT = di("wqT", [D, 512], F32)
    wkT = di("wkT", [D, 512], F32)
    wvT = di("wvT", [D, 512], F32)
    woT = di("woT", [512, D], F32)
    wbT = di("wbT", [D, D], F32)
    wrT = di("wrT", [D, 8], F32)
    aallT = di("aallT", [D, 128], F32)
    ballT = di("ballT", [128, D], F32)
    c_bq = di("c_bq", [128, 4], F32)
    c_bk = di("c_bk", [128, 4], F32)
    c_bo = di("c_bo", [128, 32], F32)
    c_bB = di("c_bB", [128, 32], F32)
    c_bv = di("c_bv", [1, 512], F32)
    c_id32 = di("c_id32", [128, 128], F32)
    c_ones65 = di("c_ones65", [65, 128], BF16)
    c_ones1f = di("c_ones1f", [1, 128], F32)
    c_o65f = di("c_o65f", [65, 64], F32)
    c_onescol = di("c_onescol", [128, 2], BF16)
    c_e16 = di("c_e16", [8, 128], BF16)
    c_eps = di("c_eps", [1, 1], F32)

    outT = nc.dram_tensor("outT", [D, T], F32, kind="ExternalOutput")
    xscr = nc.dram_tensor("xscr", [D, NK], BF16, kind="Internal")
    x2scr = nc.dram_tensor("x2scr", [D, T], BF16, kind="Internal")
    r2scr = nc.dram_tensor("r2scr", [D, T], BF16, kind="Internal")
    xscr_r = xscr.ap().rearrange("(c p) t -> p c t", p=128)
    x2scr_r = x2scr.ap().rearrange("(c p) t -> p c t", p=128)
    r2scr_r = r2scr.ap().rearrange("(c p) t -> p c t", p=128)

    s = Sched(nc)
    xscrR = [Res() for _ in range(16)]
    x2scrR = [Res() for _ in range(4)]
    r2scrR = [[Res() for _ in range(NDC)] for _ in range(2)]

    def wload(dst_bf, dstR_list, src_dram, stg, stgR, n_chunks, cw,
              eng="POOL", stripe=False):
        src_r = src_dram.ap().rearrange("(c p) n -> p c n", p=128)
        per = stg[0].shape[1] // cw
        for g in range(n_chunks // per):
            st_i = g % 2
            dma = ("ld" if st_i == 0 else "sc") if stripe else "ld"
            src = src_r[:, per * g:per * g + per, :]
            tl = s.op("SP", lambda e, d=stg[st_i], src=src, per=per:
                      e.dma_start(d.ap().rearrange("p (c n) -> p c n", c=per),
                                  src),
                      waits=stgR[st_i].wr(), dma=dma)
            stgR[st_i].did_write(tl)
            tc = s.op(eng,
                      lambda e, d=dst_bf[:, per * g * cw:(per * g + per) * cw],
                      srd=stg[st_i][:, 0:per * cw]: e.tensor_copy(d, srd),
                      waits=[tl])
            stgR[st_i].did_read(tc)
            for j in range(per):
                dstR_list[per * g + j].did_write(tc)

    with ExitStack() as st0:
        sb = lambda n, sh, dt: st0.enter_context(nc.sbuf_tensor(n, sh, dt))
        id32 = sb("id32", [128, 128], F32)
        ones65 = sb("ones65", [65, 128], BF16)
        ones1f = sb("ones1f", [1, 128], F32)
        o65f = sb("o65f", [65, 64], F32)
        onescol = sb("onescol", [128, 2], BF16)
        e16 = sb("e16", [8, 128], BF16)
        bq_sb = sb("bq_sb", [128, 4], F32)
        bk_sb = sb("bk_sb", [128, 4], F32)
        bo_sb = sb("bo_sb", [128, 32], F32)
        bB_sb = sb("bB_sb", [128, 32], F32)
        rows = sb("rows", [1, 4 * 1024], F32)
        epsc = sb("epsc", [1, 1], F32)
        rowsRes = [Res() for _ in range(4)]

        for ap_s, ap_d in ((id32, c_id32), (ones65, c_ones65),
                           (ones1f, c_ones1f), (o65f, c_o65f),
                           (onescol, c_onescol), (e16, c_e16),
                           (bq_sb, c_bq), (bk_sb, c_bk), (bo_sb, c_bo),
                           (bB_sb, c_bB),
                           (epsc, c_eps)):
            s.op("SP", lambda e, d=ap_s, src=ap_d: e.dma_start(d[:], src.ap()),
                 dma="ld")
        consts_fence = s.dma_fence("ld")

        def ln_rows(stat_s, stat_q, statSR, statQR, width, coff):
            ro = lambda i: rows[0:1, i * 1024 + coff: i * 1024 + coff + width]

            def rowop(eng, fn, rds, wrt, extra=()):
                waits = list(extra)
                for i in rds:
                    waits += rowsRes[i].rd()
                waits += rowsRes[wrt].wr()
                t = s.op(eng, fn, waits)
                for i in rds:
                    rowsRes[i].did_read(t)
                rowsRes[wrt].did_write(t)
                return t
            tmu = rowop("DVE", lambda e, d=ro(0), srd=stat_s:
                        e.tensor_scalar(d, srd, 1.0 / D, None, ALU.mult),
                        [], 0, statSR.rd())
            statSR.did_read(tmu)
            tms = rowop("DVE", lambda e, d=ro(1), srd=stat_q:
                        e.tensor_scalar(d, srd, 1.0 / D, None, ALU.mult),
                        [], 1, statQR.rd())
            statQR.did_read(tms)
            rowop("DVE", lambda e, d=ro(2), a=ro(0), b=ro(0):
                  e.tensor_tensor(d, a, b, ALU.mult), [0], 2)
            rowop("DVE", lambda e, d=ro(3), a=ro(1), b=ro(2):
                  e.tensor_tensor(d, a, b, ALU.subtract), [1, 2], 3)
            rowop("ACT", lambda e, d=ro(2), srd=ro(3):
                  e.activation(d, srd, AF.Ln, bias=epsc[:]), [3], 2)
            trs = rowop("ACT", lambda e, d=ro(1), srd=ro(2):
                        e.activation(d, srd, AF.Exp, scale=-0.5), [2], 1)
            rowop("DVE", lambda e, d=ro(3), a=ro(0), b=ro(1):
                  e.tensor_tensor(d, a, b, ALU.mult), [0, 1], 3)
            tb2 = rowop("DVE", lambda e, d=ro(0), srd=ro(3):
                        e.tensor_scalar(d, srd, -1.0, None, ALU.mult), [3], 0)
            return ro, trs, tb2

        with ExitStack() as stA:
            sbA = lambda n, sh, dt: stA.enter_context(nc.sbuf_tensor(n, sh, dt))
            OT = [sbA("OT%d" % i, [128, T], BF16) for i in range(4)]
            OTr = [[Res() for _ in range(4)] for _ in range(4)]
            OTtmp = sbA("OTtmp", [64, T], BF16)
            OTtmpr = [Res() for _ in range(4)]
            wo_bf = sbA("wo_bf", [128, 4 * D], BF16)
            woR = [[Res() for _ in range(4)] for _ in range(4)]

            with ExitStack() as stI:
                sbI = lambda n, sh, dt: stI.enter_context(nc.sbuf_tensor(n, sh, dt))
                kT = sbI("kT", [128, 4 * NK], BF16)
                kTr = [[Res() for _ in range(16)] for _ in range(4)]
                V = sbI("V", [128, 32 * 520], BF16)
                Vr = [Res() for _ in range(32)]
                qT = sbI("qT", [128, 4 * T], BF16)
                qTr = [[Res() for _ in range(4)] for _ in range(4)]

                vones = V.ap().rearrange("p (kc h x) -> p kc h x",
                                         kc=32, h=8)[:, :, :, 64:65]
                t = s.op("POOL", lambda e: e.memset(vones, 1.0),
                         waits=consts_fence)
                for r in Vr:
                    r.did_write(t)

                with ExitStack() as stP:
                    sbP = lambda n, sh, dt: stP.enter_context(nc.sbuf_tensor(n, sh, dt))
                    wk_bf = sbP("wk_bf", [128, 32 * 512], BF16)
                    wkR = [Res() for _ in range(32)]
                    wstg = [sbP("wstg%d" % i, [128, 512], F32) for i in range(2)]
                    wstgR = [Res(), Res()]
                    xstg = [sbP("xstg%d" % i, [128, 1024], F32) for i in range(3)]
                    xstgR = [Res() for _ in range(3)]
                    xbf = [sbP("xbf%d" % i, [128, 1024], BF16) for i in range(3)]
                    xbfR = [Res() for _ in range(3)]
                    wload(wk_bf, wkR, wkT, wstg, wstgR, 32, 512,
                          eng="DVE", stripe=True)
                    with nc.psum_tensor("pk0", [128, 512], F32) as pk0, \
                         nc.psum_tensor("pk1", [128, 512], F32) as pk1, \
                         nc.psum_tensor("pk2", [128, 512], F32) as pk2, \
                         nc.psum_tensor("pk3", [128, 512], F32) as pk3:
                        psK = [pk0, pk1, pk2, pk3]
                        psKR = [Res() for _ in range(4)]
                        for kg in range(16):
                            for g in range(8):
                                bi = (kg * 8 + g) % 3
                                bj = bi
                                src = x_in.ap().rearrange("(c p) n -> p c n", p=128)[
                                    :, 4 * g:4 * g + 4, 256 * kg:256 * kg + 256]
                                tl = s.op("SP", lambda e, d=xstg[bi], src=src:
                                          e.dma_start(d.ap().rearrange(
                                              "p (c n) -> p c n", c=4), src),
                                          waits=xstgR[bi].wr(), dma="ld")
                                xstgR[bi].did_write(tl)
                                if g % 2:
                                    tc = s.op("ACT", lambda e, d=xbf[bj][:],
                                              srd=xstg[bi][:]:
                                              e.activation(d, srd, AF.Copy),
                                              waits=[tl] + xbfR[bj].wr())
                                else:
                                    tc = s.op("POOL", lambda e, d=xbf[bj][:],
                                              srd=xstg[bi][:]:
                                              e.tensor_copy(d, srd),
                                              waits=[tl] + xbfR[bj].wr())
                                xstgR[bi].did_read(tc)
                                xbfR[bj].did_write(tc)
                                tw = s.op("ACT", lambda e, kg=kg, g=g, bj=bj:
                                          e.dma_start(
                                              xscr_r[:, 4 * g:4 * g + 4,
                                                     256 * kg:256 * kg + 256],
                                              xbf[bj].ap().rearrange(
                                                  "p (c n) -> p c n", c=4)),
                                          waits=[tc], dma="sca")
                                xbfR[bj].did_read(tw)
                                xscrR[kg].did_write(tw)
                                for j in range(4):
                                    dc = 4 * g + j
                                    for mc in range(4):
                                        tk = s.op("PE", mm(
                                            psK[mc][:, 0:256],
                                            wk_bf[:, dc * 512 + mc * 128:
                                                  dc * 512 + mc * 128 + 128],
                                            xbf[bj][:, j * 256: j * 256 + 256],
                                            dc == 0, dc == 31),
                                            waits=([tc] if mc == 0 else []) +
                                            wkR[dc].rd() +
                                            (psKR[mc].wr() if dc == 0 else []),
                                            tok=(j == 3))
                                        if j == 3:
                                            xbfR[bj].did_read(tk)
                                            if dc == 31:
                                                psKR[mc].did_write(tk)
                                        wkR[dc].did_read(tk)
                            for mc in range(4):
                                te = s.op("ACT",
                                          lambda e,
                                          d=kT[:, mc * NK + kg * 256:
                                               mc * NK + kg * 256 + 256],
                                          srd=psK[mc][:, 0:256],
                                          b=bk_sb[:, mc:mc + 1]:
                                          e.activation(d, srd, AF.Identity, bias=b),
                                          waits=psKR[mc].rd())
                                psKR[mc].did_read(te)
                                kTr[mc][kg].did_write(te)

                s.phase_barrier()
                if KPH < 1:
                    s.finish(); s.run(); return nc
                with ExitStack() as stP:
                    sbP = lambda n, sh, dt: stP.enter_context(nc.sbuf_tensor(n, sh, dt))
                    bv_sb = sbP("bv_sb", [1, 512], F32)
                    bvbf = sbP("bvbf", [1, 512], BF16)
                    t_bvl = s.op("SP", lambda e: e.dma_start(bv_sb[:],
                                                             c_bv.ap()),
                                 dma="ld")
                    t_bv = s.op("POOL", lambda e: e.tensor_copy(bvbf[:],
                                                                bv_sb[:]),
                                waits=[t_bvl])
                    wv_bf = sbP("wv_bf", [128, 32 * 512], BF16)
                    wvR = [Res() for _ in range(32)]
                    wstg = [sbP("wstgv%d" % i, [128, 1024], F32) for i in range(2)]
                    wstgR = [Res(), Res()]
                    xvb = [sbP("xvb%d" % i, [128, 1024], BF16) for i in range(4)]
                    xvbR = [Res() for _ in range(4)]
                    wload(wv_bf, wvR, wvT, wstg, wstgR, 32, 512,
                          eng="DVE", stripe=True)
                    with nc.psum_tensor("pv0", [128, 512], F32) as pv0, \
                         nc.psum_tensor("pv1", [128, 512], F32) as pv1:
                        psV = [pv0, pv1]
                        psVR = [Res(), Res()]
                        for kg in range(16):
                            for g in range(8):
                                bi = (kg * 8 + g) % 3
                                bj = bi
                                tl = s.op("SP", lambda e, kg=kg, g=g, bi=bi:
                                          e.dma_start(
                                              xvb[bi].ap().rearrange(
                                                  "p (c n) -> p c n", c=4),
                                              xscr_r[:, 4 * g:4 * g + 4,
                                                     256 * kg:256 * kg + 256]),
                                          waits=xvbR[bi].wr() + xscrR[kg].rd(),
                                          dma="sc")
                                xvbR[bi].did_write(tl)
                                for j in range(4):
                                    dc = 4 * g + j
                                    for ti in range(2):
                                        tk = s.op("PE", mm(
                                            psV[ti][:],
                                            xvb[bi][:, j * 256 + ti * 128:
                                                    j * 256 + ti * 128 + 128],
                                            wv_bf[:, dc * 512: dc * 512 + 512],
                                            dc == 0, False),
                                            waits=([tl] if ti == 0 else []) +
                                            wvR[dc].rd() +
                                            (psVR[ti].wr() if dc == 0 else []),
                                            tok=(j == 3))
                                        if j == 3:
                                            xvbR[bi].did_read(tk)
                                        wvR[dc].did_read(tk)
                            for ti in range(2):
                                kc = kg * 2 + ti
                                tb = s.op("PE", mm(psV[ti][:], ones65[0:1, :],
                                                   bvbf[:], False, True),
                                          waits=[t_bv])
                                psVR[ti].did_write(tb)
                                dst = V.ap().rearrange("p (kc h x) -> p kc h x",
                                                       kc=32, h=8)[:, kc, :, 0:64]
                                te = s.op("ACT",
                                          lambda e, d=dst,
                                          srd=psV[ti].ap().rearrange(
                                              "p (h x) -> p h x", h=8):
                                          e.activation(d, srd, AF.Copy),
                                          waits=[tb])
                                psVR[ti].did_read(te)
                                Vr[kc].did_write(te)

                s.phase_barrier()
                if KPH < 2:
                    s.finish(); s.run(); return nc
                with ExitStack() as stP:
                    sbP = lambda n, sh, dt: stP.enter_context(nc.sbuf_tensor(n, sh, dt))
                    wq_bf = sbP("wq_bf", [128, 32 * 512], BF16)
                    wqR = [Res() for _ in range(32)]
                    wstg = [sbP("wstgq%d" % i, [128, 1024], F32) for i in range(2)]
                    wstgR = [Res(), Res()]
                    xqb = [sbP("xqb%d" % i, [128, 2048], BF16) for i in range(3)]
                    xqbR = [Res() for _ in range(3)]
                    wload(wq_bf, wqR, wqT, wstg, wstgR, 32, 512,
                          eng="DVE", stripe=True)
                    with nc.psum_tensor("pq0", [128, 512], F32) as pq0, \
                         nc.psum_tensor("pq1", [128, 512], F32) as pq1, \
                         nc.psum_tensor("pq2", [128, 512], F32) as pq2, \
                         nc.psum_tensor("pq3", [128, 512], F32) as pq3:
                        psQ = [pq0, pq1, pq2, pq3]
                        psQR = [Res() for _ in range(4)]
                        for qg in range(4):
                            for g in range(8):
                                bi = g % 3
                                tl = s.op("SP", lambda e, qg=qg, g=g, bi=bi:
                                          e.dma_start(
                                              xqb[bi].ap().rearrange(
                                                  "p (c n) -> p c n", c=4),
                                              xscr_r[:, 4 * g:4 * g + 4,
                                                     512 * qg:512 * qg + 512]),
                                          waits=xqbR[bi].wr() +
                                          xscrR[2 * qg].rd() + xscrR[2 * qg + 1].rd(),
                                          dma="sc")
                                xqbR[bi].did_write(tl)
                                for j in range(4):
                                    dc = 4 * g + j
                                    for mc in range(4):
                                        tk = s.op("PE", mm(
                                            psQ[mc][:],
                                            wq_bf[:, dc * 512 + mc * 128:
                                                  dc * 512 + mc * 128 + 128],
                                            xqb[bi][:, j * 512: j * 512 + 512],
                                            dc == 0, dc == 31),
                                            waits=([tl] if mc == 0 else []) +
                                            wqR[dc].rd() +
                                            (psQR[mc].wr() if dc == 0 else []),
                                            tok=(j == 3))
                                        if j == 3:
                                            xqbR[bi].did_read(tk)
                                            if dc == 31:
                                                psQR[mc].did_write(tk)
                                        wqR[dc].did_read(tk)
                            for mc in range(4):
                                te = s.op("ACT",
                                          lambda e,
                                          d=qT[:, mc * T + qg * 512:
                                               mc * T + qg * 512 + 512],
                                          srd=psQ[mc][:], b=bq_sb[:, mc:mc + 1]:
                                          e.activation(d, srd, AF.Identity, bias=b),
                                          waits=psQR[mc].rd())
                                psQR[mc].did_read(te)
                                qTr[mc][qg].did_write(te)

                s.phase_barrier()
                if KPH < 3:
                    s.finish(); s.run(); return nc
                with ExitStack() as stP:
                    sbP = lambda n, sh, dt: stP.enter_context(nc.sbuf_tensor(n, sh, dt))
                    wstgo = [sbP("wstgo0", [128, 1024], F32)]
                    wstgoR = [Res()]
                    probs = sbP("probs", [128, 8192], BF16)
                    probsR = [[Res() for _ in range(8)] for _ in range(2)]
                    racc = sbP("racc", [128, 16], F32)
                    raccR = [Res(), Res()]
                    mcol = sbP("mcol", [128, 2], F32)
                    mcolR = [Res(), Res()]
                    mrow = [sbP("mrow%d" % i, [65, T], BF16) for i in range(2)]
                    mrowR = [[[Res() for _ in range(4)] for _ in range(2)]
                             for _ in range(2)]
                    mtmp = sbP("mtmp", [1, T], BF16)
                    mtmpR = [Res() for _ in range(4)]
                    rrec = sbP("rrec", [65, 2048], F32)
                    rrecR = [Res(), Res()]
                    rrecR2 = [Res(), Res()]
                    rbc = sbP("rbc", [64, 1024], F32)
                    rbcR = [Res(), Res()]

                    for hh in range(4):
                        for c in range(4):
                            st_i = 0
                            src_ = woT.ap()[c * 128:(c + 1) * 128,
                                            hh * 1024:(hh + 1) * 1024]
                            tl = s.op("SP", lambda e, d=wstgo[st_i], src=src_:
                                      e.dma_start(d[:], src),
                                      waits=wstgoR[st_i].wr(), dma="ld")
                            wstgoR[st_i].did_write(tl)
                            tc = s.op("POOL",
                                      lambda e,
                                      d=wo_bf[:, c * D + hh * 1024:
                                              c * D + hh * 1024 + 1024],
                                      srd=wstgo[st_i][:]: e.tensor_copy(d, srd),
                                      waits=[tl])
                            wstgoR[st_i].did_read(tc)
                            woR[c][hh].did_write(tc)

                    with nc.psum_tensor("sS", [128, 1024], F32) as Sg, \
                         nc.psum_tensor("rb0", [128, 1024], F32) as Rb0, \
                         nc.psum_tensor("rb1", [128, 1024], F32) as Rb1, \
                         nc.psum_tensor("a0", [65, 512], F32) as A0, \
                         nc.psum_tensor("a1", [65, 512], F32) as A1:
                        SgR = [Res(), Res()]
                        Rbig = [Rb0, Rb1]
                        RbR = [[Res(), Res()], [Res(), Res()]]
                        Ab = [A0, A1]
                        AbR = [Res(), Res()]

                        def mk_pair(hp, hx, qc, kg):
                            def run():
                                base = hx * 64
                                half = kg % 2
                                tmm = s.op("PE", mm(
                                    Sg[:, half * 512: half * 512 + 512],
                                    qT[base:base + 64,
                                       hp * T + qc * 128:
                                       hp * T + qc * 128 + 128],
                                    kT[base:base + 64,
                                       hp * NK + kg * 512:
                                       hp * NK + kg * 512 + 512],
                                    True, True),
                                    waits=SgR[half].wr() +
                                    qTr[hp][qc // 4].rd() +
                                    kTr[hp][2 * kg].rd() +
                                    kTr[hp][2 * kg + 1].rd())
                                SgR[half].did_write(tmm)
                                tr = s.op("DVE",
                                          lambda e,
                                          d=racc[:, hx * 8 + kg: hx * 8 + kg + 1],
                                          srd=Sg[:, half * 512: half * 512 + 512]:
                                          e.reduce_max(d, srd, AX.X, negate=True),
                                          waits=[tmm] +
                                          (raccR[hx].wr() if kg == 0 else []))
                                SgR[half].did_read(tr)
                                raccR[hx].did_write(tr)
                            return run

                        def mk_epia(hp, hx, qc):
                            def run():
                                tm = s.op("DVE",
                                          lambda e, d=mcol[:, hx:hx + 1],
                                          srd=racc[:, hx * 8: hx * 8 + 8]:
                                          e.tensor_reduce(d, srd, AX.X, ALU.min),
                                          waits=raccR[hx].rd() + mcolR[hx].wr())
                                raccR[hx].did_read(tm)
                                mcolR[hx].did_write(tm)
                            return run

                        def mk_epi(hp, hx, qc):
                            def run():
                                mr = mrow[hp % 2]
                                tm = mcolR[hx].rd()[0] if mcolR[hx].rd() else None
                                pt = Sg.ap()[0:1, 0:128]
                                tt = s.op("PE",
                                          lambda e, o=pt, i=mcol[:, hx:hx + 1],
                                          idn=id32[:]: e.transpose(o, i, idn),
                                          waits=([tm] if tm else []) +
                                          SgR[0].wr() + consts_fence)
                                SgR[0].did_write(tt)
                                mcolR[hx].did_read(tt)
                                if hx == 0:
                                    dmr = mr[0:1, qc * 128: qc * 128 + 128]
                                    wmr = mrowR[hp % 2][0][qc // 4].wr()
                                else:
                                    dmr = mtmp[0:1, qc * 128: qc * 128 + 128]
                                    wmr = mtmpR[qc // 4].wr()
                                tc2 = s.op("DVE",
                                           lambda e, d=dmr, srd=pt:
                                           e.tensor_copy(d, srd),
                                           waits=[tt] + wmr)
                                SgR[0].did_read(tc2)
                                if hx == 0:
                                    mrowR[hp % 2][0][qc // 4].did_write(tc2)
                                else:
                                    mtmpR[qc // 4].did_write(tc2)
                            return run

                        def mk_tdm(hp, qg):
                            def run():
                                mr = mrow[hp % 2]
                                tdm = s.op("SP",
                                           lambda e,
                                           d=mr[64:65, qg * 512: qg * 512 + 512],
                                           srd=mtmp[0:1, qg * 512: qg * 512 + 512]:
                                           e.dma_start(d, srd),
                                           waits=mtmpR[qg].rd() +
                                           mrowR[hp % 2][1][qg].wr(),
                                           dma="sc")
                                mtmpR[qg].did_read(tdm)
                                mrowR[hp % 2][1][qg].did_write(tdm)
                            return run

                        units = []
                        for hp in range(4):
                            for qg in range(4):
                                gid = hp * 4 + qg
                                for hx in range(2):
                                    for qc in range(4 * qg, 4 * qg + 4):
                                        for kg in range(8):
                                            units.append((gid, mk_pair(hp, hx, qc, kg)))
                                        units.append((gid, mk_epia(hp, hx, qc)))
                                        units.append((gid, mk_epi(hp, hx, qc)))
                                    if hx == 1:
                                        units.append((gid, mk_tdm(hp, qg)))
                        upos = [0]

                        def drain_until(gid):
                            while upos[0] < len(units) and units[upos[0]][0] <= gid:
                                units[upos[0]][1]()
                                upos[0] += 1

                        def drain_n(n):
                            for _ in range(n):
                                if upos[0] < len(units):
                                    units[upos[0]][1]()
                                    upos[0] += 1

                        evac_pend = []
                        for hp in range(4):
                            mr = mrow[hp % 2]
                            for qg in range(4):
                                drain_until(hp * 4 + qg)
                                pend = [[], []]

                                def emit_av(hx, kc, te, col, hp=hp):
                                    h = 2 * hp + hx
                                    ta = s.op("PE", mm(
                                        Ab[hx][:],
                                        V[:, kc * 520 + h * 65:
                                          kc * 520 + h * 65 + 65],
                                        probs[:, col: col + 512],
                                        kc == 0, kc == 31),
                                        waits=[te] + Vr[kc].rd() +
                                        (AbR[hx].wr() if kc == 0 else []),
                                        tok=(kc == 31))
                                    probsR[hx][kc % 8].did_read(ta)
                                    if kc == 31:
                                        AbR[hx].did_write(ta)
                                for kp in range(16):
                                    if kp in (1, 2, 3) and evac_pend:
                                        evac_pend.pop(0)()
                                    for k2 in range(2):
                                        kc = 2 * kp + k2
                                        for hx in range(2):
                                            s.op("PE", mm(
                                                Rbig[hx][:, k2 * 512:
                                                         k2 * 512 + 512],
                                                kT[hx * 64: hx * 64 + 64,
                                                   hp * NK + kc * 128:
                                                   hp * NK + kc * 128 + 128],
                                                qT[hx * 64: hx * 64 + 64,
                                                   hp * T + qg * 512:
                                                   hp * T + qg * 512 + 512],
                                                True, False),
                                                waits=RbR[hx][k2].wr(),
                                                tok=False)
                                        t2s = []
                                        for hx in range(2):
                                            t2 = s.op("PE", mm(
                                                Rbig[hx][:, k2 * 512:
                                                         k2 * 512 + 512],
                                                ones65[hx * 64: hx * 64 + 1, :],
                                                mr[hx * 64: hx * 64 + 1,
                                                   qg * 512: qg * 512 + 512],
                                                False, True),
                                                waits=mrowR[hp % 2][hx][qg].rd())
                                            mrowR[hp % 2][hx][qg].did_read(t2)
                                            RbR[hx][k2].did_write(t2)
                                            t2s.append(t2)
                                        for hx in range(2):
                                            col = hx * 4096 + (kc % 8) * 512
                                            te = s.op("ACT",
                                                      lambda e,
                                                      d=probs[:, col: col + 512],
                                                      srd=Rbig[hx][:, k2 * 512:
                                                                  k2 * 512 + 512]:
                                                      e.activation(d, srd, AF.Exp),
                                                      waits=[t2s[hx]] +
                                                      probsR[hx][kc % 8].wr())
                                            RbR[hx][k2].did_read(te)
                                            probsR[hx][kc % 8].did_write(te)
                                            pend[hx].append((hx, kc, te, col))
                                        drain_n(1)
                                    drain_n(1)
                                    for hx in range(2):
                                        if kp >= 3:
                                            while len(pend[hx]) > 4:
                                                emit_av(*pend[hx].pop(0))
                                        drain_n(1)
                                for hx in range(2):
                                    for item in pend[hx]:
                                        emit_av(*item)

                                def emit_evac_a(hx, st, hp=hp, qg=qg):
                                    c0 = hx * 1024
                                    t1 = s.op("DVE",
                                              lambda e,
                                              d=rrec[64:65, c0: c0 + 512],
                                              srd=Ab[hx].ap()[64:65, 0:512]:
                                              e.tensor_scalar(d, srd, SQRT_DH,
                                                              None, ALU.mult),
                                              waits=AbR[hx].rd(),
                                              fwaits=rrecR[hx].wr())
                                    AbR[hx].did_read(t1)
                                    rrecR[hx].did_write(t1)
                                    t2 = s.op("DVE",
                                              lambda e,
                                              d=rrec[64:65, c0 + 512: c0 + 1024],
                                              srd=rrec[64:65, c0: c0 + 512]:
                                              e.reciprocal(d, srd),
                                              waits=[t1], fwaits=rrecR2[hx].wr())
                                    rrecR[hx].did_read(t2)
                                    rrecR2[hx].did_write(t2)
                                    st["t2"] = t2

                                def emit_evac_b(hx, st, hp=hp, qg=qg):
                                    t2 = st["t2"]
                                    c0 = hx * 1024
                                    t3 = s.op("PE", mm(Sg.ap()[0:64, 512:1024],
                                                       o65f[64:65, :],
                                                       rrec[64:65,
                                                            c0 + 512: c0 + 1024],
                                                       True, True),
                                              waits=[t2] + SgR[1].wr())
                                    rrecR2[hx].did_read(t3)
                                    SgR[1].did_write(t3)
                                    t4 = s.op("DVE",
                                              lambda e,
                                              d=rbc[:, hx * 512: hx * 512 + 512],
                                              srd=Sg.ap()[0:64, 512:1024]:
                                              e.tensor_copy(d, srd),
                                              waits=[t3], fwaits=rbcR[hx].wr())
                                    SgR[1].did_read(t4)
                                    rbcR[hx].did_write(t4)
                                    st["t4"] = t4

                                def emit_evac_c(hx, st, hp=hp, qg=qg):
                                    t4 = st["t4"]
                                    if hx == 0:
                                        dst = OT[hp][0:64, qg * 512: qg * 512 + 512]
                                        wx = OTr[hp][qg].wr()
                                    else:
                                        dst = OTtmp[0:64, qg * 512: qg * 512 + 512]
                                        wx = OTtmpr[qg].wr()
                                    t5 = s.op("DVE",
                                              lambda e, d=dst,
                                              a=Ab[hx].ap()[0:64, 0:512],
                                              b=rbc[:, hx * 512: hx * 512 + 512]:
                                              e.tensor_tensor(d, a, b, ALU.mult),
                                              waits=[t4] + AbR[hx].rd() + wx)
                                    AbR[hx].did_read(t5)
                                    rbcR[hx].did_read(t5)
                                    if hx == 0:
                                        OTr[hp][qg].did_write(t5)
                                    else:
                                        OTtmpr[qg].did_write(t5)
                                st0_, st1_ = {}, {}
                                evac_pend.append(
                                    lambda a=emit_evac_a, s0=st0_, s1=st1_:
                                    (a(0, s0), a(1, s1)))
                                evac_pend.append(
                                    lambda b=emit_evac_b, s0=st0_, s1=st1_:
                                    (b(0, s0), b(1, s1)))
                                evac_pend.append(
                                    lambda c=emit_evac_c, s0=st0_, s1=st1_:
                                    (c(0, s0), c(1, s1)))
                            for f in evac_pend:
                                f()
                            del evac_pend[:]
                            tmv = s.op("SP", lambda e, hp=hp:
                                       e.dma_start(OT[hp][64:128, :], OTtmp[:]),
                                       waits=[tk for r in OTtmpr
                                              for tk in r.rd()] +
                                       [tk for qg in range(4)
                                        for tk in OTr[hp][qg].wr()],
                                       dma="sc")
                            for qg in range(4):
                                OTtmpr[qg].did_read(tmv)
                                OTr[hp][qg].did_write(tmv)

            s.phase_barrier()
            if KPH < 4:
                s.finish(); s.run(); return nc
            with ExitStack() as stP:
                sbP = lambda n, sh, dt: stP.enter_context(nc.sbuf_tensor(n, sh, dt))
                xq = sbP("xq2", [128, 32 * 512], BF16)
                xqR = [Res() for _ in range(32)]
                res = sbP("res", [128, 32 * 512], BF16)
                resR = [Res() for _ in range(32)]
                sq = sbP("sq", [128, 4 * 512], BF16)
                sqR = [Res() for _ in range(4)]
                x2g = sbP("x2g", [128, 32 * 512], BF16)
                x2gR = [Res() for _ in range(32)]
                abc = sbP("abc", [128, 512], F32)
                abcR = Res()
                bbc = sbP("bbc", [128, 512], F32)
                bbcR = Res()
                tmpf = [sbP("tmpf%d" % i, [128, 512], F32) for i in range(2)]
                tmpfR = [Res(), Res()]

                with nc.psum_tensor("po0", [128, 512], F32) as po0, \
                     nc.psum_tensor("po1", [128, 512], F32) as po1, \
                     nc.psum_tensor("po2", [128, 512], F32) as po2, \
                     nc.psum_tensor("pstS", [1, 512], F32) as pstS, \
                     nc.psum_tensor("pstQ", [1, 512], F32) as pstQ, \
                     nc.psum_tensor("pbc0", [128, 512], F32) as pbc0, \
                     nc.psum_tensor("pbc1", [128, 512], F32) as pbc1:
                    po = [po0, po1, po2]
                    poR = [Res(), Res(), Res()]
                    pstSR = Res()
                    pstQR = Res()
                    pbcR_ = [Res(), Res()]
                    opend = []
                    for qg in range(4):
                        for g in range(8):
                            tl = s.op("SP", lambda e, qg=qg, g=g: e.dma_start(
                                xq.ap().rearrange("p (c n) -> p c n", c=32)[
                                    :, 4 * g:4 * g + 4, :],
                                xscr_r[:, 4 * g:4 * g + 4,
                                       qg * 512: qg * 512 + 512]),
                                waits=[tk for j in range(4)
                                       for tk in xqR[4 * g + j].wr()] +
                                xscrR[2 * qg].rd() + xscrR[2 * qg + 1].rd(),
                                dma="sc")
                            for j in range(4):
                                xqR[4 * g + j].did_write(tl)
                        ppend = []
                        def emit_stats(mc, te, ts, sqi):
                            t_s = s.op("PE", mm(pstS[:], onescol[:, 0:1],
                                                res[:, mc * 512: mc * 512 + 512],
                                                mc == 0, mc == NDC - 1),
                                       waits=[te] +
                                       (pstSR.wr() if mc == 0 else []))
                            t_q = s.op("PE", mm(pstQ[:], onescol[:, 1:2],
                                                sq[:, sqi * 512: sqi * 512 + 512],
                                                mc == 0, mc == NDC - 1),
                                       waits=[ts] +
                                       (pstQR.wr() if mc == 0 else []))
                            resR[mc].did_read(t_s)
                            sqR[sqi].did_read(t_q)
                            if mc == NDC - 1:
                                pstSR.did_write(t_s)
                                pstQR.did_write(t_q)
                        for mc in range(NDC):
                            for _ in range(2):
                                if opend:
                                    opend.pop(0)()
                            pi = mc % 3
                            w0 = poR[pi].wr()
                            last = None
                            for hp_ in range(4):
                                last = s.op("PE", mm(
                                    po[pi][:],
                                    wo_bf[:, hp_ * D + mc * 128:
                                          hp_ * D + mc * 128 + 128],
                                    OT[hp_][:, qg * 512: qg * 512 + 512],
                                    hp_ == 0, hp_ == 3),
                                    waits=(w0 if hp_ == 0 else []) +
                                    woR[hp_][mc // 8].rd() + OTr[hp_][qg].rd(),
                                    tok=(hp_ == 3))
                            poR[pi].did_write(last)
                            for hp_ in range(4):
                                woR[hp_][mc // 8].did_read(last)
                                OTr[hp_][qg].did_read(last)
                            if len(ppend) > 2:
                                emit_stats(*ppend.pop(0))
                            te = s.op("DVE",
                                      lambda e,
                                      d=res[:, mc * 512: mc * 512 + 512],
                                      a=po[pi][:], b=bo_sb[:, mc:mc + 1],
                                      c2=xq[:, mc * 512: mc * 512 + 512]:
                                      e.scalar_tensor_tensor(d, a, b, c2,
                                                             ALU.add, ALU.add),
                                      waits=[last] + xqR[mc].rd() + resR[mc].wr())
                            poR[pi].did_read(te)
                            xqR[mc].did_read(te)
                            resR[mc].did_write(te)
                            sqi = mc % 4
                            ts = s.op("ACT",
                                      lambda e, d=sq[:, sqi * 512: sqi * 512 + 512],
                                      a=res[:, mc * 512: mc * 512 + 512]:
                                      e.activation(d, a, AF.Square),
                                      waits=[te] + sqR[sqi].wr())
                            resR[mc].did_read(ts)
                            sqR[sqi].did_write(ts)
                            ppend.append((mc, te, ts, sqi))
                        while ppend:
                            emit_stats(*ppend.pop(0))
                        ro, trs, tb2 = ln_rows(pstS[:], pstQ[:], pstSR, pstQR,
                                               512, 0)

                        def mk_bcast(ro=ro, trs=trs, tb2=tb2):
                            def f():
                                tba = s.op("PE", mm(pbc0[:], ones1f[:], ro(1),
                                                    True, True),
                                           waits=[trs] + pbcR_[0].wr() +
                                           consts_fence)
                                rowsRes[1].did_read(tba)
                                pbcR_[0].did_write(tba)
                                tbb = s.op("PE", mm(pbc1[:], ones1f[:], ro(0),
                                                    True, True),
                                           waits=[tb2] + pbcR_[1].wr())
                                rowsRes[0].did_read(tbb)
                                pbcR_[1].did_write(tbb)
                                tca = s.op("DVE", lambda e, d=abc[:], srd=pbc0[:]:
                                           e.tensor_copy(d, srd),
                                           waits=[tba] + abcR.wr())
                                pbcR_[0].did_read(tca)
                                abcR.did_write(tca)
                                tcb = s.op("DVE", lambda e, d=bbc[:], srd=pbc1[:]:
                                           e.tensor_copy(d, srd),
                                           waits=[tbb] + bbcR.wr())
                                pbcR_[1].did_read(tcb)
                                bbcR.did_write(tcb)
                            return f
                        opend.append(mk_bcast())
                        for mc in range(NDC):
                            def mk_apply(mc=mc):
                                def f():
                                    ti = mc % 2
                                    t1 = s.op("POOL",
                                              lambda e, d=tmpf[ti][:],
                                              a=res[:, mc * 512: mc * 512 + 512],
                                              b=abc[:]:
                                              e.tensor_tensor(d, a, b, ALU.mult),
                                              waits=resR[mc].rd() + abcR.rd() +
                                              tmpfR[ti].wr())
                                    resR[mc].did_read(t1)
                                    abcR.did_read(t1)
                                    tmpfR[ti].did_write(t1)
                                    t2 = s.op("DVE",
                                              lambda e,
                                              d=x2g[:, mc * 512: mc * 512 + 512],
                                              a=tmpf[ti][:], b=bbc[:]:
                                              e.tensor_tensor(d, a, b, ALU.add),
                                              waits=[t1] + bbcR.rd() +
                                              x2gR[mc].wr())
                                    tmpfR[ti].did_read(t2)
                                    bbcR.did_read(t2)
                                    x2gR[mc].did_write(t2)
                                return f
                            opend.append(mk_apply())
                            if mc % 4 == 3:
                                def mk_spill(qg=qg, g=mc // 4):
                                    def f():
                                        tw = s.op("ACT",
                                                  lambda e, qg=qg, g=g:
                                                  e.dma_start(
                                                      x2scr_r[
                                                          :, 4 * g:4 * g + 4,
                                                          qg * 512:
                                                          qg * 512 + 512],
                                                      x2g.ap().rearrange(
                                                          "p (c n) -> p c n",
                                                          c=32)[
                                                          :, 4 * g:4 * g + 4, :]),
                                                  waits=[tk for j in range(4)
                                                         for tk in
                                                         x2gR[4 * g + j].rd()],
                                                  dma="sca")
                                        for j in range(4):
                                            x2gR[4 * g + j].did_read(tw)
                                        x2scrR[qg].did_write(tw)
                                    return f
                                opend.append(mk_spill())
                    while opend:
                        opend.pop(0)()

        s.phase_barrier()
        if KPH < 5:
            s.finish(); s.run(); return nc
        with ExitStack() as stP:
            sbP = lambda n, sh, dt: stP.enter_context(nc.sbuf_tensor(n, sh, dt))
            x2h = sbP("x2h", [128, 32 * 1024], BF16)
            x2hR = [Res() for _ in range(32)]
            wq_bf = [sbP("wqf%d" % i, [128, 32 * 256], BF16) for i in range(2)]
            wqR = [[Res() for _ in range(32)] for _ in range(2)]
            wstg = [sbP("wstgf%d" % i, [128, 1024], F32) for i in range(2)]
            wstgR = [Res(), Res()]
            ar_bf = sbP("ar_bf", [128, 32 * 128], BF16)
            arR = Res()
            br_bf = sbP("br_bf", [128, 32 * 128], BF16)
            brR = Res()
            wr_bf = sbP("wr_bf", [128, 32 * 8], BF16)
            wrR = Res()
            hs = sbP("hs", [128, 1024], BF16)
            hsR = Res()
            ce = sbP("ce", [128, 1024], BF16)
            ceR = Res()
            lgt = sbP("lgt", [128, 64], F32)
            lgtR = Res()
            lg2 = sbP("lg2", [128, 64], F32)
            lg2R = Res()
            xpb = sbP("xpb", [128, 64], F32)
            xpbR = Res()
            prb = sbP("prb", [128, 64], F32)
            prbR = Res()
            mskb = sbP("mskb", [128, 64], F32)
            mskbR = Res()
            stt = sbP("stt", [128, 64], F32)
            sttR = Res()
            cmb = sbP("cmb", [128, 64], F32)
            cmbR = Res()
            cmT = sbP("cmT", [8, 1024], BF16)
            cmTR = Res()
            lg8 = sbP("lg8", [8, 1024], F32)
            lg8R = Res()
            scc = sbP("scc", [128, 8], F32)
            sccR = Res()
            res2c = [sbP("res2c%d" % i, [128, 1024], BF16) for i in range(2)]
            res2cR = [Res(), Res()]
            res2d = [sbP("res2d%d" % i, [128, 1024], BF16) for i in range(2)]
            res2dR = [Res(), Res()]
            sqc = [sbP("sqc%d" % i, [128, 1024], BF16) for i in range(2)]
            sqcR = [Res(), Res()]
            a2 = sbP("a2", [128, 1024], F32)
            a2R = Res()
            b2 = sbP("b2", [128, 1024], F32)
            b2R = Res()
            tmpo = [sbP("tmpo%d" % i, [128, 1024], F32) for i in range(2)]
            tmpoR = [Res(), Res()]
            outst = [sbP("outst%d" % i, [128, 1024], F32) for i in range(2)]
            outstR = [Res(), Res()]

            def small_load(dst, dstR, src3, n):
                per = max(1, 1024 // n)
                toks = []
                for p in range((32 + per - 1) // per):
                    c0 = p * per
                    cw = min(per, 32 - c0)
                    w = cw * n
                    st_i = p % 2
                    tl = s.op("SP",
                              lambda e, d=wstg[st_i].ap().rearrange(
                                  "q (c n) -> q c n", n=n)[:, 0:cw, :],
                              src=src3[:, c0:c0 + cw, :]: e.dma_start(d, src),
                              waits=wstgR[st_i].wr(), dma="ld")
                    wstgR[st_i].did_write(tl)
                    tc = s.op("POOL",
                              lambda e, d=dst[:, c0 * n:c0 * n + w],
                              srd=wstg[st_i][:, 0:w]: e.tensor_copy(d, srd),
                              waits=[tl])
                    wstgR[st_i].did_read(tc)
                    toks.append(tc)
                dstR.did_write(toks[-1])

            small_load(ar_bf, arR,
                       aallT.ap().rearrange("(c p) n -> p c n", p=128), 128)
            small_load(br_bf, brR,
                       ballT.ap().rearrange("p (c n) -> p c n", n=128), 128)
            small_load(wr_bf, wrR,
                       wrT.ap().rearrange("(c p) n -> p c n", p=128), 8)

            with nc.psum_tensor("pf0", [128, 1024], F32) as pf0, \
                 nc.psum_tensor("pf1", [128, 1024], F32) as pf1, \
                 nc.psum_tensor("pss", [1, 1024], F32) as pss, \
                 nc.psum_tensor("psq", [1, 1024], F32) as psq:
                pf = [pf0, pf1]
                pfR = [Res(), Res()]
                pssR = Res()
                psqR = Res()
                apply_pend = []

                def emit_apply(ath, mc, ta2, tb2c):
                    ri = mc % 2
                    trl = s.op("SP", lambda e, mc=mc, ath=ath, ri=ri:
                               e.dma_start(
                                   res2d[ri][:],
                                   r2scr_r[:, mc, ath * 1024: ath * 1024 + 1024]),
                               waits=res2dR[ri].wr() + r2scrR[ath][mc].rd(),
                               dma="sc")
                    res2dR[ri].did_write(trl)
                    r2scrR[ath][mc].did_read(trl)
                    t1 = s.op("POOL", lambda e, d=tmpo[ri][:], a=res2d[ri][:],
                              b=a2[:]: e.tensor_tensor(d, a, b, ALU.mult),
                              waits=[trl, ta2] + tmpoR[ri].wr())
                    res2dR[ri].did_read(t1)
                    a2R.did_read(t1)
                    tmpoR[ri].did_write(t1)
                    t2 = s.op("DVE", lambda e, d=outst[ri][:], a=tmpo[ri][:],
                              b=b2[:]: e.tensor_tensor(d, a, b, ALU.add),
                              waits=[t1, tb2c] + outstR[ri].wr())
                    tmpoR[ri].did_read(t2)
                    b2R.did_read(t2)
                    outstR[ri].did_write(t2)
                    two = s.op("ACT", lambda e, mc=mc, ath=ath, ri=ri:
                               e.dma_start(
                                   outT.ap()[mc * 128: mc * 128 + 128,
                                             ath * 1024: ath * 1024 + 1024],
                                   outst[ri][:]),
                               waits=[t2], dma="out")
                    outstR[ri].did_read(two)
                for th in range(2):
                    for g in range(8):
                        tl = s.op("SP", lambda e, th=th, g=g: e.dma_start(
                            x2h.ap().rearrange("p (c n) -> p c n", c=32)[
                                :, 4 * g:4 * g + 4, :],
                            x2scr_r[:, 4 * g:4 * g + 4,
                                    th * 1024: th * 1024 + 1024]),
                            waits=[tk for j in range(4)
                                   for tk in x2hR[4 * g + j].wr()] +
                            x2scrR[2 * th].rd() + x2scrR[2 * th + 1].rd(),
                            dma=("sc" if g % 2 else "ld"))
                        for j in range(4):
                            x2hR[4 * g + j].did_write(tl)
                    w0 = pfR[0].wr()
                    last = None
                    for dc in range(NDC):
                        for sub in range(2):
                            last = s.op("PE", mm(
                                pf[0].ap()[0:8, sub * 512: sub * 512 + 512],
                                wr_bf[:, dc * 8: dc * 8 + 8],
                                x2h[:, dc * 1024 + sub * 512:
                                    dc * 1024 + sub * 512 + 512],
                                dc == 0, dc == NDC - 1),
                                waits=(w0 if dc == 0 and sub == 0 else []) +
                                wrR.rd() + x2hR[dc].rd(),
                                tok=(dc == NDC - 1 and sub == 1))
                    pfR[0].did_write(last)
                    wrR.did_read(last)
                    for dc in range(NDC):
                        x2hR[dc].did_read(last)
                    tlg = s.op("DVE", lambda e, d=lg8[:],
                               srd=pf[0].ap()[0:8, 0:1024]:
                               e.tensor_copy(d, srd), waits=[last] + lg8R.wr())
                    pfR[0].did_read(tlg)
                    lg8R.did_write(tlg)
                    w0 = pfR[0].wr()
                    last = None
                    for dc in range(NDC):
                        for sub in range(2):
                            last = s.op("PE", mm(
                                pf[0][:, sub * 512: sub * 512 + 512],
                                ar_bf[:, dc * 128: dc * 128 + 128],
                                x2h[:, dc * 1024 + sub * 512:
                                    dc * 1024 + sub * 512 + 512],
                                dc == 0, dc == NDC - 1),
                                waits=(w0 if dc == 0 and sub == 0 else []) +
                                arR.rd() + x2hR[dc].rd(),
                                tok=(dc == NDC - 1 and sub == 1))
                    pfR[0].did_write(last)
                    arR.did_read(last)
                    for dc in range(NDC):
                        x2hR[dc].did_read(last)
                    t_loraA = last
                    for qc in range(8):
                        tt = s.op("PE",
                                  lambda e, o=pf[1].ap()[0:128, 0:8],
                                  i=lg8[:, qc * 128: qc * 128 + 128],
                                  idn=id32[0:8, 0:8]: e.transpose(o, i, idn),
                                  waits=lg8R.rd() + pfR[1].wr())
                        lg8R.did_read(tt)
                        pfR[1].did_write(tt)
                        tc = s.op("DVE", lambda e, d=lgt[:, qc * 8: qc * 8 + 8],
                                  srd=pf[1].ap()[0:128, 0:8]: e.tensor_copy(d, srd),
                                  waits=[tt] + (lgtR.wr() if qc == 0 else []))
                        pfR[1].did_read(tc)
                        lgtR.did_write(tc)
                    l3 = lgt.ap().rearrange("p (q x) -> p q x", x=8)
                    l23 = lg2.ap().rearrange("p (q x) -> p q x", x=8)
                    xp3 = xpb.ap().rearrange("p (q x) -> p q x", x=8)
                    pr3 = prb.ap().rearrange("p (q x) -> p q x", x=8)
                    mk3 = mskb.ap().rearrange("p (q x) -> p q x", x=8)
                    cm3 = cmb.ap().rearrange("p (q x) -> p q x", x=8)
                    stc = lambda i: stt[:, i * 8:(i + 1) * 8]
                    bc = lambda i: stt.ap().rearrange(
                        "p (c x) -> p c x", x=8)[:, i:i + 1, :].transpose(
                        [0, 2, 1]).to_broadcast([128, 8, 8])
                    t1 = s.op("DVE", lambda e: e.reduce_max(
                        stc(0), l3, AX.X, negate=True),
                        waits=lgtR.rd() + sttR.wr())
                    lgtR.did_read(t1)
                    sttR.did_write(t1)
                    t2 = s.op("DVE", lambda e: e.tensor_tensor(
                        l23, l3, bc(0), ALU.add),
                        waits=[t1] + lg2R.wr())
                    lgtR.did_read(t2)
                    sttR.did_read(t2)
                    lg2R.did_write(t2)
                    t3 = s.op("ACT", lambda e: e.activation(
                        xpb[:], lg2[:], AF.Exp), waits=[t2] + xpbR.wr())
                    lg2R.did_read(t3)
                    xpbR.did_write(t3)
                    t4 = s.op("DVE", lambda e: e.reduce_sum(
                        stc(1), xp3, AX.X), waits=[t3])
                    xpbR.did_read(t4)
                    sttR.did_write(t4)
                    t5 = s.op("DVE", lambda e: e.reciprocal(
                        stc(2), stc(1)), waits=[t4])
                    sttR.did_write(t5)
                    t6 = s.op("DVE", lambda e: e.tensor_tensor(
                        pr3, xp3, bc(2), ALU.mult), waits=[t5] + prbR.wr())
                    xpbR.did_read(t6)
                    sttR.did_read(t6)
                    prbR.did_write(t6)
                    t7 = s.op("DVE", lambda e: e.reduce_max(
                        stc(3), pr3, AX.X), waits=[t6])
                    prbR.did_read(t7)
                    sttR.did_write(t7)
                    t8 = s.op("DVE", lambda e: e.tensor_tensor(
                        mk3, pr3, bc(3), ALU.is_ge), waits=[t7] + mskbR.wr())
                    prbR.did_read(t8)
                    sttR.did_read(t8)
                    mskbR.did_write(t8)
                    t9 = s.op("DVE", lambda e: e.scalar_tensor_tensor(
                        l23, mskb[:], -1e30, prb[:], ALU.mult, ALU.add),
                        waits=[t8] + lg2R.wr())
                    mskbR.did_read(t9)
                    prbR.did_read(t9)
                    lg2R.did_write(t9)
                    t10 = s.op("DVE", lambda e: e.reduce_max(
                        stc(4), l23, AX.X), waits=[t9])
                    lg2R.did_read(t10)
                    sttR.did_write(t10)
                    t11 = s.op("DVE", lambda e: e.tensor_tensor(
                        mk3, pr3, bc(4), ALU.is_ge), waits=[t10] + mskbR.wr())
                    prbR.did_read(t11)
                    sttR.did_read(t11)
                    mskbR.did_write(t11)
                    t12 = s.op("DVE", lambda e: e.tensor_tensor(
                        stc(5), stc(3), stc(4), ALU.add), waits=[t10, t7])
                    sttR.did_write(t12)
                    t13 = s.op("DVE", lambda e: e.reciprocal(
                        stc(6), stc(5)), waits=[t12])
                    sttR.did_write(t13)
                    t14 = s.op("DVE", lambda e: e.tensor_scalar(
                        stc(5), stc(6), 2.0, None, ALU.mult), waits=[t13])
                    sttR.did_write(t14)
                    t15 = s.op("DVE", lambda e: e.tensor_tensor(
                        xp3, mk3, pr3, ALU.mult), waits=[t11] + xpbR.wr())
                    mskbR.did_read(t15)
                    prbR.did_read(t15)
                    xpbR.did_write(t15)
                    t16 = s.op("DVE", lambda e: e.tensor_tensor(
                        cm3, xp3, bc(5), ALU.mult),
                        waits=[t15, t14] + cmbR.wr())
                    xpbR.did_read(t16)
                    sttR.did_read(t16)
                    cmbR.did_write(t16)
                    for qc in range(8):
                        tt = s.op("PE", lambda e, o=pf[1].ap()[0:8, 0:128],
                                  i=cmb[:, qc * 8: qc * 8 + 8], idn=id32[:]:
                                  e.transpose(o, i, idn),
                                  waits=[t16] + pfR[1].wr())
                        cmbR.did_read(tt)
                        pfR[1].did_write(tt)
                        tc3 = s.op("DVE",
                                   lambda e, d=cmT[:, qc * 128: qc * 128 + 128],
                                   srd=pf[1].ap()[0:8, 0:128]:
                                   e.tensor_copy(d, srd),
                                   waits=[tt] + (cmTR.wr() if qc == 0 else []))
                        pfR[1].did_read(tc3)
                        cmTR.did_write(tc3)
                    tex = None
                    for sub in range(2):
                        tex = s.op("PE", mm(
                            pf[1].ap()[0:128, sub * 512: sub * 512 + 512],
                            e16[:], cmT[:, sub * 512: sub * 512 + 512],
                            True, True),
                            waits=cmTR.rd() +
                            (pfR[1].wr() if sub == 0 else []) + consts_fence,
                            tok=(sub == 1))
                    pfR[1].did_write(tex)
                    cmTR.did_read(tex)
                    tce = s.op("DVE", lambda e, d=ce[:],
                               srd=pf[1].ap()[0:128, 0:1024]:
                               e.tensor_copy(d, srd), waits=[tex] + ceR.wr())
                    pfR[1].did_read(tce)
                    ceR.did_write(tce)
                    ths = s.op("DVE", lambda e, d=hs[:], a=pf[0][:], b=ce[:]:
                               e.tensor_tensor(d, a, b, ALU.mult),
                               waits=[t_loraA] + ceR.rd() + hsR.wr())
                    pfR[0].did_read(ths)
                    ceR.did_read(ths)
                    hsR.did_write(ths)
                    wbT_r = wbT.ap().rearrange("(c p) n -> p c n", p=128)
                    for wq in range(16):
                        wb = wq % 2
                        for _ in range(2):
                            if apply_pend:
                                emit_apply(*apply_pend.pop(0))
                        for g4 in range(8):
                            st_i = g4 % 2
                            src = wbT_r[:, 4 * g4:4 * g4 + 4,
                                        wq * 256: wq * 256 + 256]
                            tl = s.op("SP",
                                      lambda e, d=wstg[st_i].ap().rearrange(
                                          "p (c n) -> p c n", n=256)[:, 0:4, :],
                                      src=src: e.dma_start(d, src),
                                      waits=wstgR[st_i].wr(), dma="ld")
                            wstgR[st_i].did_write(tl)
                            tc = s.op("POOL",
                                      lambda e,
                                      d=wq_bf[wb][:, g4 * 1024: g4 * 1024 + 1024],
                                      srd=wstg[st_i][:, 0:1024]:
                                      e.tensor_copy(d, srd),
                                      waits=[tl] +
                                      [tk for j in range(4)
                                       for tk in wqR[wb][4 * g4 + j].wr()])
                            wstgR[st_i].did_read(tc)
                            for j in range(4):
                                wqR[wb][4 * g4 + j].did_write(tc)
                        if wq == 0:
                            fpend = []
                            def emit_fstats(mc, te, ts, ri):
                                for sub in range(2):
                                    t_s = s.op("PE", mm(
                                        pss[0:1, sub * 512: sub * 512 + 512],
                                        onescol[:, 0:1],
                                        res2c[ri][:, sub * 512: sub * 512 + 512],
                                        mc == 0, mc == NDC - 1),
                                        waits=[te] +
                                        (pssR.wr() if mc == 0 and sub == 0 else []))
                                    t_q = s.op("PE", mm(
                                        psq[0:1, sub * 512: sub * 512 + 512],
                                        onescol[:, 1:2],
                                        sqc[ri][:, sub * 512: sub * 512 + 512],
                                        mc == 0, mc == NDC - 1),
                                        waits=[ts] +
                                        (psqR.wr() if mc == 0 and sub == 0 else []))
                                    res2cR[ri].did_read(t_s)
                                    sqcR[ri].did_read(t_q)
                                    if mc == NDC - 1 and sub == 1:
                                        pssR.did_write(t_s)
                                        psqR.did_write(t_q)
                        for m4 in range(2):
                            mc = wq * 2 + m4
                            pi = m4 % 2
                            w0 = pfR[pi].wr()
                            last = None
                            for sub in range(2):
                                for dc in range(NDC):
                                    s.op("PE", mm(
                                        pf[pi][:, sub * 512: sub * 512 + 512],
                                        wq_bf[wb][:, dc * 256 + m4 * 128:
                                              dc * 256 + m4 * 128 + 128],
                                        x2h[:, dc * 1024 + sub * 512:
                                            dc * 1024 + sub * 512 + 512],
                                        dc == 0, False),
                                        waits=(w0 if dc == 0 and sub == 0
                                               else []) +
                                        wqR[wb][dc].rd() + x2hR[dc].rd(),
                                        tok=False)
                                last = s.op("PE", mm(
                                    pf[pi][:, sub * 512: sub * 512 + 512],
                                    br_bf[:, mc * 128: mc * 128 + 128],
                                    hs[:, sub * 512: sub * 512 + 512],
                                    False, True),
                                    waits=brR.rd() + hsR.rd(),
                                    tok=(sub == 1))
                            pfR[pi].did_write(last)
                            for dc in range(NDC):
                                wqR[wb][dc].did_read(last)
                                x2hR[dc].did_read(last)
                            brR.did_read(last)
                            hsR.did_read(last)
                            ri = mc % 2
                            te = s.op("DVE",
                                      lambda e, d=res2c[ri][:], a=pf[pi][:],
                                      b=bB_sb[:, mc:mc + 1],
                                      c2=x2h[:, mc * 1024: mc * 1024 + 1024]:
                                      e.scalar_tensor_tensor(d, a, b, c2,
                                                             ALU.add, ALU.add),
                                      waits=[last] + x2hR[mc].rd() +
                                      res2cR[ri].wr())
                            pfR[pi].did_read(te)
                            x2hR[mc].did_read(te)
                            res2cR[ri].did_write(te)
                            twr = s.op("ACT", lambda e, mc=mc, th=th, ri=ri:
                                       e.dma_start(
                                           r2scr_r[:, mc,
                                                   th * 1024: th * 1024 + 1024],
                                           res2c[ri][:]),
                                       waits=[te] + r2scrR[th][mc].wr(),
                                       dma="sca")
                            res2cR[ri].did_read(twr)
                            r2scrR[th][mc].did_write(twr)
                            ts = s.op("ACT", lambda e, d=sqc[ri][:],
                                      a=res2c[ri][:]:
                                      e.activation(d, a, AF.Square),
                                      waits=[te] + sqcR[ri].wr())
                            res2cR[ri].did_read(ts)
                            sqcR[ri].did_write(ts)
                            fpend.append((mc, te, ts, ri))
                            if len(fpend) > 1:
                                emit_fstats(*fpend.pop(0))
                            if mc == NDC - 1:
                                while fpend:
                                    emit_fstats(*fpend.pop(0))
                    ro2, trs, tb2 = ln_rows(pss[0:1, :], psq[0:1, :],
                                            pssR, psqR, 1024, 0)
                    tba = None
                    for sub in range(2):
                        tba = s.op("PE", mm(
                            pf[0][:, sub * 512: sub * 512 + 512], ones1f[:],
                            rows[0:1, 1024 + sub * 512: 1024 + sub * 512 + 512],
                            True, True),
                            waits=[trs] + (pfR[0].wr() if sub == 0 else []),
                            tok=(sub == 1))
                    rowsRes[1].did_read(tba)
                    pfR[0].did_write(tba)
                    tbb = None
                    for sub in range(2):
                        tbb = s.op("PE", mm(
                            pf[1][:, sub * 512: sub * 512 + 512], ones1f[:],
                            rows[0:1, sub * 512: sub * 512 + 512],
                            True, True),
                            waits=[tb2] + (pfR[1].wr() if sub == 0 else []),
                            tok=(sub == 1))
                    rowsRes[0].did_read(tbb)
                    pfR[1].did_write(tbb)
                    tca = s.op("DVE", lambda e, d=a2[:], srd=pf[0][:]:
                               e.tensor_copy(d, srd), waits=[tba] + a2R.wr())
                    pfR[0].did_read(tca)
                    a2R.did_write(tca)
                    tcb = s.op("DVE", lambda e, d=b2[:], srd=pf[1][:]:
                               e.tensor_copy(d, srd), waits=[tbb] + b2R.wr())
                    pfR[1].did_read(tcb)
                    b2R.did_write(tcb)
                    ta2, tb2c = tca, tcb
                    for mc in range(NDC):
                        apply_pend.append((th, mc, ta2, tb2c))
                    if th == 1:
                        while apply_pend:
                            emit_apply(*apply_pend.pop(0))

        s.finish()
        s.run()
    return nc


_NC_CACHE = {}


def kernel(L0_attn_x, L0_attn_Wqk_w, L0_attn_Wqk_b, L0_attn_Wv_w, L0_attn_Wv_b,
           L0_attn_Wo_w, L0_attn_Wo_b, L0_ffn_W_base, L0_ffn_b_base,
           L0_ffn_W_router, L0_ffn_A_stack, L0_ffn_B_stack):
    x = np.asarray(L0_attn_x, np.float32)
    Wqk = np.asarray(L0_attn_Wqk_w, np.float32)
    bqk = np.asarray(L0_attn_Wqk_b, np.float32)
    Wv = np.asarray(L0_attn_Wv_w, np.float32)
    bv = np.asarray(L0_attn_Wv_b, np.float32)
    Wo = np.asarray(L0_attn_Wo_w, np.float32)
    bo = np.asarray(L0_attn_Wo_b, np.float32)
    WB = np.asarray(L0_ffn_W_base, np.float32)
    bB = np.asarray(L0_ffn_b_base, np.float32)
    Wr = np.asarray(L0_ffn_W_router, np.float32)
    A = np.asarray(L0_ffn_A_stack, np.float32)
    Bs = np.asarray(L0_ffn_B_stack, np.float32)

    C = np.ascontiguousarray
    Wqk_r = Wqk.reshape(H, E, 2, D)
    bqk_r = bqk.reshape(H, E, 2)
    consts = {
        "wqT": C(Wqk_r[:, :, 0, :].reshape(H * E, D).T),
        "wkT": C(Wqk_r[:, :, 1, :].reshape(H * E, D).T),
        "wvT": C(Wv.T),
        "woT": C(Wo.T),
        "wbT": C(WB.T),
        "wrT": C(Wr.T),
        "aallT": C(A.reshape(128, D).T),
        "ballT": C(Bs.transpose(0, 2, 1).reshape(128, D)),
        "c_bq": C(bqk_r[:, :, 0].reshape(4, 128).T),
        "c_bk": C(bqk_r[:, :, 1].reshape(4, 128).T),
        "c_bo": C(bo.reshape(32, 128).T),
        "c_bB": C(bB.reshape(32, 128).T),
        "c_bv": C(bv.reshape(1, 512)),
        "c_id32": np.eye(128, dtype=np.float32),
        "c_ones65": np.ones((65, 128), NPBF),
        "c_ones1f": np.ones((1, 128), np.float32),
        "c_o65f": np.ones((65, 64), np.float32),
        "c_onescol": np.ones((128, 2), NPBF),
        "c_e16": np.repeat(np.eye(8, dtype=np.float32), 16, axis=1).astype(NPBF),
        "c_eps": np.full((1, 1), 1e-5, np.float32),
    }

    in_maps = []
    for c in range(8):
        b, half = c // 2, c % 2
        xT = C(x[b].T)
        if half:
            xrot = C(np.concatenate([xT[:, 2048:], xT[:, :2048]], axis=1))
        else:
            xrot = xT
        m = dict(consts)
        m["xrot"] = xrot
        in_maps.append(m)

    if "prog" not in _NC_CACHE:
        _NC_CACHE["prog"] = build_program()
    nc = _NC_CACHE["prog"]
    res = run_bass_kernel_spmd(nc, in_maps, core_ids=list(range(8)))
    out = np.empty_like(x)
    for c in range(8):
        b, half = c // 2, c % 2
        outT_np = np.asarray(res.results[c]["outT"], np.float32)
        out[b, half * T:(half + 1) * T, :] = outT_np.T
    return out


# version 20
# speedup vs baseline: 1.0602x; 1.0195x over previous
import os
import numpy as np
import ml_dtypes
from contextlib import ExitStack

import concourse.bass as bass
import concourse.mybir as mybir
from concourse.bass_utils import run_bass_kernel_spmd

F32 = mybir.dt.float32
BF16 = mybir.dt.bfloat16
AF = mybir.ActivationFunctionType
ALU = mybir.AluOpType
AX = mybir.AxisListType
NPBF = ml_dtypes.bfloat16

D = 4096
T = 2048
NK = 4096
H, E = 8, 64
NDC = D // 128
EPS = 1e-5
SQRT_DH = float(np.sqrt(512.0))

ENGS = ("PE", "ACT", "DVE", "POOL", "SP")
DMA_STREAMS = {"ld": 3, "sc": 3, "sca": 2, "out": 2}
DMAS = tuple("%s%d" % (st, i) for st, n in DMA_STREAMS.items() for i in range(n))


class Res(object):
    __slots__ = ("w", "r")

    def __init__(self):
        self.w = None
        self.r = []

    def rd(self):
        return [self.w] if self.w is not None else []

    def wr(self):
        out = list(self.r)
        if self.w is not None:
            out.append(self.w)
        return out

    def did_read(self, tok):
        if tok is not None:
            self.r.append(tok)

    def did_write(self, tok):
        if tok is not None:
            self.w = tok
            self.r = []


class Sched(object):
    def __init__(self, nc):
        self.nc = nc
        self.sems = {}
        for name in ENGS + DMAS:
            self.sems[name] = nc.alloc_semaphore("sem_" + name)
        self.cnt = dict((n, 0) for n in self.sems)
        self.seen = dict((e, {}) for e in ENGS)
        self.blocks = [dict((e, []) for e in ENGS)]
        self.rr = dict((st, 0) for st in DMA_STREAMS)
        self.pending = dict((e, []) for e in ENGS)

    def new_block(self):
        self.blocks.append(dict((e, []) for e in ENGS))
        for e in ENGS:
            self.seen[e] = dict((k, v) for k, v in self.cnt.items() if k in ENGS)

    def phase_barrier(self):
        self.new_block()
        fence = self.dma_fence(*DMA_STREAMS.keys())
        for e in ENGS:
            self.pending[e] = list(fence)

    def dma_fence(self, *streams):
        out = []
        for st in streams:
            for i in range(DMA_STREAMS[st]):
                sn = "%s%d" % (st, i)
                if self.cnt[sn]:
                    out.append((sn, self.cnt[sn]))
        return out

    def op(self, eng, fn, waits=(), dma=None, tok=True, fwaits=()):
        if dma is not None:
            i = self.rr[dma]
            self.rr[dma] = (i + 1) % DMA_STREAMS[dma]
            semname = "%s%d" % (dma, i)
            waits = list(waits) + [(semname, self.cnt[semname])]
        if self.pending[eng]:
            waits = list(waits) + self.pending[eng]
            self.pending[eng] = []
        wl = [wv for wv in fwaits if wv is not None]
        seen = self.seen[eng]
        for wv in waits:
            if wv is None:
                continue
            sn, v = wv
            if v > seen.get(sn, 0) and v > 0:
                wl.append((sn, v))
                seen[sn] = v
        if dma is None:
            semname = eng
            inc = 1
            make_tok = tok
        else:
            inc = 16
            make_tok = True
        post = None
        if make_tok:
            self.cnt[semname] += inc
            post = self.cnt[semname]
        sems = self.sems

        def emit(e, fn=fn, wl=wl, semname=semname, inc=inc, post=post):
            for sn, v in wl:
                e.wait_ge(sems[sn], v)
            ins = fn(e)
            if post is not None:
                ins.then_inc(sems[semname], inc)
        self.blocks[-1][eng].append(emit)
        return (semname, post) if make_tok else None

    def finish(self):
        sems = self.sems
        fence = self.dma_fence("out")

        def emit(e):
            for sn, v in fence:
                e.wait_ge(sems[sn], v)
        self.blocks[-1]["SP"].append(emit)

    def run(self):
        nc = self.nc
        for blk_ops in self.blocks:
            with nc.Block() as blk:
                for eng, meth in (("PE", blk.tensor), ("ACT", blk.scalar),
                                  ("DVE", blk.vector), ("POOL", blk.gpsimd),
                                  ("SP", blk.sync)):
                    ops = blk_ops[eng]
                    if not ops:
                        continue

                    def body(engine, ops=ops):
                        for f in ops:
                            f(engine)
                    meth(body)


def mm(o, l, r, st, sp):
    return lambda e: e.matmul(o, l, r, start=st, stop=sp, skip_group_check=True)


def build_program():
    KPH = int(os.environ.get("KPH", "9"))

    nc = bass.Bass("TRN2", target_bir_lowering=False)

    di = lambda n, sh, dt: nc.dram_tensor(n, sh, dt, kind="ExternalInput")
    x_in = di("xrot", [D, NK], F32)
    wqT = di("wqT", [D, 512], F32)
    wkT = di("wkT", [D, 512], F32)
    wvT = di("wvT", [D, 512], F32)
    woT = di("woT", [512, D], F32)
    wbT = di("wbT", [D, D], F32)
    wrT = di("wrT", [D, 8], F32)
    aallT = di("aallT", [D, 128], F32)
    ballT = di("ballT", [128, D], F32)
    c_bq = di("c_bq", [128, 4], F32)
    c_bk = di("c_bk", [128, 4], F32)
    c_bo = di("c_bo", [128, 32], F32)
    c_bB = di("c_bB", [128, 32], F32)
    c_bv = di("c_bv", [1, 512], F32)
    c_id32 = di("c_id32", [128, 128], F32)
    c_ones65 = di("c_ones65", [65, 128], BF16)
    c_ones1f = di("c_ones1f", [1, 128], F32)
    c_o65f = di("c_o65f", [65, 64], F32)
    c_onescol = di("c_onescol", [128, 2], BF16)
    c_e16 = di("c_e16", [8, 128], BF16)
    c_eps = di("c_eps", [1, 1], F32)

    outT = nc.dram_tensor("outT", [D, T], F32, kind="ExternalOutput")
    xscr = nc.dram_tensor("xscr", [D, NK], BF16, kind="Internal")
    x2scr = nc.dram_tensor("x2scr", [D, T], BF16, kind="Internal")
    r2scr = nc.dram_tensor("r2scr", [D, T], BF16, kind="Internal")
    xscr_r = xscr.ap().rearrange("(c p) t -> p c t", p=128)
    x2scr_r = x2scr.ap().rearrange("(c p) t -> p c t", p=128)
    r2scr_r = r2scr.ap().rearrange("(c p) t -> p c t", p=128)

    s = Sched(nc)
    xscrR = [Res() for _ in range(16)]
    x2scrR = [Res() for _ in range(4)]
    r2scrR = [[Res() for _ in range(NDC)] for _ in range(2)]

    def wload(dst_bf, dstR_list, src_dram, stg, stgR, n_chunks, cw,
              eng="POOL", stripe=False):
        src_r = src_dram.ap().rearrange("(c p) n -> p c n", p=128)
        per = stg[0].shape[1] // cw
        for g in range(n_chunks // per):
            st_i = g % 2
            dma = ("ld" if st_i == 0 else "sc") if stripe else "ld"
            src = src_r[:, per * g:per * g + per, :]
            tl = s.op("SP", lambda e, d=stg[st_i], src=src, per=per:
                      e.dma_start(d.ap().rearrange("p (c n) -> p c n", c=per),
                                  src),
                      waits=stgR[st_i].wr(), dma=dma)
            stgR[st_i].did_write(tl)
            tc = s.op(eng,
                      lambda e, d=dst_bf[:, per * g * cw:(per * g + per) * cw],
                      srd=stg[st_i][:, 0:per * cw]: e.tensor_copy(d, srd),
                      waits=[tl])
            stgR[st_i].did_read(tc)
            for j in range(per):
                dstR_list[per * g + j].did_write(tc)

    def wjobs(dst_bf, dstR_list, src_dram, stg, stgR, n_chunks, cw,
              eng="DVE", dma="ld"):
        src_r = src_dram.ap().rearrange("(c p) n -> p c n", p=128)
        per = stg[0].shape[1] // cw
        nb = len(stg)

        def mk(g):
            def f():
                st_i = g % nb
                src = src_r[:, per * g:per * g + per, :]
                tl = s.op("SP", lambda e, d=stg[st_i], src=src, per=per:
                          e.dma_start(d.ap().rearrange(
                              "p (c n) -> p c n", c=per), src),
                          waits=stgR[st_i].wr(), dma=dma)
                stgR[st_i].did_write(tl)
                tc = s.op(eng,
                          lambda e,
                          d=dst_bf[:, per * g * cw:(per * g + per) * cw],
                          srd=stg[st_i][:, 0:per * cw]: e.tensor_copy(d, srd),
                          waits=[tl])
                stgR[st_i].did_read(tc)
                for j in range(per):
                    dstR_list[per * g + j].did_write(tc)
            return f
        return [mk(g) for g in range(n_chunks // per)]

    with ExitStack() as st0:
        sb = lambda n, sh, dt: st0.enter_context(nc.sbuf_tensor(n, sh, dt))
        id32 = sb("id32", [128, 128], F32)
        ones65 = sb("ones65", [65, 128], BF16)
        ones1f = sb("ones1f", [1, 128], F32)
        o65f = sb("o65f", [65, 64], F32)
        onescol = sb("onescol", [128, 2], BF16)
        e16 = sb("e16", [8, 128], BF16)
        bq_sb = sb("bq_sb", [128, 4], F32)
        bk_sb = sb("bk_sb", [128, 4], F32)
        bo_sb = sb("bo_sb", [128, 32], F32)
        bB_sb = sb("bB_sb", [128, 32], F32)
        rows = sb("rows", [1, 4 * 1024], F32)
        epsc = sb("epsc", [1, 1], F32)
        rowsRes = [Res() for _ in range(4)]

        for ap_s, ap_d in ((id32, c_id32), (ones65, c_ones65),
                           (ones1f, c_ones1f), (o65f, c_o65f),
                           (onescol, c_onescol), (e16, c_e16),
                           (bq_sb, c_bq), (bk_sb, c_bk), (bo_sb, c_bo),
                           (bB_sb, c_bB),
                           (epsc, c_eps)):
            s.op("SP", lambda e, d=ap_s, src=ap_d: e.dma_start(d[:], src.ap()),
                 dma="ld")
        consts_fence = s.dma_fence("ld")

        def ln_rows(stat_s, stat_q, statSR, statQR, width, coff):
            ro = lambda i: rows[0:1, i * 1024 + coff: i * 1024 + coff + width]

            def rowop(eng, fn, rds, wrt, extra=()):
                waits = list(extra)
                for i in rds:
                    waits += rowsRes[i].rd()
                waits += rowsRes[wrt].wr()
                t = s.op(eng, fn, waits)
                for i in rds:
                    rowsRes[i].did_read(t)
                rowsRes[wrt].did_write(t)
                return t
            tmu = rowop("DVE", lambda e, d=ro(0), srd=stat_s:
                        e.tensor_scalar(d, srd, 1.0 / D, None, ALU.mult),
                        [], 0, statSR.rd())
            statSR.did_read(tmu)
            tms = rowop("DVE", lambda e, d=ro(1), srd=stat_q:
                        e.tensor_scalar(d, srd, 1.0 / D, None, ALU.mult),
                        [], 1, statQR.rd())
            statQR.did_read(tms)
            rowop("DVE", lambda e, d=ro(2), a=ro(0), b=ro(0):
                  e.tensor_tensor(d, a, b, ALU.mult), [0], 2)
            rowop("DVE", lambda e, d=ro(3), a=ro(1), b=ro(2):
                  e.tensor_tensor(d, a, b, ALU.subtract), [1, 2], 3)
            rowop("ACT", lambda e, d=ro(2), srd=ro(3):
                  e.activation(d, srd, AF.Ln, bias=epsc[:]), [3], 2)
            trs = rowop("ACT", lambda e, d=ro(1), srd=ro(2):
                        e.activation(d, srd, AF.Exp, scale=-0.5), [2], 1)
            rowop("DVE", lambda e, d=ro(3), a=ro(0), b=ro(1):
                  e.tensor_tensor(d, a, b, ALU.mult), [0, 1], 3)
            tb2 = rowop("DVE", lambda e, d=ro(0), srd=ro(3):
                        e.tensor_scalar(d, srd, -1.0, None, ALU.mult), [3], 0)
            return ro, trs, tb2

        with ExitStack() as stA:
            sbA = lambda n, sh, dt: stA.enter_context(nc.sbuf_tensor(n, sh, dt))
            OT = [sbA("OT%d" % i, [128, T], BF16) for i in range(4)]
            OTr = [[Res() for _ in range(4)] for _ in range(4)]
            OTtmp = sbA("OTtmp", [64, T], BF16)
            OTtmpr = [Res() for _ in range(4)]
            wo_bf = sbA("wo_bf", [128, 4 * D], BF16)
            woR = [[Res() for _ in range(4)] for _ in range(4)]

            with ExitStack() as stI:
                sbI = lambda n, sh, dt: stI.enter_context(nc.sbuf_tensor(n, sh, dt))
                kT = sbI("kT", [128, 4 * NK], BF16)
                kTr = [[Res() for _ in range(16)] for _ in range(4)]
                V = sbI("V", [128, 32 * 520], BF16)
                Vr = [Res() for _ in range(32)]
                qT = sbI("qT", [128, 4 * T], BF16)
                qTr = [[Res() for _ in range(4)] for _ in range(4)]

                vones = V.ap().rearrange("p (kc h x) -> p kc h x",
                                         kc=32, h=8)[:, :, :, 64:65]
                t = s.op("POOL", lambda e: e.memset(vones, 1.0),
                         waits=consts_fence)
                for r in Vr:
                    r.did_write(t)

                with ExitStack() as stP:
                    sbP = lambda n, sh, dt: stP.enter_context(nc.sbuf_tensor(n, sh, dt))
                    wk_bf = sbP("wk_bf", [128, 32 * 512], BF16)
                    wkR = [Res() for _ in range(32)]
                    wstg = [sbP("wstg%d" % i, [128, 512], F32) for i in range(3)]
                    wstgR = [Res(), Res(), Res()]
                    xstg = [sbP("xstg%d" % i, [128, 1024], F32) for i in range(3)]
                    xstgR = [Res() for _ in range(3)]
                    xbf = [sbP("xbf%d" % i, [128, 1024], BF16) for i in range(3)]
                    xbfR = [Res() for _ in range(3)]
                    wkj = wjobs(wk_bf, wkR, wkT, wstg, wstgR, 32, 512,
                                eng="DVE", dma="sc")
                    with nc.psum_tensor("pk0", [128, 512], F32) as pk0, \
                         nc.psum_tensor("pk1", [128, 512], F32) as pk1, \
                         nc.psum_tensor("pk2", [128, 512], F32) as pk2, \
                         nc.psum_tensor("pk3", [128, 512], F32) as pk3:
                        psK = [pk0, pk1, pk2, pk3]
                        psKR = [Res() for _ in range(4)]
                        for kg in range(16):
                            for g in range(8):
                                bi = (kg * 8 + g) % 3
                                bj = bi
                                src = x_in.ap().rearrange("(c p) n -> p c n", p=128)[
                                    :, 4 * g:4 * g + 4, 256 * kg:256 * kg + 256]
                                tl = s.op("SP", lambda e, d=xstg[bi], src=src:
                                          e.dma_start(d.ap().rearrange(
                                              "p (c n) -> p c n", c=4), src),
                                          waits=xstgR[bi].wr(), dma="ld")
                                xstgR[bi].did_write(tl)
                                for _ in range(4):
                                    if wkj:
                                        wkj.pop(0)()
                                if g % 2:
                                    tc = s.op("ACT", lambda e, d=xbf[bj][:],
                                              srd=xstg[bi][:]:
                                              e.activation(d, srd, AF.Copy),
                                              waits=[tl] + xbfR[bj].wr())
                                else:
                                    tc = s.op("POOL", lambda e, d=xbf[bj][:],
                                              srd=xstg[bi][:]:
                                              e.tensor_copy(d, srd),
                                              waits=[tl] + xbfR[bj].wr())
                                xstgR[bi].did_read(tc)
                                xbfR[bj].did_write(tc)
                                tw = s.op("ACT", lambda e, kg=kg, g=g, bj=bj:
                                          e.dma_start(
                                              xscr_r[:, 4 * g:4 * g + 4,
                                                     256 * kg:256 * kg + 256],
                                              xbf[bj].ap().rearrange(
                                                  "p (c n) -> p c n", c=4)),
                                          waits=[tc], dma="sca")
                                xbfR[bj].did_read(tw)
                                xscrR[kg].did_write(tw)
                                for j in range(4):
                                    dc = 4 * g + j
                                    for mc in range(4):
                                        tk = s.op("PE", mm(
                                            psK[mc][:, 0:256],
                                            wk_bf[:, dc * 512 + mc * 128:
                                                  dc * 512 + mc * 128 + 128],
                                            xbf[bj][:, j * 256: j * 256 + 256],
                                            dc == 0, dc == 31),
                                            waits=([tc] if mc == 0 else []) +
                                            wkR[dc].rd() +
                                            (psKR[mc].wr() if dc == 0 else []),
                                            tok=(j == 3))
                                        if j == 3:
                                            xbfR[bj].did_read(tk)
                                            if dc == 31:
                                                psKR[mc].did_write(tk)
                                        wkR[dc].did_read(tk)
                            for mc in range(4):
                                te = s.op("ACT",
                                          lambda e,
                                          d=kT[:, mc * NK + kg * 256:
                                               mc * NK + kg * 256 + 256],
                                          srd=psK[mc][:, 0:256],
                                          b=bk_sb[:, mc:mc + 1]:
                                          e.activation(d, srd, AF.Identity, bias=b),
                                          waits=psKR[mc].rd())
                                psKR[mc].did_read(te)
                                kTr[mc][kg].did_write(te)

                s.phase_barrier()
                if KPH < 1:
                    s.finish(); s.run(); return nc
                with ExitStack() as stP:
                    sbP = lambda n, sh, dt: stP.enter_context(nc.sbuf_tensor(n, sh, dt))
                    bv_sb = sbP("bv_sb", [1, 512], F32)
                    bvbf = sbP("bvbf", [1, 512], BF16)
                    t_bvl = s.op("SP", lambda e: e.dma_start(bv_sb[:],
                                                             c_bv.ap()),
                                 dma="ld")
                    t_bv = s.op("POOL", lambda e: e.tensor_copy(bvbf[:],
                                                                bv_sb[:]),
                                waits=[t_bvl])
                    wv_bf = sbP("wv_bf", [128, 32 * 512], BF16)
                    wvR = [Res() for _ in range(32)]
                    wstg = [sbP("wstgv%d" % i, [128, 512], F32) for i in range(3)]
                    wstgR = [Res(), Res(), Res()]
                    xvb = [sbP("xvb%d" % i, [128, 1024], BF16) for i in range(4)]
                    xvbR = [Res() for _ in range(4)]
                    wvj = wjobs(wv_bf, wvR, wvT, wstg, wstgR, 32, 512,
                                eng="DVE", dma="ld")
                    with nc.psum_tensor("pv0", [128, 512], F32) as pv0, \
                         nc.psum_tensor("pv1", [128, 512], F32) as pv1:
                        psV = [pv0, pv1]
                        psVR = [Res(), Res()]
                        for kg in range(16):
                            for g in range(8):
                                bi = (kg * 8 + g) % 3
                                bj = bi
                                tl = s.op("SP", lambda e, kg=kg, g=g, bi=bi:
                                          e.dma_start(
                                              xvb[bi].ap().rearrange(
                                                  "p (c n) -> p c n", c=4),
                                              xscr_r[:, 4 * g:4 * g + 4,
                                                     256 * kg:256 * kg + 256]),
                                          waits=xvbR[bi].wr() + xscrR[kg].rd(),
                                          dma="sc")
                                xvbR[bi].did_write(tl)
                                for _ in range(4):
                                    if wvj:
                                        wvj.pop(0)()
                                for j in range(4):
                                    dc = 4 * g + j
                                    for ti in range(2):
                                        tk = s.op("PE", mm(
                                            psV[ti][:],
                                            xvb[bi][:, j * 256 + ti * 128:
                                                    j * 256 + ti * 128 + 128],
                                            wv_bf[:, dc * 512: dc * 512 + 512],
                                            dc == 0, False),
                                            waits=([tl] if ti == 0 else []) +
                                            wvR[dc].rd() +
                                            (psVR[ti].wr() if dc == 0 else []),
                                            tok=(j == 3))
                                        if j == 3:
                                            xvbR[bi].did_read(tk)
                                        wvR[dc].did_read(tk)
                            for ti in range(2):
                                kc = kg * 2 + ti
                                tb = s.op("PE", mm(psV[ti][:], ones65[0:1, :],
                                                   bvbf[:], False, True),
                                          waits=[t_bv])
                                psVR[ti].did_write(tb)
                                dst = V.ap().rearrange("p (kc h x) -> p kc h x",
                                                       kc=32, h=8)[:, kc, :, 0:64]
                                te = s.op("ACT",
                                          lambda e, d=dst,
                                          srd=psV[ti].ap().rearrange(
                                              "p (h x) -> p h x", h=8):
                                          e.activation(d, srd, AF.Copy),
                                          waits=[tb])
                                psVR[ti].did_read(te)
                                Vr[kc].did_write(te)

                s.phase_barrier()
                if KPH < 2:
                    s.finish(); s.run(); return nc
                with ExitStack() as stP:
                    sbP = lambda n, sh, dt: stP.enter_context(nc.sbuf_tensor(n, sh, dt))
                    wq_bf = sbP("wq_bf", [128, 32 * 512], BF16)
                    wqR = [Res() for _ in range(32)]
                    wstg = [sbP("wstgq%d" % i, [128, 512], F32) for i in range(3)]
                    wstgR = [Res(), Res(), Res()]
                    xqb = [sbP("xqb%d" % i, [128, 2048], BF16) for i in range(3)]
                    xqbR = [Res() for _ in range(3)]
                    wqj = wjobs(wq_bf, wqR, wqT, wstg, wstgR, 32, 512,
                                eng="DVE", dma="ld")
                    with nc.psum_tensor("pq0", [128, 512], F32) as pq0, \
                         nc.psum_tensor("pq1", [128, 512], F32) as pq1, \
                         nc.psum_tensor("pq2", [128, 512], F32) as pq2, \
                         nc.psum_tensor("pq3", [128, 512], F32) as pq3:
                        psQ = [pq0, pq1, pq2, pq3]
                        psQR = [Res() for _ in range(4)]
                        for qg in range(4):
                            for g in range(8):
                                bi = g % 3
                                tl = s.op("SP", lambda e, qg=qg, g=g, bi=bi:
                                          e.dma_start(
                                              xqb[bi].ap().rearrange(
                                                  "p (c n) -> p c n", c=4),
                                              xscr_r[:, 4 * g:4 * g + 4,
                                                     512 * qg:512 * qg + 512]),
                                          waits=xqbR[bi].wr() +
                                          xscrR[2 * qg].rd() + xscrR[2 * qg + 1].rd(),
                                          dma="sc")
                                xqbR[bi].did_write(tl)
                                for _ in range(4):
                                    if wqj:
                                        wqj.pop(0)()
                                for j in range(4):
                                    dc = 4 * g + j
                                    for mc in range(4):
                                        tk = s.op("PE", mm(
                                            psQ[mc][:],
                                            wq_bf[:, dc * 512 + mc * 128:
                                                  dc * 512 + mc * 128 + 128],
                                            xqb[bi][:, j * 512: j * 512 + 512],
                                            dc == 0, dc == 31),
                                            waits=([tl] if mc == 0 else []) +
                                            wqR[dc].rd() +
                                            (psQR[mc].wr() if dc == 0 else []),
                                            tok=(j == 3))
                                        if j == 3:
                                            xqbR[bi].did_read(tk)
                                            if dc == 31:
                                                psQR[mc].did_write(tk)
                                        wqR[dc].did_read(tk)
                            for mc in range(4):
                                te = s.op("ACT",
                                          lambda e,
                                          d=qT[:, mc * T + qg * 512:
                                               mc * T + qg * 512 + 512],
                                          srd=psQ[mc][:], b=bq_sb[:, mc:mc + 1]:
                                          e.activation(d, srd, AF.Identity, bias=b),
                                          waits=psQR[mc].rd())
                                psQR[mc].did_read(te)
                                qTr[mc][qg].did_write(te)

                s.phase_barrier()
                if KPH < 3:
                    s.finish(); s.run(); return nc
                with ExitStack() as stP:
                    sbP = lambda n, sh, dt: stP.enter_context(nc.sbuf_tensor(n, sh, dt))
                    wstgo = [sbP("wstgo0", [128, 1024], F32)]
                    wstgoR = [Res()]
                    probs = sbP("probs", [128, 8192], BF16)
                    probsR = [[Res() for _ in range(8)] for _ in range(2)]
                    racc = sbP("racc", [128, 16], F32)
                    raccR = [Res(), Res()]
                    mcol = sbP("mcol", [128, 2], F32)
                    mcolR = [Res(), Res()]
                    mrow = [sbP("mrow%d" % i, [65, T], BF16) for i in range(2)]
                    mrowR = [[[Res() for _ in range(4)] for _ in range(2)]
                             for _ in range(2)]
                    mtmp = sbP("mtmp", [1, T], BF16)
                    mtmpR = [Res() for _ in range(4)]
                    rrec = sbP("rrec", [65, 2048], F32)
                    rrecR = [Res(), Res()]
                    rrecR2 = [Res(), Res()]
                    rbc = sbP("rbc", [64, 1024], F32)
                    rbcR = [Res(), Res()]

                    def mk_woj(hh, c):
                        def f():
                            src_ = woT.ap()[c * 128:(c + 1) * 128,
                                            hh * 1024:(hh + 1) * 1024]
                            tl = s.op("SP", lambda e, d=wstgo[0], src=src_:
                                      e.dma_start(d[:], src),
                                      waits=wstgoR[0].wr(), dma="ld")
                            wstgoR[0].did_write(tl)
                            tc = s.op("POOL",
                                      lambda e,
                                      d=wo_bf[:, c * D + hh * 1024:
                                              c * D + hh * 1024 + 1024],
                                      srd=wstgo[0][:]: e.tensor_copy(d, srd),
                                      waits=[tl])
                            wstgoR[0].did_read(tc)
                            woR[c][hh].did_write(tc)
                        return f
                    woj = [mk_woj(hh, c) for hh in range(4) for c in range(4)]

                    with nc.psum_tensor("sS", [128, 1024], F32) as Sg, \
                         nc.psum_tensor("rb0", [128, 1024], F32) as Rb0, \
                         nc.psum_tensor("rb1", [128, 1024], F32) as Rb1, \
                         nc.psum_tensor("a0", [65, 512], F32) as A0, \
                         nc.psum_tensor("a1", [65, 512], F32) as A1:
                        SgR = [Res(), Res()]
                        Rbig = [Rb0, Rb1]
                        RbR = [[Res(), Res()], [Res(), Res()]]
                        Ab = [A0, A1]
                        AbR = [Res(), Res()]

                        def mk_pair(hp, hx, qc, kg):
                            def run():
                                base = hx * 64
                                half = kg % 2
                                tmm = s.op("PE", mm(
                                    Sg[:, half * 512: half * 512 + 512],
                                    qT[base:base + 64,
                                       hp * T + qc * 128:
                                       hp * T + qc * 128 + 128],
                                    kT[base:base + 64,
                                       hp * NK + kg * 512:
                                       hp * NK + kg * 512 + 512],
                                    True, True),
                                    waits=SgR[half].wr() +
                                    qTr[hp][qc // 4].rd() +
                                    kTr[hp][2 * kg].rd() +
                                    kTr[hp][2 * kg + 1].rd())
                                SgR[half].did_write(tmm)
                                tr = s.op("DVE",
                                          lambda e,
                                          d=racc[:, hx * 8 + kg: hx * 8 + kg + 1],
                                          srd=Sg[:, half * 512: half * 512 + 512]:
                                          e.reduce_max(d, srd, AX.X, negate=True),
                                          waits=[tmm] +
                                          (raccR[hx].wr() if kg == 0 else []))
                                SgR[half].did_read(tr)
                                raccR[hx].did_write(tr)
                            return run

                        def mk_epia(hp, hx, qc):
                            def run():
                                tm = s.op("DVE",
                                          lambda e, d=mcol[:, hx:hx + 1],
                                          srd=racc[:, hx * 8: hx * 8 + 8]:
                                          e.tensor_reduce(d, srd, AX.X, ALU.min),
                                          waits=raccR[hx].rd() + mcolR[hx].wr())
                                raccR[hx].did_read(tm)
                                mcolR[hx].did_write(tm)
                            return run

                        def mk_epi(hp, hx, qc):
                            def run():
                                mr = mrow[hp % 2]
                                tm = mcolR[hx].rd()[0] if mcolR[hx].rd() else None
                                pt = Sg.ap()[0:1, 0:128]
                                tt = s.op("PE",
                                          lambda e, o=pt, i=mcol[:, hx:hx + 1],
                                          idn=id32[:]: e.transpose(o, i, idn),
                                          waits=([tm] if tm else []) +
                                          SgR[0].wr() + consts_fence)
                                SgR[0].did_write(tt)
                                mcolR[hx].did_read(tt)
                                if hx == 0:
                                    dmr = mr[0:1, qc * 128: qc * 128 + 128]
                                    wmr = mrowR[hp % 2][0][qc // 4].wr()
                                else:
                                    dmr = mtmp[0:1, qc * 128: qc * 128 + 128]
                                    wmr = mtmpR[qc // 4].wr()
                                tc2 = s.op("DVE",
                                           lambda e, d=dmr, srd=pt:
                                           e.tensor_copy(d, srd),
                                           waits=[tt] + wmr)
                                SgR[0].did_read(tc2)
                                if hx == 0:
                                    mrowR[hp % 2][0][qc // 4].did_write(tc2)
                                else:
                                    mtmpR[qc // 4].did_write(tc2)
                            return run

                        def mk_tdm(hp, qg):
                            def run():
                                mr = mrow[hp % 2]
                                tdm = s.op("SP",
                                           lambda e,
                                           d=mr[64:65, qg * 512: qg * 512 + 512],
                                           srd=mtmp[0:1, qg * 512: qg * 512 + 512]:
                                           e.dma_start(d, srd),
                                           waits=mtmpR[qg].rd() +
                                           mrowR[hp % 2][1][qg].wr(),
                                           dma="sc")
                                mtmpR[qg].did_read(tdm)
                                mrowR[hp % 2][1][qg].did_write(tdm)
                            return run

                        units = []
                        for hp in range(4):
                            for qg in range(4):
                                gid = hp * 4 + qg
                                for hx in range(2):
                                    for qc in range(4 * qg, 4 * qg + 4):
                                        for kg in range(8):
                                            units.append((gid, mk_pair(hp, hx, qc, kg)))
                                        units.append((gid, mk_epia(hp, hx, qc)))
                                        units.append((gid, mk_epi(hp, hx, qc)))
                                    if hx == 1:
                                        units.append((gid, mk_tdm(hp, qg)))
                        upos = [0]

                        def drain_until(gid):
                            while upos[0] < len(units) and units[upos[0]][0] <= gid:
                                units[upos[0]][1]()
                                upos[0] += 1

                        def drain_n(n):
                            for _ in range(n):
                                if upos[0] < len(units):
                                    units[upos[0]][1]()
                                    upos[0] += 1

                        evac_pend = []
                        for hp in range(4):
                            mr = mrow[hp % 2]
                            for qg in range(4):
                                if woj:
                                    woj.pop(0)()
                                drain_until(hp * 4 + qg)
                                pend = [[], []]

                                def emit_av(hx, kc, te, col, hp=hp):
                                    h = 2 * hp + hx
                                    ta = s.op("PE", mm(
                                        Ab[hx][:],
                                        V[:, kc * 520 + h * 65:
                                          kc * 520 + h * 65 + 65],
                                        probs[:, col: col + 512],
                                        kc == 0, kc == 31),
                                        waits=[te] + Vr[kc].rd() +
                                        (AbR[hx].wr() if kc == 0 else []),
                                        tok=(kc == 31))
                                    probsR[hx][kc % 8].did_read(ta)
                                    if kc == 31:
                                        AbR[hx].did_write(ta)
                                for kp in range(16):
                                    if kp in (1, 2, 3) and evac_pend:
                                        evac_pend.pop(0)()
                                    for k2 in range(2):
                                        kc = 2 * kp + k2
                                        for hx in range(2):
                                            s.op("PE", mm(
                                                Rbig[hx][:, k2 * 512:
                                                         k2 * 512 + 512],
                                                kT[hx * 64: hx * 64 + 64,
                                                   hp * NK + kc * 128:
                                                   hp * NK + kc * 128 + 128],
                                                qT[hx * 64: hx * 64 + 64,
                                                   hp * T + qg * 512:
                                                   hp * T + qg * 512 + 512],
                                                True, False),
                                                waits=RbR[hx][k2].wr(),
                                                tok=False)
                                        t2s = []
                                        for hx in range(2):
                                            t2 = s.op("PE", mm(
                                                Rbig[hx][:, k2 * 512:
                                                         k2 * 512 + 512],
                                                ones65[hx * 64: hx * 64 + 1, :],
                                                mr[hx * 64: hx * 64 + 1,
                                                   qg * 512: qg * 512 + 512],
                                                False, True),
                                                waits=mrowR[hp % 2][hx][qg].rd())
                                            mrowR[hp % 2][hx][qg].did_read(t2)
                                            RbR[hx][k2].did_write(t2)
                                            t2s.append(t2)
                                        for hx in range(2):
                                            col = hx * 4096 + (kc % 8) * 512
                                            te = s.op("ACT",
                                                      lambda e,
                                                      d=probs[:, col: col + 512],
                                                      srd=Rbig[hx][:, k2 * 512:
                                                                  k2 * 512 + 512]:
                                                      e.activation(d, srd, AF.Exp),
                                                      waits=[t2s[hx]] +
                                                      probsR[hx][kc % 8].wr())
                                            RbR[hx][k2].did_read(te)
                                            probsR[hx][kc % 8].did_write(te)
                                            pend[hx].append((hx, kc, te, col))
                                        drain_n(1)
                                    drain_n(1)
                                    for hx in range(2):
                                        if kp >= 3:
                                            while len(pend[hx]) > 4:
                                                emit_av(*pend[hx].pop(0))
                                        drain_n(1)
                                for hx in range(2):
                                    for item in pend[hx]:
                                        emit_av(*item)

                                def emit_evac_a(hx, st, hp=hp, qg=qg):
                                    c0 = hx * 1024
                                    t1 = s.op("DVE",
                                              lambda e,
                                              d=rrec[64:65, c0: c0 + 512],
                                              srd=Ab[hx].ap()[64:65, 0:512]:
                                              e.tensor_scalar(d, srd, SQRT_DH,
                                                              None, ALU.mult),
                                              waits=AbR[hx].rd(),
                                              fwaits=rrecR[hx].wr())
                                    AbR[hx].did_read(t1)
                                    rrecR[hx].did_write(t1)
                                    t2 = s.op("DVE",
                                              lambda e,
                                              d=rrec[64:65, c0 + 512: c0 + 1024],
                                              srd=rrec[64:65, c0: c0 + 512]:
                                              e.reciprocal(d, srd),
                                              waits=[t1], fwaits=rrecR2[hx].wr())
                                    rrecR[hx].did_read(t2)
                                    rrecR2[hx].did_write(t2)
                                    st["t2"] = t2

                                def emit_evac_b(hx, st, hp=hp, qg=qg):
                                    t2 = st["t2"]
                                    c0 = hx * 1024
                                    t3 = s.op("PE", mm(Sg.ap()[0:64, 512:1024],
                                                       o65f[64:65, :],
                                                       rrec[64:65,
                                                            c0 + 512: c0 + 1024],
                                                       True, True),
                                              waits=[t2] + SgR[1].wr())
                                    rrecR2[hx].did_read(t3)
                                    SgR[1].did_write(t3)
                                    t4 = s.op("DVE",
                                              lambda e,
                                              d=rbc[:, hx * 512: hx * 512 + 512],
                                              srd=Sg.ap()[0:64, 512:1024]:
                                              e.tensor_copy(d, srd),
                                              waits=[t3], fwaits=rbcR[hx].wr())
                                    SgR[1].did_read(t4)
                                    rbcR[hx].did_write(t4)
                                    st["t4"] = t4

                                def emit_evac_c(hx, st, hp=hp, qg=qg):
                                    t4 = st["t4"]
                                    if hx == 0:
                                        dst = OT[hp][0:64, qg * 512: qg * 512 + 512]
                                        wx = OTr[hp][qg].wr()
                                    else:
                                        dst = OTtmp[0:64, qg * 512: qg * 512 + 512]
                                        wx = OTtmpr[qg].wr()
                                    t5 = s.op("DVE",
                                              lambda e, d=dst,
                                              a=Ab[hx].ap()[0:64, 0:512],
                                              b=rbc[:, hx * 512: hx * 512 + 512]:
                                              e.tensor_tensor(d, a, b, ALU.mult),
                                              waits=[t4] + AbR[hx].rd() + wx)
                                    AbR[hx].did_read(t5)
                                    rbcR[hx].did_read(t5)
                                    if hx == 0:
                                        OTr[hp][qg].did_write(t5)
                                    else:
                                        OTtmpr[qg].did_write(t5)
                                st0_, st1_ = {}, {}
                                evac_pend.append(
                                    lambda a=emit_evac_a, s0=st0_, s1=st1_:
                                    (a(0, s0), a(1, s1)))
                                evac_pend.append(
                                    lambda b=emit_evac_b, s0=st0_, s1=st1_:
                                    (b(0, s0), b(1, s1)))
                                evac_pend.append(
                                    lambda c=emit_evac_c, s0=st0_, s1=st1_:
                                    (c(0, s0), c(1, s1)))
                            for f in evac_pend:
                                f()
                            del evac_pend[:]
                            tmv = s.op("SP", lambda e, hp=hp:
                                       e.dma_start(OT[hp][64:128, :], OTtmp[:]),
                                       waits=[tk for r in OTtmpr
                                              for tk in r.rd()] +
                                       [tk for qg in range(4)
                                        for tk in OTr[hp][qg].wr()],
                                       dma="sc")
                            for qg in range(4):
                                OTtmpr[qg].did_read(tmv)
                                OTr[hp][qg].did_write(tmv)

            s.phase_barrier()
            if KPH < 4:
                s.finish(); s.run(); return nc
            with ExitStack() as stP:
                sbP = lambda n, sh, dt: stP.enter_context(nc.sbuf_tensor(n, sh, dt))
                xq = sbP("xq2", [128, 32 * 512], BF16)
                xqR = [Res() for _ in range(32)]
                res = sbP("res", [128, 32 * 512], BF16)
                resR = [Res() for _ in range(32)]
                sq = sbP("sq", [128, 4 * 512], BF16)
                sqR = [Res() for _ in range(4)]
                x2g = sbP("x2g", [128, 32 * 512], BF16)
                x2gR = [Res() for _ in range(32)]
                abc = sbP("abc", [128, 512], F32)
                abcR = Res()
                bbc = sbP("bbc", [128, 512], F32)
                bbcR = Res()
                tmpf = [sbP("tmpf%d" % i, [128, 512], F32) for i in range(2)]
                tmpfR = [Res(), Res()]

                with nc.psum_tensor("po0", [128, 512], F32) as po0, \
                     nc.psum_tensor("po1", [128, 512], F32) as po1, \
                     nc.psum_tensor("po2", [128, 512], F32) as po2, \
                     nc.psum_tensor("pstS", [1, 512], F32) as pstS, \
                     nc.psum_tensor("pstQ", [1, 512], F32) as pstQ, \
                     nc.psum_tensor("pbc0", [128, 512], F32) as pbc0, \
                     nc.psum_tensor("pbc1", [128, 512], F32) as pbc1:
                    po = [po0, po1, po2]
                    poR = [Res(), Res(), Res()]
                    pstSR = Res()
                    pstQR = Res()
                    pbcR_ = [Res(), Res()]
                    opend = []
                    for qg in range(4):
                        for g in range(8):
                            tl = s.op("SP", lambda e, qg=qg, g=g: e.dma_start(
                                xq.ap().rearrange("p (c n) -> p c n", c=32)[
                                    :, 4 * g:4 * g + 4, :],
                                xscr_r[:, 4 * g:4 * g + 4,
                                       qg * 512: qg * 512 + 512]),
                                waits=[tk for j in range(4)
                                       for tk in xqR[4 * g + j].wr()] +
                                xscrR[2 * qg].rd() + xscrR[2 * qg + 1].rd(),
                                dma="sc")
                            for j in range(4):
                                xqR[4 * g + j].did_write(tl)
                        ppend = []
                        def emit_stats(mc, te, ts, sqi):
                            t_s = s.op("PE", mm(pstS[:], onescol[:, 0:1],
                                                res[:, mc * 512: mc * 512 + 512],
                                                mc == 0, mc == NDC - 1),
                                       waits=[te] +
                                       (pstSR.wr() if mc == 0 else []))
                            t_q = s.op("PE", mm(pstQ[:], onescol[:, 1:2],
                                                sq[:, sqi * 512: sqi * 512 + 512],
                                                mc == 0, mc == NDC - 1),
                                       waits=[ts] +
                                       (pstQR.wr() if mc == 0 else []))
                            resR[mc].did_read(t_s)
                            sqR[sqi].did_read(t_q)
                            if mc == NDC - 1:
                                pstSR.did_write(t_s)
                                pstQR.did_write(t_q)
                        for mc in range(NDC):
                            for _ in range(2):
                                if opend:
                                    opend.pop(0)()
                            pi = mc % 3
                            w0 = poR[pi].wr()
                            last = None
                            for hp_ in range(4):
                                last = s.op("PE", mm(
                                    po[pi][:],
                                    wo_bf[:, hp_ * D + mc * 128:
                                          hp_ * D + mc * 128 + 128],
                                    OT[hp_][:, qg * 512: qg * 512 + 512],
                                    hp_ == 0, hp_ == 3),
                                    waits=(w0 if hp_ == 0 else []) +
                                    woR[hp_][mc // 8].rd() + OTr[hp_][qg].rd(),
                                    tok=(hp_ == 3))
                            poR[pi].did_write(last)
                            for hp_ in range(4):
                                woR[hp_][mc // 8].did_read(last)
                                OTr[hp_][qg].did_read(last)
                            if len(ppend) > 2:
                                emit_stats(*ppend.pop(0))
                            te = s.op("DVE",
                                      lambda e,
                                      d=res[:, mc * 512: mc * 512 + 512],
                                      a=po[pi][:], b=bo_sb[:, mc:mc + 1],
                                      c2=xq[:, mc * 512: mc * 512 + 512]:
                                      e.scalar_tensor_tensor(d, a, b, c2,
                                                             ALU.add, ALU.add),
                                      waits=[last] + xqR[mc].rd() + resR[mc].wr())
                            poR[pi].did_read(te)
                            xqR[mc].did_read(te)
                            resR[mc].did_write(te)
                            sqi = mc % 4
                            ts = s.op("ACT",
                                      lambda e, d=sq[:, sqi * 512: sqi * 512 + 512],
                                      a=res[:, mc * 512: mc * 512 + 512]:
                                      e.activation(d, a, AF.Square),
                                      waits=[te] + sqR[sqi].wr())
                            resR[mc].did_read(ts)
                            sqR[sqi].did_write(ts)
                            ppend.append((mc, te, ts, sqi))
                        while ppend:
                            emit_stats(*ppend.pop(0))
                        ro, trs, tb2 = ln_rows(pstS[:], pstQ[:], pstSR, pstQR,
                                               512, 0)

                        def mk_bcast(ro=ro, trs=trs, tb2=tb2):
                            def f():
                                tba = s.op("PE", mm(pbc0[:], ones1f[:], ro(1),
                                                    True, True),
                                           waits=[trs] + pbcR_[0].wr() +
                                           consts_fence)
                                rowsRes[1].did_read(tba)
                                pbcR_[0].did_write(tba)
                                tbb = s.op("PE", mm(pbc1[:], ones1f[:], ro(0),
                                                    True, True),
                                           waits=[tb2] + pbcR_[1].wr())
                                rowsRes[0].did_read(tbb)
                                pbcR_[1].did_write(tbb)
                                tca = s.op("DVE", lambda e, d=abc[:], srd=pbc0[:]:
                                           e.tensor_copy(d, srd),
                                           waits=[tba] + abcR.wr())
                                pbcR_[0].did_read(tca)
                                abcR.did_write(tca)
                                tcb = s.op("DVE", lambda e, d=bbc[:], srd=pbc1[:]:
                                           e.tensor_copy(d, srd),
                                           waits=[tbb] + bbcR.wr())
                                pbcR_[1].did_read(tcb)
                                bbcR.did_write(tcb)
                            return f
                        opend.append(mk_bcast())
                        for mc in range(NDC):
                            def mk_apply(mc=mc):
                                def f():
                                    ti = mc % 2
                                    t1 = s.op("POOL",
                                              lambda e, d=tmpf[ti][:],
                                              a=res[:, mc * 512: mc * 512 + 512],
                                              b=abc[:]:
                                              e.tensor_tensor(d, a, b, ALU.mult),
                                              waits=resR[mc].rd() + abcR.rd() +
                                              tmpfR[ti].wr())
                                    resR[mc].did_read(t1)
                                    abcR.did_read(t1)
                                    tmpfR[ti].did_write(t1)
                                    t2 = s.op("DVE",
                                              lambda e,
                                              d=x2g[:, mc * 512: mc * 512 + 512],
                                              a=tmpf[ti][:], b=bbc[:]:
                                              e.tensor_tensor(d, a, b, ALU.add),
                                              waits=[t1] + bbcR.rd() +
                                              x2gR[mc].wr())
                                    tmpfR[ti].did_read(t2)
                                    bbcR.did_read(t2)
                                    x2gR[mc].did_write(t2)
                                return f
                            opend.append(mk_apply())
                            if mc % 4 == 3:
                                def mk_spill(qg=qg, g=mc // 4):
                                    def f():
                                        tw = s.op("ACT",
                                                  lambda e, qg=qg, g=g:
                                                  e.dma_start(
                                                      x2scr_r[
                                                          :, 4 * g:4 * g + 4,
                                                          qg * 512:
                                                          qg * 512 + 512],
                                                      x2g.ap().rearrange(
                                                          "p (c n) -> p c n",
                                                          c=32)[
                                                          :, 4 * g:4 * g + 4, :]),
                                                  waits=[tk for j in range(4)
                                                         for tk in
                                                         x2gR[4 * g + j].rd()],
                                                  dma="sca")
                                        for j in range(4):
                                            x2gR[4 * g + j].did_read(tw)
                                        x2scrR[qg].did_write(tw)
                                    return f
                                opend.append(mk_spill())
                    while opend:
                        opend.pop(0)()

        s.phase_barrier()
        if KPH < 5:
            s.finish(); s.run(); return nc
        with ExitStack() as stP:
            sbP = lambda n, sh, dt: stP.enter_context(nc.sbuf_tensor(n, sh, dt))
            x2h = sbP("x2h", [128, 32 * 1024], BF16)
            x2hR = [Res() for _ in range(32)]
            wq_bf = [sbP("wqf%d" % i, [128, 32 * 256], BF16) for i in range(2)]
            wqR = [[Res() for _ in range(32)] for _ in range(2)]
            wstg = [sbP("wstgf%d" % i, [128, 1024], F32) for i in range(2)]
            wstgR = [Res(), Res()]
            ar_bf = sbP("ar_bf", [128, 32 * 128], BF16)
            arR = Res()
            br_bf = sbP("br_bf", [128, 32 * 128], BF16)
            brR = Res()
            wr_bf = sbP("wr_bf", [128, 32 * 8], BF16)
            wrR = Res()
            hs = sbP("hs", [128, 1024], BF16)
            hsR = Res()
            ce = sbP("ce", [128, 1024], BF16)
            ceR = Res()
            lgt = sbP("lgt", [128, 64], F32)
            lgtR = Res()
            lg2 = sbP("lg2", [128, 64], F32)
            lg2R = Res()
            xpb = sbP("xpb", [128, 64], F32)
            xpbR = Res()
            prb = sbP("prb", [128, 64], F32)
            prbR = Res()
            mskb = sbP("mskb", [128, 64], F32)
            mskbR = Res()
            stt = sbP("stt", [128, 64], F32)
            sttR = Res()
            cmb = sbP("cmb", [128, 64], F32)
            cmbR = Res()
            cmT = sbP("cmT", [8, 1024], BF16)
            cmTR = Res()
            lg8 = sbP("lg8", [8, 1024], F32)
            lg8R = Res()
            scc = sbP("scc", [128, 8], F32)
            sccR = Res()
            res2c = [sbP("res2c%d" % i, [128, 1024], BF16) for i in range(2)]
            res2cR = [Res(), Res()]
            res2d = [sbP("res2d%d" % i, [128, 1024], BF16) for i in range(2)]
            res2dR = [Res(), Res()]
            sqc = [sbP("sqc%d" % i, [128, 1024], BF16) for i in range(2)]
            sqcR = [Res(), Res()]
            a2 = sbP("a2", [128, 1024], F32)
            a2R = Res()
            b2 = sbP("b2", [128, 1024], F32)
            b2R = Res()
            tmpo = [sbP("tmpo%d" % i, [128, 1024], F32) for i in range(2)]
            tmpoR = [Res(), Res()]
            outst = [sbP("outst%d" % i, [128, 1024], F32) for i in range(2)]
            outstR = [Res(), Res()]

            def small_load(dst, dstR, src3, n):
                per = max(1, 1024 // n)
                toks = []
                for p in range((32 + per - 1) // per):
                    c0 = p * per
                    cw = min(per, 32 - c0)
                    w = cw * n
                    st_i = p % 2
                    tl = s.op("SP",
                              lambda e, d=wstg[st_i].ap().rearrange(
                                  "q (c n) -> q c n", n=n)[:, 0:cw, :],
                              src=src3[:, c0:c0 + cw, :]: e.dma_start(d, src),
                              waits=wstgR[st_i].wr(), dma="ld")
                    wstgR[st_i].did_write(tl)
                    tc = s.op("POOL",
                              lambda e, d=dst[:, c0 * n:c0 * n + w],
                              srd=wstg[st_i][:, 0:w]: e.tensor_copy(d, srd),
                              waits=[tl])
                    wstgR[st_i].did_read(tc)
                    toks.append(tc)
                dstR.did_write(toks[-1])

            small_load(ar_bf, arR,
                       aallT.ap().rearrange("(c p) n -> p c n", p=128), 128)
            small_load(br_bf, brR,
                       ballT.ap().rearrange("p (c n) -> p c n", n=128), 128)
            small_load(wr_bf, wrR,
                       wrT.ap().rearrange("(c p) n -> p c n", p=128), 8)

            with nc.psum_tensor("pf0", [128, 1024], F32) as pf0, \
                 nc.psum_tensor("pf1", [128, 1024], F32) as pf1, \
                 nc.psum_tensor("pss", [1, 1024], F32) as pss, \
                 nc.psum_tensor("psq", [1, 1024], F32) as psq:
                pf = [pf0, pf1]
                pfR = [Res(), Res()]
                pssR = Res()
                psqR = Res()
                apply_pend = []

                def emit_apply(ath, mc, ta2, tb2c):
                    ri = mc % 2
                    trl = s.op("SP", lambda e, mc=mc, ath=ath, ri=ri:
                               e.dma_start(
                                   res2d[ri][:],
                                   r2scr_r[:, mc, ath * 1024: ath * 1024 + 1024]),
                               waits=res2dR[ri].wr() + r2scrR[ath][mc].rd(),
                               dma="sc")
                    res2dR[ri].did_write(trl)
                    r2scrR[ath][mc].did_read(trl)
                    t1 = s.op("POOL", lambda e, d=tmpo[ri][:], a=res2d[ri][:],
                              b=a2[:]: e.tensor_tensor(d, a, b, ALU.mult),
                              waits=[trl, ta2] + tmpoR[ri].wr())
                    res2dR[ri].did_read(t1)
                    a2R.did_read(t1)
                    tmpoR[ri].did_write(t1)
                    t2 = s.op("DVE", lambda e, d=outst[ri][:], a=tmpo[ri][:],
                              b=b2[:]: e.tensor_tensor(d, a, b, ALU.add),
                              waits=[t1, tb2c] + outstR[ri].wr())
                    tmpoR[ri].did_read(t2)
                    b2R.did_read(t2)
                    outstR[ri].did_write(t2)
                    two = s.op("ACT", lambda e, mc=mc, ath=ath, ri=ri:
                               e.dma_start(
                                   outT.ap()[mc * 128: mc * 128 + 128,
                                             ath * 1024: ath * 1024 + 1024],
                                   outst[ri][:]),
                               waits=[t2], dma="out")
                    outstR[ri].did_read(two)
                for th in range(2):
                    for g in range(8):
                        tl = s.op("SP", lambda e, th=th, g=g: e.dma_start(
                            x2h.ap().rearrange("p (c n) -> p c n", c=32)[
                                :, 4 * g:4 * g + 4, :],
                            x2scr_r[:, 4 * g:4 * g + 4,
                                    th * 1024: th * 1024 + 1024]),
                            waits=[tk for j in range(4)
                                   for tk in x2hR[4 * g + j].wr()] +
                            x2scrR[2 * th].rd() + x2scrR[2 * th + 1].rd(),
                            dma=("sc" if g % 2 else "ld"))
                        for j in range(4):
                            x2hR[4 * g + j].did_write(tl)
                    w0 = pfR[0].wr()
                    last = None
                    for dc in range(NDC):
                        for sub in range(2):
                            last = s.op("PE", mm(
                                pf[0].ap()[0:8, sub * 512: sub * 512 + 512],
                                wr_bf[:, dc * 8: dc * 8 + 8],
                                x2h[:, dc * 1024 + sub * 512:
                                    dc * 1024 + sub * 512 + 512],
                                dc == 0, dc == NDC - 1),
                                waits=(w0 if dc == 0 and sub == 0 else []) +
                                wrR.rd() + x2hR[dc].rd(),
                                tok=(dc == NDC - 1 and sub == 1))
                    pfR[0].did_write(last)
                    wrR.did_read(last)
                    for dc in range(NDC):
                        x2hR[dc].did_read(last)
                    tlg = s.op("DVE", lambda e, d=lg8[:],
                               srd=pf[0].ap()[0:8, 0:1024]:
                               e.tensor_copy(d, srd), waits=[last] + lg8R.wr())
                    pfR[0].did_read(tlg)
                    lg8R.did_write(tlg)
                    w0 = pfR[0].wr()
                    last = None
                    for dc in range(NDC):
                        for sub in range(2):
                            last = s.op("PE", mm(
                                pf[0][:, sub * 512: sub * 512 + 512],
                                ar_bf[:, dc * 128: dc * 128 + 128],
                                x2h[:, dc * 1024 + sub * 512:
                                    dc * 1024 + sub * 512 + 512],
                                dc == 0, dc == NDC - 1),
                                waits=(w0 if dc == 0 and sub == 0 else []) +
                                arR.rd() + x2hR[dc].rd(),
                                tok=(dc == NDC - 1 and sub == 1))
                    pfR[0].did_write(last)
                    arR.did_read(last)
                    for dc in range(NDC):
                        x2hR[dc].did_read(last)
                    t_loraA = last
                    for qc in range(8):
                        tt = s.op("PE",
                                  lambda e, o=pf[1].ap()[0:128, 0:8],
                                  i=lg8[:, qc * 128: qc * 128 + 128],
                                  idn=id32[0:8, 0:8]: e.transpose(o, i, idn),
                                  waits=lg8R.rd() + pfR[1].wr())
                        lg8R.did_read(tt)
                        pfR[1].did_write(tt)
                        tc = s.op("DVE", lambda e, d=lgt[:, qc * 8: qc * 8 + 8],
                                  srd=pf[1].ap()[0:128, 0:8]: e.tensor_copy(d, srd),
                                  waits=[tt] + (lgtR.wr() if qc == 0 else []))
                        pfR[1].did_read(tc)
                        lgtR.did_write(tc)
                    l3 = lgt.ap().rearrange("p (q x) -> p q x", x=8)
                    l23 = lg2.ap().rearrange("p (q x) -> p q x", x=8)
                    xp3 = xpb.ap().rearrange("p (q x) -> p q x", x=8)
                    pr3 = prb.ap().rearrange("p (q x) -> p q x", x=8)
                    mk3 = mskb.ap().rearrange("p (q x) -> p q x", x=8)
                    cm3 = cmb.ap().rearrange("p (q x) -> p q x", x=8)
                    stc = lambda i: stt[:, i * 8:(i + 1) * 8]
                    bc = lambda i: stt.ap().rearrange(
                        "p (c x) -> p c x", x=8)[:, i:i + 1, :].transpose(
                        [0, 2, 1]).to_broadcast([128, 8, 8])
                    t1 = s.op("DVE", lambda e: e.reduce_max(
                        stc(0), l3, AX.X, negate=True),
                        waits=lgtR.rd() + sttR.wr())
                    lgtR.did_read(t1)
                    sttR.did_write(t1)
                    t2 = s.op("DVE", lambda e: e.tensor_tensor(
                        l23, l3, bc(0), ALU.add),
                        waits=[t1] + lg2R.wr())
                    lgtR.did_read(t2)
                    sttR.did_read(t2)
                    lg2R.did_write(t2)
                    t3 = s.op("ACT", lambda e: e.activation(
                        xpb[:], lg2[:], AF.Exp), waits=[t2] + xpbR.wr())
                    lg2R.did_read(t3)
                    xpbR.did_write(t3)
                    t4 = s.op("DVE", lambda e: e.reduce_sum(
                        stc(1), xp3, AX.X), waits=[t3])
                    xpbR.did_read(t4)
                    sttR.did_write(t4)
                    t5 = s.op("DVE", lambda e: e.reciprocal(
                        stc(2), stc(1)), waits=[t4])
                    sttR.did_write(t5)
                    t6 = s.op("DVE", lambda e: e.tensor_tensor(
                        pr3, xp3, bc(2), ALU.mult), waits=[t5] + prbR.wr())
                    xpbR.did_read(t6)
                    sttR.did_read(t6)
                    prbR.did_write(t6)
                    t7 = s.op("DVE", lambda e: e.reduce_max(
                        stc(3), pr3, AX.X), waits=[t6])
                    prbR.did_read(t7)
                    sttR.did_write(t7)
                    t8 = s.op("DVE", lambda e: e.tensor_tensor(
                        mk3, pr3, bc(3), ALU.is_ge), waits=[t7] + mskbR.wr())
                    prbR.did_read(t8)
                    sttR.did_read(t8)
                    mskbR.did_write(t8)
                    t9 = s.op("DVE", lambda e: e.scalar_tensor_tensor(
                        l23, mskb[:], -1e30, prb[:], ALU.mult, ALU.add),
                        waits=[t8] + lg2R.wr())
                    mskbR.did_read(t9)
                    prbR.did_read(t9)
                    lg2R.did_write(t9)
                    t10 = s.op("DVE", lambda e: e.reduce_max(
                        stc(4), l23, AX.X), waits=[t9])
                    lg2R.did_read(t10)
                    sttR.did_write(t10)
                    t11 = s.op("DVE", lambda e: e.tensor_tensor(
                        mk3, pr3, bc(4), ALU.is_ge), waits=[t10] + mskbR.wr())
                    prbR.did_read(t11)
                    sttR.did_read(t11)
                    mskbR.did_write(t11)
                    t12 = s.op("DVE", lambda e: e.tensor_tensor(
                        stc(5), stc(3), stc(4), ALU.add), waits=[t10, t7])
                    sttR.did_write(t12)
                    t13 = s.op("DVE", lambda e: e.reciprocal(
                        stc(6), stc(5)), waits=[t12])
                    sttR.did_write(t13)
                    t14 = s.op("DVE", lambda e: e.tensor_scalar(
                        stc(5), stc(6), 2.0, None, ALU.mult), waits=[t13])
                    sttR.did_write(t14)
                    t15 = s.op("DVE", lambda e: e.tensor_tensor(
                        xp3, mk3, pr3, ALU.mult), waits=[t11] + xpbR.wr())
                    mskbR.did_read(t15)
                    prbR.did_read(t15)
                    xpbR.did_write(t15)
                    t16 = s.op("DVE", lambda e: e.tensor_tensor(
                        cm3, xp3, bc(5), ALU.mult),
                        waits=[t15, t14] + cmbR.wr())
                    xpbR.did_read(t16)
                    sttR.did_read(t16)
                    cmbR.did_write(t16)
                    for qc in range(8):
                        tt = s.op("PE", lambda e, o=pf[1].ap()[0:8, 0:128],
                                  i=cmb[:, qc * 8: qc * 8 + 8], idn=id32[:]:
                                  e.transpose(o, i, idn),
                                  waits=[t16] + pfR[1].wr())
                        cmbR.did_read(tt)
                        pfR[1].did_write(tt)
                        tc3 = s.op("DVE",
                                   lambda e, d=cmT[:, qc * 128: qc * 128 + 128],
                                   srd=pf[1].ap()[0:8, 0:128]:
                                   e.tensor_copy(d, srd),
                                   waits=[tt] + (cmTR.wr() if qc == 0 else []))
                        pfR[1].did_read(tc3)
                        cmTR.did_write(tc3)
                    tex = None
                    for sub in range(2):
                        tex = s.op("PE", mm(
                            pf[1].ap()[0:128, sub * 512: sub * 512 + 512],
                            e16[:], cmT[:, sub * 512: sub * 512 + 512],
                            True, True),
                            waits=cmTR.rd() +
                            (pfR[1].wr() if sub == 0 else []) + consts_fence,
                            tok=(sub == 1))
                    pfR[1].did_write(tex)
                    cmTR.did_read(tex)
                    tce = s.op("DVE", lambda e, d=ce[:],
                               srd=pf[1].ap()[0:128, 0:1024]:
                               e.tensor_copy(d, srd), waits=[tex] + ceR.wr())
                    pfR[1].did_read(tce)
                    ceR.did_write(tce)
                    ths = s.op("DVE", lambda e, d=hs[:], a=pf[0][:], b=ce[:]:
                               e.tensor_tensor(d, a, b, ALU.mult),
                               waits=[t_loraA] + ceR.rd() + hsR.wr())
                    pfR[0].did_read(ths)
                    ceR.did_read(ths)
                    hsR.did_write(ths)
                    wbT_r = wbT.ap().rearrange("(c p) n -> p c n", p=128)
                    for wq in range(16):
                        wb = wq % 2
                        for _ in range(2):
                            if apply_pend:
                                emit_apply(*apply_pend.pop(0))
                        for g4 in range(8):
                            st_i = g4 % 2
                            src = wbT_r[:, 4 * g4:4 * g4 + 4,
                                        wq * 256: wq * 256 + 256]
                            tl = s.op("SP",
                                      lambda e, d=wstg[st_i].ap().rearrange(
                                          "p (c n) -> p c n", n=256)[:, 0:4, :],
                                      src=src: e.dma_start(d, src),
                                      waits=wstgR[st_i].wr(), dma="ld")
                            wstgR[st_i].did_write(tl)
                            tc = s.op("POOL",
                                      lambda e,
                                      d=wq_bf[wb][:, g4 * 1024: g4 * 1024 + 1024],
                                      srd=wstg[st_i][:, 0:1024]:
                                      e.tensor_copy(d, srd),
                                      waits=[tl] +
                                      [tk for j in range(4)
                                       for tk in wqR[wb][4 * g4 + j].wr()])
                            wstgR[st_i].did_read(tc)
                            for j in range(4):
                                wqR[wb][4 * g4 + j].did_write(tc)
                        if wq == 0:
                            fpend = []
                            def emit_fstats(mc, te, ts, ri):
                                for sub in range(2):
                                    t_s = s.op("PE", mm(
                                        pss[0:1, sub * 512: sub * 512 + 512],
                                        onescol[:, 0:1],
                                        res2c[ri][:, sub * 512: sub * 512 + 512],
                                        mc == 0, mc == NDC - 1),
                                        waits=[te] +
                                        (pssR.wr() if mc == 0 and sub == 0 else []))
                                    t_q = s.op("PE", mm(
                                        psq[0:1, sub * 512: sub * 512 + 512],
                                        onescol[:, 1:2],
                                        sqc[ri][:, sub * 512: sub * 512 + 512],
                                        mc == 0, mc == NDC - 1),
                                        waits=[ts] +
                                        (psqR.wr() if mc == 0 and sub == 0 else []))
                                    res2cR[ri].did_read(t_s)
                                    sqcR[ri].did_read(t_q)
                                    if mc == NDC - 1 and sub == 1:
                                        pssR.did_write(t_s)
                                        psqR.did_write(t_q)
                        for m4 in range(2):
                            mc = wq * 2 + m4
                            pi = m4 % 2
                            w0 = pfR[pi].wr()
                            last = None
                            for sub in range(2):
                                for dc in range(NDC):
                                    s.op("PE", mm(
                                        pf[pi][:, sub * 512: sub * 512 + 512],
                                        wq_bf[wb][:, dc * 256 + m4 * 128:
                                              dc * 256 + m4 * 128 + 128],
                                        x2h[:, dc * 1024 + sub * 512:
                                            dc * 1024 + sub * 512 + 512],
                                        dc == 0, False),
                                        waits=(w0 if dc == 0 and sub == 0
                                               else []) +
                                        wqR[wb][dc].rd() + x2hR[dc].rd(),
                                        tok=False)
                                last = s.op("PE", mm(
                                    pf[pi][:, sub * 512: sub * 512 + 512],
                                    br_bf[:, mc * 128: mc * 128 + 128],
                                    hs[:, sub * 512: sub * 512 + 512],
                                    False, True),
                                    waits=brR.rd() + hsR.rd(),
                                    tok=(sub == 1))
                            pfR[pi].did_write(last)
                            for dc in range(NDC):
                                wqR[wb][dc].did_read(last)
                                x2hR[dc].did_read(last)
                            brR.did_read(last)
                            hsR.did_read(last)
                            ri = mc % 2
                            te = s.op("DVE",
                                      lambda e, d=res2c[ri][:], a=pf[pi][:],
                                      b=bB_sb[:, mc:mc + 1],
                                      c2=x2h[:, mc * 1024: mc * 1024 + 1024]:
                                      e.scalar_tensor_tensor(d, a, b, c2,
                                                             ALU.add, ALU.add),
                                      waits=[last] + x2hR[mc].rd() +
                                      res2cR[ri].wr())
                            pfR[pi].did_read(te)
                            x2hR[mc].did_read(te)
                            res2cR[ri].did_write(te)
                            twr = s.op("ACT", lambda e, mc=mc, th=th, ri=ri:
                                       e.dma_start(
                                           r2scr_r[:, mc,
                                                   th * 1024: th * 1024 + 1024],
                                           res2c[ri][:]),
                                       waits=[te] + r2scrR[th][mc].wr(),
                                       dma="sca")
                            res2cR[ri].did_read(twr)
                            r2scrR[th][mc].did_write(twr)
                            ts = s.op("ACT", lambda e, d=sqc[ri][:],
                                      a=res2c[ri][:]:
                                      e.activation(d, a, AF.Square),
                                      waits=[te] + sqcR[ri].wr())
                            res2cR[ri].did_read(ts)
                            sqcR[ri].did_write(ts)
                            fpend.append((mc, te, ts, ri))
                            if len(fpend) > 1:
                                emit_fstats(*fpend.pop(0))
                            if mc == NDC - 1:
                                while fpend:
                                    emit_fstats(*fpend.pop(0))
                    ro2, trs, tb2 = ln_rows(pss[0:1, :], psq[0:1, :],
                                            pssR, psqR, 1024, 0)
                    tba = None
                    for sub in range(2):
                        tba = s.op("PE", mm(
                            pf[0][:, sub * 512: sub * 512 + 512], ones1f[:],
                            rows[0:1, 1024 + sub * 512: 1024 + sub * 512 + 512],
                            True, True),
                            waits=[trs] + (pfR[0].wr() if sub == 0 else []),
                            tok=(sub == 1))
                    rowsRes[1].did_read(tba)
                    pfR[0].did_write(tba)
                    tbb = None
                    for sub in range(2):
                        tbb = s.op("PE", mm(
                            pf[1][:, sub * 512: sub * 512 + 512], ones1f[:],
                            rows[0:1, sub * 512: sub * 512 + 512],
                            True, True),
                            waits=[tb2] + (pfR[1].wr() if sub == 0 else []),
                            tok=(sub == 1))
                    rowsRes[0].did_read(tbb)
                    pfR[1].did_write(tbb)
                    tca = s.op("DVE", lambda e, d=a2[:], srd=pf[0][:]:
                               e.tensor_copy(d, srd), waits=[tba] + a2R.wr())
                    pfR[0].did_read(tca)
                    a2R.did_write(tca)
                    tcb = s.op("DVE", lambda e, d=b2[:], srd=pf[1][:]:
                               e.tensor_copy(d, srd), waits=[tbb] + b2R.wr())
                    pfR[1].did_read(tcb)
                    b2R.did_write(tcb)
                    ta2, tb2c = tca, tcb
                    for mc in range(NDC):
                        apply_pend.append((th, mc, ta2, tb2c))
                    if th == 1:
                        while apply_pend:
                            emit_apply(*apply_pend.pop(0))

        s.finish()
        s.run()
    return nc


_NC_CACHE = {}


def kernel(L0_attn_x, L0_attn_Wqk_w, L0_attn_Wqk_b, L0_attn_Wv_w, L0_attn_Wv_b,
           L0_attn_Wo_w, L0_attn_Wo_b, L0_ffn_W_base, L0_ffn_b_base,
           L0_ffn_W_router, L0_ffn_A_stack, L0_ffn_B_stack):
    x = np.asarray(L0_attn_x, np.float32)
    Wqk = np.asarray(L0_attn_Wqk_w, np.float32)
    bqk = np.asarray(L0_attn_Wqk_b, np.float32)
    Wv = np.asarray(L0_attn_Wv_w, np.float32)
    bv = np.asarray(L0_attn_Wv_b, np.float32)
    Wo = np.asarray(L0_attn_Wo_w, np.float32)
    bo = np.asarray(L0_attn_Wo_b, np.float32)
    WB = np.asarray(L0_ffn_W_base, np.float32)
    bB = np.asarray(L0_ffn_b_base, np.float32)
    Wr = np.asarray(L0_ffn_W_router, np.float32)
    A = np.asarray(L0_ffn_A_stack, np.float32)
    Bs = np.asarray(L0_ffn_B_stack, np.float32)

    C = np.ascontiguousarray
    Wqk_r = Wqk.reshape(H, E, 2, D)
    bqk_r = bqk.reshape(H, E, 2)
    consts = {
        "wqT": C(Wqk_r[:, :, 0, :].reshape(H * E, D).T),
        "wkT": C(Wqk_r[:, :, 1, :].reshape(H * E, D).T),
        "wvT": C(Wv.T),
        "woT": C(Wo.T),
        "wbT": C(WB.T),
        "wrT": C(Wr.T),
        "aallT": C(A.reshape(128, D).T),
        "ballT": C(Bs.transpose(0, 2, 1).reshape(128, D)),
        "c_bq": C(bqk_r[:, :, 0].reshape(4, 128).T),
        "c_bk": C(bqk_r[:, :, 1].reshape(4, 128).T),
        "c_bo": C(bo.reshape(32, 128).T),
        "c_bB": C(bB.reshape(32, 128).T),
        "c_bv": C(bv.reshape(1, 512)),
        "c_id32": np.eye(128, dtype=np.float32),
        "c_ones65": np.ones((65, 128), NPBF),
        "c_ones1f": np.ones((1, 128), np.float32),
        "c_o65f": np.ones((65, 64), np.float32),
        "c_onescol": np.ones((128, 2), NPBF),
        "c_e16": np.repeat(np.eye(8, dtype=np.float32), 16, axis=1).astype(NPBF),
        "c_eps": np.full((1, 1), 1e-5, np.float32),
    }

    in_maps = []
    for c in range(8):
        b, half = c // 2, c % 2
        xT = C(x[b].T)
        if half:
            xrot = C(np.concatenate([xT[:, 2048:], xT[:, :2048]], axis=1))
        else:
            xrot = xT
        m = dict(consts)
        m["xrot"] = xrot
        in_maps.append(m)

    if "prog" not in _NC_CACHE:
        _NC_CACHE["prog"] = build_program()
    nc = _NC_CACHE["prog"]
    res = run_bass_kernel_spmd(nc, in_maps, core_ids=list(range(8)))
    out = np.empty_like(x)
    for c in range(8):
        b, half = c // 2, c % 2
        outT_np = np.asarray(res.results[c]["outT"], np.float32)
        out[b, half * T:(half + 1) * T, :] = outT_np.T
    return out


# version 21
# speedup vs baseline: 1.0854x; 1.0238x over previous
import os
import numpy as np
import ml_dtypes
from contextlib import ExitStack

import concourse.bass as bass
import concourse.mybir as mybir
from concourse.bass_utils import run_bass_kernel_spmd

F32 = mybir.dt.float32
BF16 = mybir.dt.bfloat16
AF = mybir.ActivationFunctionType
ALU = mybir.AluOpType
AX = mybir.AxisListType
NPBF = ml_dtypes.bfloat16

D = 4096
T = 2048
NK = 4096
H, E = 8, 64
NDC = D // 128
EPS = 1e-5
SQRT_DH = float(np.sqrt(512.0))

ENGS = ("PE", "ACT", "DVE", "POOL", "SP")
DMA_STREAMS = {"ld": 3, "sc": 3, "sca": 2, "out": 2}
DMAS = tuple("%s%d" % (st, i) for st, n in DMA_STREAMS.items() for i in range(n))


class Res(object):
    __slots__ = ("w", "r")

    def __init__(self):
        self.w = None
        self.r = []

    def rd(self):
        return [self.w] if self.w is not None else []

    def wr(self):
        out = list(self.r)
        if self.w is not None:
            out.append(self.w)
        return out

    def did_read(self, tok):
        if tok is not None:
            self.r.append(tok)

    def did_write(self, tok):
        if tok is not None:
            self.w = tok
            self.r = []


class Sched(object):
    def __init__(self, nc):
        self.nc = nc
        self.sems = {}
        for name in ENGS + DMAS:
            self.sems[name] = nc.alloc_semaphore("sem_" + name)
        self.cnt = dict((n, 0) for n in self.sems)
        self.seen = dict((e, {}) for e in ENGS)
        self.blocks = [dict((e, []) for e in ENGS)]
        self.rr = dict((st, 0) for st in DMA_STREAMS)
        self.pending = dict((e, []) for e in ENGS)

    def new_block(self):
        self.blocks.append(dict((e, []) for e in ENGS))
        for e in ENGS:
            self.seen[e] = dict((k, v) for k, v in self.cnt.items() if k in ENGS)

    def phase_barrier(self):
        self.new_block()
        fence = self.dma_fence(*DMA_STREAMS.keys())
        for e in ENGS:
            self.pending[e] = list(fence)

    def dma_fence(self, *streams):
        out = []
        for st in streams:
            for i in range(DMA_STREAMS[st]):
                sn = "%s%d" % (st, i)
                if self.cnt[sn]:
                    out.append((sn, self.cnt[sn]))
        return out

    def op(self, eng, fn, waits=(), dma=None, tok=True, fwaits=()):
        if dma is not None:
            i = self.rr[dma]
            self.rr[dma] = (i + 1) % DMA_STREAMS[dma]
            semname = "%s%d" % (dma, i)
            waits = list(waits) + [(semname, self.cnt[semname])]
        if self.pending[eng]:
            waits = list(waits) + self.pending[eng]
            self.pending[eng] = []
        wl = [wv for wv in fwaits if wv is not None]
        seen = self.seen[eng]
        for wv in waits:
            if wv is None:
                continue
            sn, v = wv
            if v > seen.get(sn, 0) and v > 0:
                wl.append((sn, v))
                seen[sn] = v
        if dma is None:
            semname = eng
            inc = 1
            make_tok = tok
        else:
            inc = 16
            make_tok = True
        post = None
        if make_tok:
            self.cnt[semname] += inc
            post = self.cnt[semname]
        sems = self.sems

        def emit(e, fn=fn, wl=wl, semname=semname, inc=inc, post=post):
            for sn, v in wl:
                e.wait_ge(sems[sn], v)
            ins = fn(e)
            if post is not None:
                ins.then_inc(sems[semname], inc)
        self.blocks[-1][eng].append(emit)
        return (semname, post) if make_tok else None

    def finish(self):
        sems = self.sems
        fence = self.dma_fence("out")

        def emit(e):
            for sn, v in fence:
                e.wait_ge(sems[sn], v)
        self.blocks[-1]["SP"].append(emit)

    def run(self):
        nc = self.nc
        for blk_ops in self.blocks:
            with nc.Block() as blk:
                for eng, meth in (("PE", blk.tensor), ("ACT", blk.scalar),
                                  ("DVE", blk.vector), ("POOL", blk.gpsimd),
                                  ("SP", blk.sync)):
                    ops = blk_ops[eng]
                    if not ops:
                        continue

                    def body(engine, ops=ops):
                        for f in ops:
                            f(engine)
                    meth(body)


def mm(o, l, r, st, sp):
    return lambda e: e.matmul(o, l, r, start=st, stop=sp, skip_group_check=True)


def build_program():
    KPH = int(os.environ.get("KPH", "9"))

    nc = bass.Bass("TRN2", target_bir_lowering=False)

    di = lambda n, sh, dt: nc.dram_tensor(n, sh, dt, kind="ExternalInput")
    x_in = di("xrot", [D, NK], F32)
    wqT = di("wqT", [D, 512], F32)
    wkT = di("wkT", [D, 512], F32)
    wvT = di("wvT", [D, 512], F32)
    woT = di("woT", [512, D], F32)
    wbT = di("wbT", [D, D], F32)
    wrT = di("wrT", [D, 8], F32)
    aallT = di("aallT", [D, 128], F32)
    ballT = di("ballT", [128, D], F32)
    c_bq = di("c_bq", [128, 4], F32)
    c_bk = di("c_bk", [128, 4], F32)
    c_bo = di("c_bo", [128, 32], F32)
    c_bB = di("c_bB", [128, 32], F32)
    c_bv = di("c_bv", [1, 512], F32)
    c_id32 = di("c_id32", [128, 128], F32)
    c_ones65 = di("c_ones65", [65, 128], BF16)
    c_ones1f = di("c_ones1f", [1, 128], F32)
    c_o65f = di("c_o65f", [65, 64], F32)
    c_onescol = di("c_onescol", [128, 2], BF16)
    c_e16 = di("c_e16", [8, 128], BF16)
    c_eps = di("c_eps", [1, 1], F32)

    outT = nc.dram_tensor("outT", [D, T], F32, kind="ExternalOutput")
    xscr = nc.dram_tensor("xscr", [D, NK], BF16, kind="Internal")
    x2scr = nc.dram_tensor("x2scr", [D, T], BF16, kind="Internal")
    r2scr = nc.dram_tensor("r2scr", [D, T], BF16, kind="Internal")
    xscr_r = xscr.ap().rearrange("(c p) t -> p c t", p=128)
    x2scr_r = x2scr.ap().rearrange("(c p) t -> p c t", p=128)
    r2scr_r = r2scr.ap().rearrange("(c p) t -> p c t", p=128)

    s = Sched(nc)
    xscrR = [Res() for _ in range(16)]
    x2scrR = [Res() for _ in range(4)]
    r2scrR = [[Res() for _ in range(NDC)] for _ in range(2)]

    def wload(dst_bf, dstR_list, src_dram, stg, stgR, n_chunks, cw,
              eng="POOL", stripe=False):
        src_r = src_dram.ap().rearrange("(c p) n -> p c n", p=128)
        per = stg[0].shape[1] // cw
        for g in range(n_chunks // per):
            st_i = g % 2
            dma = ("ld" if st_i == 0 else "sc") if stripe else "ld"
            src = src_r[:, per * g:per * g + per, :]
            tl = s.op("SP", lambda e, d=stg[st_i], src=src, per=per:
                      e.dma_start(d.ap().rearrange("p (c n) -> p c n", c=per),
                                  src),
                      waits=stgR[st_i].wr(), dma=dma)
            stgR[st_i].did_write(tl)
            tc = s.op(eng,
                      lambda e, d=dst_bf[:, per * g * cw:(per * g + per) * cw],
                      srd=stg[st_i][:, 0:per * cw]: e.tensor_copy(d, srd),
                      waits=[tl])
            stgR[st_i].did_read(tc)
            for j in range(per):
                dstR_list[per * g + j].did_write(tc)

    def wjobs(dst_bf, dstR_list, src_dram, stg, stgR, n_chunks, cw,
              eng="DVE", dma="ld"):
        src_r = src_dram.ap().rearrange("(c p) n -> p c n", p=128)
        per = stg[0].shape[1] // cw
        nb = len(stg)

        def mk(g):
            def f():
                st_i = g % nb
                src = src_r[:, per * g:per * g + per, :]
                tl = s.op("SP", lambda e, d=stg[st_i], src=src, per=per:
                          e.dma_start(d.ap().rearrange(
                              "p (c n) -> p c n", c=per), src),
                          waits=stgR[st_i].wr(), dma=dma)
                stgR[st_i].did_write(tl)
                tc = s.op(eng,
                          lambda e,
                          d=dst_bf[:, per * g * cw:(per * g + per) * cw],
                          srd=stg[st_i][:, 0:per * cw]: e.tensor_copy(d, srd),
                          waits=[tl])
                stgR[st_i].did_read(tc)
                for j in range(per):
                    dstR_list[per * g + j].did_write(tc)
            return f
        return [mk(g) for g in range(n_chunks // per)]

    with ExitStack() as st0:
        sb = lambda n, sh, dt: st0.enter_context(nc.sbuf_tensor(n, sh, dt))
        id32 = sb("id32", [128, 128], F32)
        ones65 = sb("ones65", [65, 128], BF16)
        ones1f = sb("ones1f", [1, 128], F32)
        o65f = sb("o65f", [65, 64], F32)
        onescol = sb("onescol", [128, 2], BF16)
        e16 = sb("e16", [8, 128], BF16)
        bq_sb = sb("bq_sb", [128, 4], F32)
        bk_sb = sb("bk_sb", [128, 4], F32)
        bo_sb = sb("bo_sb", [128, 32], F32)
        bB_sb = sb("bB_sb", [128, 32], F32)
        rows = sb("rows", [1, 4 * 1024], F32)
        epsc = sb("epsc", [1, 1], F32)
        rowsRes = [Res() for _ in range(4)]

        for ap_s, ap_d in ((id32, c_id32), (ones65, c_ones65),
                           (ones1f, c_ones1f), (o65f, c_o65f),
                           (onescol, c_onescol), (e16, c_e16),
                           (bq_sb, c_bq), (bk_sb, c_bk), (bo_sb, c_bo),
                           (bB_sb, c_bB),
                           (epsc, c_eps)):
            s.op("SP", lambda e, d=ap_s, src=ap_d: e.dma_start(d[:], src.ap()),
                 dma="ld")
        consts_fence = s.dma_fence("ld")

        def ln_rows(stat_s, stat_q, statSR, statQR, width, coff):
            ro = lambda i: rows[0:1, i * 1024 + coff: i * 1024 + coff + width]

            def rowop(eng, fn, rds, wrt, extra=()):
                waits = list(extra)
                for i in rds:
                    waits += rowsRes[i].rd()
                waits += rowsRes[wrt].wr()
                t = s.op(eng, fn, waits)
                for i in rds:
                    rowsRes[i].did_read(t)
                rowsRes[wrt].did_write(t)
                return t
            tmu = rowop("DVE", lambda e, d=ro(0), srd=stat_s:
                        e.tensor_scalar(d, srd, 1.0 / D, None, ALU.mult),
                        [], 0, statSR.rd())
            statSR.did_read(tmu)
            tms = rowop("DVE", lambda e, d=ro(1), srd=stat_q:
                        e.tensor_scalar(d, srd, 1.0 / D, None, ALU.mult),
                        [], 1, statQR.rd())
            statQR.did_read(tms)
            rowop("DVE", lambda e, d=ro(2), a=ro(0), b=ro(0):
                  e.tensor_tensor(d, a, b, ALU.mult), [0], 2)
            rowop("DVE", lambda e, d=ro(3), a=ro(1), b=ro(2):
                  e.tensor_tensor(d, a, b, ALU.subtract), [1, 2], 3)
            rowop("ACT", lambda e, d=ro(2), srd=ro(3):
                  e.activation(d, srd, AF.Ln, bias=epsc[:]), [3], 2)
            trs = rowop("ACT", lambda e, d=ro(1), srd=ro(2):
                        e.activation(d, srd, AF.Exp, scale=-0.5), [2], 1)
            rowop("DVE", lambda e, d=ro(3), a=ro(0), b=ro(1):
                  e.tensor_tensor(d, a, b, ALU.mult), [0, 1], 3)
            tb2 = rowop("DVE", lambda e, d=ro(0), srd=ro(3):
                        e.tensor_scalar(d, srd, -1.0, None, ALU.mult), [3], 0)
            return ro, trs, tb2

        with ExitStack() as stA:
            sbA = lambda n, sh, dt: stA.enter_context(nc.sbuf_tensor(n, sh, dt))
            OT = [sbA("OT%d" % i, [128, T], BF16) for i in range(4)]
            OTr = [[Res() for _ in range(4)] for _ in range(4)]
            OTtmp = sbA("OTtmp", [64, T], BF16)
            OTtmpr = [Res() for _ in range(4)]
            wo_bf = sbA("wo_bf", [128, 4 * D], BF16)
            woR = [[Res() for _ in range(4)] for _ in range(4)]

            with ExitStack() as stI:
                sbI = lambda n, sh, dt: stI.enter_context(nc.sbuf_tensor(n, sh, dt))
                kT = sbI("kT", [128, 4 * NK], BF16)
                kTr = [[Res() for _ in range(16)] for _ in range(4)]
                V = sbI("V", [128, 32 * 520], BF16)
                Vr = [Res() for _ in range(32)]
                qT = sbI("qT", [128, 4 * T], BF16)
                qTr = [[Res() for _ in range(4)] for _ in range(4)]

                vones = V.ap().rearrange("p (kc h x) -> p kc h x",
                                         kc=32, h=8)[:, :, :, 64:65]
                t = s.op("POOL", lambda e: e.memset(vones, 1.0),
                         waits=consts_fence)
                for r in Vr:
                    r.did_write(t)

                with ExitStack() as stP:
                    sbP = lambda n, sh, dt: stP.enter_context(nc.sbuf_tensor(n, sh, dt))
                    wk_bf = sbP("wk_bf", [128, 32 * 512], BF16)
                    wkR = [Res() for _ in range(32)]
                    wstg = [sbP("wstg%d" % i, [128, 512], F32) for i in range(3)]
                    wstgR = [Res(), Res(), Res()]
                    xstg = [sbP("xstg%d" % i, [128, 1024], F32) for i in range(3)]
                    xstgR = [Res() for _ in range(3)]
                    xbf = [sbP("xbf%d" % i, [128, 1024], BF16) for i in range(3)]
                    xbfR = [Res() for _ in range(3)]
                    wkj = wjobs(wk_bf, wkR, wkT, wstg, wstgR, 32, 512,
                                eng="DVE", dma="sc")
                    with nc.psum_tensor("pk0", [128, 512], F32) as pk0, \
                         nc.psum_tensor("pk1", [128, 512], F32) as pk1, \
                         nc.psum_tensor("pk2", [128, 512], F32) as pk2, \
                         nc.psum_tensor("pk3", [128, 512], F32) as pk3:
                        psK = [pk0, pk1, pk2, pk3]
                        psKR = [Res() for _ in range(4)]
                        for kg in range(16):
                            for g in range(8):
                                bi = (kg * 8 + g) % 3
                                bj = bi
                                src = x_in.ap().rearrange("(c p) n -> p c n", p=128)[
                                    :, 4 * g:4 * g + 4, 256 * kg:256 * kg + 256]
                                tl = s.op("SP", lambda e, d=xstg[bi], src=src:
                                          e.dma_start(d.ap().rearrange(
                                              "p (c n) -> p c n", c=4), src),
                                          waits=xstgR[bi].wr(), dma="ld")
                                xstgR[bi].did_write(tl)
                                for _ in range(4):
                                    if wkj:
                                        wkj.pop(0)()
                                if g % 2:
                                    tc = s.op("ACT", lambda e, d=xbf[bj][:],
                                              srd=xstg[bi][:]:
                                              e.activation(d, srd, AF.Copy),
                                              waits=[tl] + xbfR[bj].wr())
                                else:
                                    tc = s.op("POOL", lambda e, d=xbf[bj][:],
                                              srd=xstg[bi][:]:
                                              e.tensor_copy(d, srd),
                                              waits=[tl] + xbfR[bj].wr())
                                xstgR[bi].did_read(tc)
                                xbfR[bj].did_write(tc)
                                tw = s.op("ACT", lambda e, kg=kg, g=g, bj=bj:
                                          e.dma_start(
                                              xscr_r[:, 4 * g:4 * g + 4,
                                                     256 * kg:256 * kg + 256],
                                              xbf[bj].ap().rearrange(
                                                  "p (c n) -> p c n", c=4)),
                                          waits=[tc], dma="sca")
                                xbfR[bj].did_read(tw)
                                xscrR[kg].did_write(tw)
                                for j in range(4):
                                    dc = 4 * g + j
                                    for mc in range(4):
                                        tk = s.op("PE", mm(
                                            psK[mc][:, 0:256],
                                            wk_bf[:, dc * 512 + mc * 128:
                                                  dc * 512 + mc * 128 + 128],
                                            xbf[bj][:, j * 256: j * 256 + 256],
                                            dc == 0, dc == 31),
                                            waits=([tc] if mc == 0 else []) +
                                            wkR[dc].rd() +
                                            (psKR[mc].wr() if dc == 0 else []),
                                            tok=(j == 3))
                                        if j == 3:
                                            xbfR[bj].did_read(tk)
                                            if dc == 31:
                                                psKR[mc].did_write(tk)
                                        wkR[dc].did_read(tk)
                            for mc in range(4):
                                te = s.op("ACT",
                                          lambda e,
                                          d=kT[:, mc * NK + kg * 256:
                                               mc * NK + kg * 256 + 256],
                                          srd=psK[mc][:, 0:256],
                                          b=bk_sb[:, mc:mc + 1]:
                                          e.activation(d, srd, AF.Identity, bias=b),
                                          waits=psKR[mc].rd())
                                psKR[mc].did_read(te)
                                kTr[mc][kg].did_write(te)

                s.phase_barrier()
                if KPH < 1:
                    s.finish(); s.run(); return nc
                with ExitStack() as stP:
                    sbP = lambda n, sh, dt: stP.enter_context(nc.sbuf_tensor(n, sh, dt))
                    bv_sb = sbP("bv_sb", [1, 512], F32)
                    bvbf = sbP("bvbf", [1, 512], BF16)
                    t_bvl = s.op("SP", lambda e: e.dma_start(bv_sb[:],
                                                             c_bv.ap()),
                                 dma="ld")
                    t_bv = s.op("POOL", lambda e: e.tensor_copy(bvbf[:],
                                                                bv_sb[:]),
                                waits=[t_bvl])
                    wv_bf = sbP("wv_bf", [128, 32 * 512], BF16)
                    wvR = [Res() for _ in range(32)]
                    wstg = [sbP("wstgv%d" % i, [128, 512], F32) for i in range(3)]
                    wstgR = [Res(), Res(), Res()]
                    xvb = [sbP("xvb%d" % i, [128, 1024], BF16) for i in range(4)]
                    xvbR = [Res() for _ in range(4)]
                    wvj = wjobs(wv_bf, wvR, wvT, wstg, wstgR, 32, 512,
                                eng="DVE", dma="ld")
                    with nc.psum_tensor("pv0", [128, 512], F32) as pv0, \
                         nc.psum_tensor("pv1", [128, 512], F32) as pv1:
                        psV = [pv0, pv1]
                        psVR = [Res(), Res()]
                        for kg in range(16):
                            for g in range(8):
                                bi = (kg * 8 + g) % 3
                                bj = bi
                                tl = s.op("SP", lambda e, kg=kg, g=g, bi=bi:
                                          e.dma_start(
                                              xvb[bi].ap().rearrange(
                                                  "p (c n) -> p c n", c=4),
                                              xscr_r[:, 4 * g:4 * g + 4,
                                                     256 * kg:256 * kg + 256]),
                                          waits=xvbR[bi].wr() + xscrR[kg].rd(),
                                          dma="sc")
                                xvbR[bi].did_write(tl)
                                for _ in range(4):
                                    if wvj:
                                        wvj.pop(0)()
                                for j in range(4):
                                    dc = 4 * g + j
                                    for ti in range(2):
                                        tk = s.op("PE", mm(
                                            psV[ti][:],
                                            xvb[bi][:, j * 256 + ti * 128:
                                                    j * 256 + ti * 128 + 128],
                                            wv_bf[:, dc * 512: dc * 512 + 512],
                                            dc == 0, False),
                                            waits=([tl] if ti == 0 else []) +
                                            wvR[dc].rd() +
                                            (psVR[ti].wr() if dc == 0 else []),
                                            tok=(j == 3))
                                        if j == 3:
                                            xvbR[bi].did_read(tk)
                                        wvR[dc].did_read(tk)
                            for ti in range(2):
                                kc = kg * 2 + ti
                                tb = s.op("PE", mm(psV[ti][:], ones65[0:1, :],
                                                   bvbf[:], False, True),
                                          waits=[t_bv])
                                psVR[ti].did_write(tb)
                                dst = V.ap().rearrange("p (kc h x) -> p kc h x",
                                                       kc=32, h=8)[:, kc, :, 0:64]
                                te = s.op("ACT",
                                          lambda e, d=dst,
                                          srd=psV[ti].ap().rearrange(
                                              "p (h x) -> p h x", h=8):
                                          e.activation(d, srd, AF.Copy),
                                          waits=[tb])
                                psVR[ti].did_read(te)
                                Vr[kc].did_write(te)

                s.phase_barrier()
                if KPH < 2:
                    s.finish(); s.run(); return nc
                with ExitStack() as stP:
                    sbP = lambda n, sh, dt: stP.enter_context(nc.sbuf_tensor(n, sh, dt))
                    wq_bf = sbP("wq_bf", [128, 32 * 512], BF16)
                    wqR = [Res() for _ in range(32)]
                    wstg = [sbP("wstgq%d" % i, [128, 512], F32) for i in range(3)]
                    wstgR = [Res(), Res(), Res()]
                    xqb = [sbP("xqb%d" % i, [128, 2048], BF16) for i in range(3)]
                    xqbR = [Res() for _ in range(3)]
                    wqj = wjobs(wq_bf, wqR, wqT, wstg, wstgR, 32, 512,
                                eng="DVE", dma="ld")
                    with nc.psum_tensor("pq0", [128, 512], F32) as pq0, \
                         nc.psum_tensor("pq1", [128, 512], F32) as pq1, \
                         nc.psum_tensor("pq2", [128, 512], F32) as pq2, \
                         nc.psum_tensor("pq3", [128, 512], F32) as pq3:
                        psQ = [pq0, pq1, pq2, pq3]
                        psQR = [Res() for _ in range(4)]
                        for qg in range(4):
                            for g in range(8):
                                bi = g % 3
                                tl = s.op("SP", lambda e, qg=qg, g=g, bi=bi:
                                          e.dma_start(
                                              xqb[bi].ap().rearrange(
                                                  "p (c n) -> p c n", c=4),
                                              xscr_r[:, 4 * g:4 * g + 4,
                                                     512 * qg:512 * qg + 512]),
                                          waits=xqbR[bi].wr() +
                                          xscrR[2 * qg].rd() + xscrR[2 * qg + 1].rd(),
                                          dma="sc")
                                xqbR[bi].did_write(tl)
                                for _ in range(4):
                                    if wqj:
                                        wqj.pop(0)()
                                for j in range(4):
                                    dc = 4 * g + j
                                    for mc in range(4):
                                        tk = s.op("PE", mm(
                                            psQ[mc][:],
                                            wq_bf[:, dc * 512 + mc * 128:
                                                  dc * 512 + mc * 128 + 128],
                                            xqb[bi][:, j * 512: j * 512 + 512],
                                            dc == 0, dc == 31),
                                            waits=([tl] if mc == 0 else []) +
                                            wqR[dc].rd() +
                                            (psQR[mc].wr() if dc == 0 else []),
                                            tok=(j == 3))
                                        if j == 3:
                                            xqbR[bi].did_read(tk)
                                            if dc == 31:
                                                psQR[mc].did_write(tk)
                                        wqR[dc].did_read(tk)
                            for mc in range(4):
                                te = s.op("ACT",
                                          lambda e,
                                          d=qT[:, mc * T + qg * 512:
                                               mc * T + qg * 512 + 512],
                                          srd=psQ[mc][:], b=bq_sb[:, mc:mc + 1]:
                                          e.activation(d, srd, AF.Identity, bias=b),
                                          waits=psQR[mc].rd())
                                psQR[mc].did_read(te)
                                qTr[mc][qg].did_write(te)

                s.phase_barrier()
                if KPH < 3:
                    s.finish(); s.run(); return nc
                with ExitStack() as stP:
                    sbP = lambda n, sh, dt: stP.enter_context(nc.sbuf_tensor(n, sh, dt))
                    wstgo = [sbP("wstgo0", [128, 1024], F32)]
                    wstgoR = [Res()]
                    probs = sbP("probs", [128, 8192], BF16)
                    probsR = [[Res() for _ in range(8)] for _ in range(2)]
                    racc = sbP("racc", [128, 16], F32)
                    raccR = [Res(), Res()]
                    mcol = sbP("mcol", [128, 2], F32)
                    mcolR = [Res(), Res()]
                    mrow = [sbP("mrow%d" % i, [65, T], BF16) for i in range(2)]
                    mrowR = [[[Res() for _ in range(4)] for _ in range(2)]
                             for _ in range(2)]
                    mtmp = sbP("mtmp", [1, T], BF16)
                    mtmpR = [Res() for _ in range(4)]
                    rrec = sbP("rrec", [65, 2048], F32)
                    rrecR = [Res(), Res()]
                    rrecR2 = [Res(), Res()]
                    rbc = sbP("rbc", [64, 1024], F32)
                    rbcR = [Res(), Res()]

                    def mk_woj(hh, c):
                        def f():
                            src_ = woT.ap()[c * 128:(c + 1) * 128,
                                            hh * 1024:(hh + 1) * 1024]
                            tl = s.op("SP", lambda e, d=wstgo[0], src=src_:
                                      e.dma_start(d[:], src),
                                      waits=wstgoR[0].wr(), dma="ld")
                            wstgoR[0].did_write(tl)
                            tc = s.op("POOL",
                                      lambda e,
                                      d=wo_bf[:, c * D + hh * 1024:
                                              c * D + hh * 1024 + 1024],
                                      srd=wstgo[0][:]: e.tensor_copy(d, srd),
                                      waits=[tl])
                            wstgoR[0].did_read(tc)
                            woR[c][hh].did_write(tc)
                        return f
                    woj = [mk_woj(hh, c) for hh in range(4) for c in range(4)]

                    with nc.psum_tensor("sS", [128, 1024], F32) as Sg, \
                         nc.psum_tensor("rb0", [128, 1024], F32) as Rb0, \
                         nc.psum_tensor("rb1", [128, 1024], F32) as Rb1, \
                         nc.psum_tensor("a0", [65, 512], F32) as A0, \
                         nc.psum_tensor("a1", [65, 512], F32) as A1:
                        SgR = [Res(), Res()]
                        Rbig = [Rb0, Rb1]
                        RbR = [[Res(), Res()], [Res(), Res()]]
                        Ab = [A0, A1]
                        AbR = [Res(), Res()]

                        def mk_pair(hp, hx, qc, kg):
                            def run():
                                base = hx * 64
                                half = kg % 2
                                tmm = s.op("PE", mm(
                                    Sg[:, half * 512: half * 512 + 512],
                                    qT[base:base + 64,
                                       hp * T + qc * 128:
                                       hp * T + qc * 128 + 128],
                                    kT[base:base + 64,
                                       hp * NK + kg * 512:
                                       hp * NK + kg * 512 + 512],
                                    True, True),
                                    waits=SgR[half].wr() +
                                    qTr[hp][qc // 4].rd() +
                                    kTr[hp][2 * kg].rd() +
                                    kTr[hp][2 * kg + 1].rd())
                                SgR[half].did_write(tmm)
                                tr = s.op("DVE",
                                          lambda e,
                                          d=racc[:, hx * 8 + kg: hx * 8 + kg + 1],
                                          srd=Sg[:, half * 512: half * 512 + 512]:
                                          e.reduce_max(d, srd, AX.X, negate=True),
                                          waits=[tmm] +
                                          (raccR[hx].wr() if kg == 0 else []))
                                SgR[half].did_read(tr)
                                raccR[hx].did_write(tr)
                            return run

                        def mk_epia(hp, hx, qc):
                            def run():
                                tm = s.op("DVE",
                                          lambda e, d=mcol[:, hx:hx + 1],
                                          srd=racc[:, hx * 8: hx * 8 + 8]:
                                          e.tensor_reduce(d, srd, AX.X, ALU.min),
                                          waits=raccR[hx].rd() + mcolR[hx].wr())
                                raccR[hx].did_read(tm)
                                mcolR[hx].did_write(tm)
                            return run

                        def mk_epi(hp, hx, qc):
                            def run():
                                mr = mrow[hp % 2]
                                tm = mcolR[hx].rd()[0] if mcolR[hx].rd() else None
                                pt = Sg.ap()[0:1, 0:128]
                                tt = s.op("PE",
                                          lambda e, o=pt, i=mcol[:, hx:hx + 1],
                                          idn=id32[:]: e.transpose(o, i, idn),
                                          waits=([tm] if tm else []) +
                                          SgR[0].wr() + consts_fence)
                                SgR[0].did_write(tt)
                                mcolR[hx].did_read(tt)
                                if hx == 0:
                                    dmr = mr[0:1, qc * 128: qc * 128 + 128]
                                    wmr = mrowR[hp % 2][0][qc // 4].wr()
                                else:
                                    dmr = mtmp[0:1, qc * 128: qc * 128 + 128]
                                    wmr = mtmpR[qc // 4].wr()
                                tc2 = s.op("DVE",
                                           lambda e, d=dmr, srd=pt:
                                           e.tensor_copy(d, srd),
                                           waits=[tt] + wmr)
                                SgR[0].did_read(tc2)
                                if hx == 0:
                                    mrowR[hp % 2][0][qc // 4].did_write(tc2)
                                else:
                                    mtmpR[qc // 4].did_write(tc2)
                            return run

                        def mk_tdm(hp, qg):
                            def run():
                                mr = mrow[hp % 2]
                                tdm = s.op("SP",
                                           lambda e,
                                           d=mr[64:65, qg * 512: qg * 512 + 512],
                                           srd=mtmp[0:1, qg * 512: qg * 512 + 512]:
                                           e.dma_start(d, srd),
                                           waits=mtmpR[qg].rd() +
                                           mrowR[hp % 2][1][qg].wr(),
                                           dma="sc")
                                mtmpR[qg].did_read(tdm)
                                mrowR[hp % 2][1][qg].did_write(tdm)
                            return run

                        units = []
                        for hp in range(4):
                            for qg in range(4):
                                gid = hp * 4 + qg
                                for hx in range(2):
                                    for qc in range(4 * qg, 4 * qg + 4):
                                        for kg in range(8):
                                            units.append((gid, mk_pair(hp, hx, qc, kg)))
                                        units.append((gid, mk_epia(hp, hx, qc)))
                                        units.append((gid, mk_epi(hp, hx, qc)))
                                    if hx == 1:
                                        units.append((gid, mk_tdm(hp, qg)))
                        upos = [0]

                        def drain_until(gid):
                            while upos[0] < len(units) and units[upos[0]][0] <= gid:
                                units[upos[0]][1]()
                                upos[0] += 1

                        def drain_n(n):
                            for _ in range(n):
                                if upos[0] < len(units):
                                    units[upos[0]][1]()
                                    upos[0] += 1

                        evac_pend = []
                        for hp in range(4):
                            mr = mrow[hp % 2]
                            for qg in range(4):
                                if woj:
                                    woj.pop(0)()
                                drain_until(hp * 4 + qg)
                                pend = [[], []]

                                def emit_av(hx, kc, te, col, hp=hp):
                                    h = 2 * hp + hx
                                    ta = s.op("PE", mm(
                                        Ab[hx][:],
                                        V[:, kc * 520 + h * 65:
                                          kc * 520 + h * 65 + 65],
                                        probs[:, col: col + 512],
                                        kc == 0, kc == 31),
                                        waits=[te] + Vr[kc].rd() +
                                        (AbR[hx].wr() if kc == 0 else []),
                                        tok=(kc == 31))
                                    probsR[hx][kc % 8].did_read(ta)
                                    if kc == 31:
                                        AbR[hx].did_write(ta)
                                for kp in range(16):
                                    if kp in (1, 2, 3) and evac_pend:
                                        evac_pend.pop(0)()
                                    for k2 in range(2):
                                        kc = 2 * kp + k2
                                        for hx in range(2):
                                            s.op("PE", mm(
                                                Rbig[hx][:, k2 * 512:
                                                         k2 * 512 + 512],
                                                kT[hx * 64: hx * 64 + 64,
                                                   hp * NK + kc * 128:
                                                   hp * NK + kc * 128 + 128],
                                                qT[hx * 64: hx * 64 + 64,
                                                   hp * T + qg * 512:
                                                   hp * T + qg * 512 + 512],
                                                True, False),
                                                waits=RbR[hx][k2].wr(),
                                                tok=False)
                                        t2s = []
                                        for hx in range(2):
                                            t2 = s.op("PE", mm(
                                                Rbig[hx][:, k2 * 512:
                                                         k2 * 512 + 512],
                                                ones65[hx * 64: hx * 64 + 1, :],
                                                mr[hx * 64: hx * 64 + 1,
                                                   qg * 512: qg * 512 + 512],
                                                False, True),
                                                waits=mrowR[hp % 2][hx][qg].rd())
                                            mrowR[hp % 2][hx][qg].did_read(t2)
                                            RbR[hx][k2].did_write(t2)
                                            t2s.append(t2)
                                        for hx in range(2):
                                            col = hx * 4096 + (kc % 8) * 512
                                            te = s.op("ACT",
                                                      lambda e,
                                                      d=probs[:, col: col + 512],
                                                      srd=Rbig[hx][:, k2 * 512:
                                                                  k2 * 512 + 512]:
                                                      e.activation(d, srd, AF.Exp),
                                                      waits=[t2s[hx]] +
                                                      probsR[hx][kc % 8].wr())
                                            RbR[hx][k2].did_read(te)
                                            probsR[hx][kc % 8].did_write(te)
                                            pend[hx].append((hx, kc, te, col))
                                        drain_n(1)
                                    drain_n(1)
                                    for hx in range(2):
                                        if kp >= 3:
                                            while len(pend[hx]) > 4:
                                                emit_av(*pend[hx].pop(0))
                                        drain_n(1)
                                for hx in range(2):
                                    for item in pend[hx]:
                                        emit_av(*item)

                                def emit_evac_a(hx, st, hp=hp, qg=qg):
                                    c0 = hx * 1024
                                    t1 = s.op("ACT",
                                              lambda e,
                                              d=rrec[64:65, c0: c0 + 512],
                                              srd=Ab[hx].ap()[64:65, 0:512]:
                                              e.activation(d, srd, AF.Ln,
                                                           scale=SQRT_DH),
                                              waits=AbR[hx].rd(),
                                              fwaits=rrecR[hx].wr())
                                    AbR[hx].did_read(t1)
                                    rrecR[hx].did_write(t1)
                                    t2 = s.op("ACT",
                                              lambda e,
                                              d=rrec[64:65, c0 + 512: c0 + 1024],
                                              srd=rrec[64:65, c0: c0 + 512]:
                                              e.activation(d, srd, AF.Exp,
                                                           scale=-1.0),
                                              waits=[t1], fwaits=rrecR2[hx].wr())
                                    rrecR[hx].did_read(t2)
                                    rrecR2[hx].did_write(t2)
                                    st["t2"] = t2

                                def emit_evac_b(hx, st, hp=hp, qg=qg):
                                    t2 = st["t2"]
                                    c0 = hx * 1024
                                    t3 = s.op("PE", mm(Sg.ap()[0:64, 512:1024],
                                                       o65f[64:65, :],
                                                       rrec[64:65,
                                                            c0 + 512: c0 + 1024],
                                                       True, True),
                                              waits=[t2] + SgR[1].wr())
                                    rrecR2[hx].did_read(t3)
                                    SgR[1].did_write(t3)
                                    t4 = s.op("DVE",
                                              lambda e,
                                              d=rbc[:, hx * 512: hx * 512 + 512],
                                              srd=Sg.ap()[0:64, 512:1024]:
                                              e.tensor_copy(d, srd),
                                              waits=[t3], fwaits=rbcR[hx].wr())
                                    SgR[1].did_read(t4)
                                    rbcR[hx].did_write(t4)
                                    st["t4"] = t4

                                def emit_evac_c(hx, st, hp=hp, qg=qg):
                                    t4 = st["t4"]
                                    if hx == 0:
                                        dst = OT[hp][0:64, qg * 512: qg * 512 + 512]
                                        wx = OTr[hp][qg].wr()
                                    else:
                                        dst = OTtmp[0:64, qg * 512: qg * 512 + 512]
                                        wx = OTtmpr[qg].wr()
                                    t5 = s.op("DVE",
                                              lambda e, d=dst,
                                              a=Ab[hx].ap()[0:64, 0:512],
                                              b=rbc[:, hx * 512: hx * 512 + 512]:
                                              e.tensor_tensor(d, a, b, ALU.mult),
                                              waits=[t4] + AbR[hx].rd() + wx)
                                    AbR[hx].did_read(t5)
                                    rbcR[hx].did_read(t5)
                                    if hx == 0:
                                        OTr[hp][qg].did_write(t5)
                                    else:
                                        OTtmpr[qg].did_write(t5)
                                st0_, st1_ = {}, {}
                                evac_pend.append(
                                    lambda a=emit_evac_a, s0=st0_, s1=st1_:
                                    (a(0, s0), a(1, s1)))
                                evac_pend.append(
                                    lambda b=emit_evac_b, s0=st0_, s1=st1_:
                                    (b(0, s0), b(1, s1)))
                                evac_pend.append(
                                    lambda c=emit_evac_c, s0=st0_, s1=st1_:
                                    (c(0, s0), c(1, s1)))
                            for f in evac_pend:
                                f()
                            del evac_pend[:]
                            tmv = s.op("SP", lambda e, hp=hp:
                                       e.dma_start(OT[hp][64:128, :], OTtmp[:]),
                                       waits=[tk for r in OTtmpr
                                              for tk in r.rd()] +
                                       [tk for qg in range(4)
                                        for tk in OTr[hp][qg].wr()],
                                       dma="sc")
                            for qg in range(4):
                                OTtmpr[qg].did_read(tmv)
                                OTr[hp][qg].did_write(tmv)

            s.phase_barrier()
            if KPH < 4:
                s.finish(); s.run(); return nc
            with ExitStack() as stP:
                sbP = lambda n, sh, dt: stP.enter_context(nc.sbuf_tensor(n, sh, dt))
                xq = sbP("xq2", [128, 32 * 512], BF16)
                xqR = [Res() for _ in range(32)]
                res = sbP("res", [128, 32 * 512], BF16)
                resR = [Res() for _ in range(32)]
                sq = sbP("sq", [128, 4 * 512], BF16)
                sqR = [Res() for _ in range(4)]
                x2g = sbP("x2g", [128, 32 * 512], BF16)
                x2gR = [Res() for _ in range(32)]
                abc = sbP("abc", [128, 512], F32)
                abcR = Res()
                bbc = sbP("bbc", [128, 512], F32)
                bbcR = Res()
                tmpf = [sbP("tmpf%d" % i, [128, 512], F32) for i in range(2)]
                tmpfR = [Res(), Res()]

                with nc.psum_tensor("po0", [128, 512], F32) as po0, \
                     nc.psum_tensor("po1", [128, 512], F32) as po1, \
                     nc.psum_tensor("po2", [128, 512], F32) as po2, \
                     nc.psum_tensor("pstS", [1, 512], F32) as pstS, \
                     nc.psum_tensor("pstQ", [1, 512], F32) as pstQ, \
                     nc.psum_tensor("pbc0", [128, 512], F32) as pbc0, \
                     nc.psum_tensor("pbc1", [128, 512], F32) as pbc1:
                    po = [po0, po1, po2]
                    poR = [Res(), Res(), Res()]
                    pstSR = Res()
                    pstQR = Res()
                    pbcR_ = [Res(), Res()]
                    opend = []
                    for qg in range(4):
                        for g in range(8):
                            tl = s.op("SP", lambda e, qg=qg, g=g: e.dma_start(
                                xq.ap().rearrange("p (c n) -> p c n", c=32)[
                                    :, 4 * g:4 * g + 4, :],
                                xscr_r[:, 4 * g:4 * g + 4,
                                       qg * 512: qg * 512 + 512]),
                                waits=[tk for j in range(4)
                                       for tk in xqR[4 * g + j].wr()] +
                                xscrR[2 * qg].rd() + xscrR[2 * qg + 1].rd(),
                                dma="sc")
                            for j in range(4):
                                xqR[4 * g + j].did_write(tl)
                        ppend = []
                        def emit_stats(mc, te, ts, sqi):
                            t_s = s.op("PE", mm(pstS[:], onescol[:, 0:1],
                                                res[:, mc * 512: mc * 512 + 512],
                                                mc == 0, mc == NDC - 1),
                                       waits=[te] +
                                       (pstSR.wr() if mc == 0 else []))
                            t_q = s.op("PE", mm(pstQ[:], onescol[:, 1:2],
                                                sq[:, sqi * 512: sqi * 512 + 512],
                                                mc == 0, mc == NDC - 1),
                                       waits=[ts] +
                                       (pstQR.wr() if mc == 0 else []))
                            resR[mc].did_read(t_s)
                            sqR[sqi].did_read(t_q)
                            if mc == NDC - 1:
                                pstSR.did_write(t_s)
                                pstQR.did_write(t_q)
                        for mc in range(NDC):
                            for _ in range(2):
                                if opend:
                                    opend.pop(0)()
                            pi = mc % 3
                            w0 = poR[pi].wr()
                            last = None
                            for hp_ in range(4):
                                last = s.op("PE", mm(
                                    po[pi][:],
                                    wo_bf[:, hp_ * D + mc * 128:
                                          hp_ * D + mc * 128 + 128],
                                    OT[hp_][:, qg * 512: qg * 512 + 512],
                                    hp_ == 0, hp_ == 3),
                                    waits=(w0 if hp_ == 0 else []) +
                                    woR[hp_][mc // 8].rd() + OTr[hp_][qg].rd(),
                                    tok=(hp_ == 3))
                            poR[pi].did_write(last)
                            for hp_ in range(4):
                                woR[hp_][mc // 8].did_read(last)
                                OTr[hp_][qg].did_read(last)
                            if len(ppend) > 2:
                                emit_stats(*ppend.pop(0))
                            te = s.op("DVE",
                                      lambda e,
                                      d=res[:, mc * 512: mc * 512 + 512],
                                      a=po[pi][:], b=bo_sb[:, mc:mc + 1],
                                      c2=xq[:, mc * 512: mc * 512 + 512]:
                                      e.scalar_tensor_tensor(d, a, b, c2,
                                                             ALU.add, ALU.add),
                                      waits=[last] + xqR[mc].rd() + resR[mc].wr())
                            poR[pi].did_read(te)
                            xqR[mc].did_read(te)
                            resR[mc].did_write(te)
                            sqi = mc % 4
                            ts = s.op("ACT",
                                      lambda e, d=sq[:, sqi * 512: sqi * 512 + 512],
                                      a=res[:, mc * 512: mc * 512 + 512]:
                                      e.activation(d, a, AF.Square),
                                      waits=[te] + sqR[sqi].wr())
                            resR[mc].did_read(ts)
                            sqR[sqi].did_write(ts)
                            ppend.append((mc, te, ts, sqi))
                        while ppend:
                            emit_stats(*ppend.pop(0))
                        ro, trs, tb2 = ln_rows(pstS[:], pstQ[:], pstSR, pstQR,
                                               512, 0)

                        def mk_bcast(ro=ro, trs=trs, tb2=tb2):
                            def f():
                                tba = s.op("PE", mm(pbc0[:], ones1f[:], ro(1),
                                                    True, True),
                                           waits=[trs] + pbcR_[0].wr() +
                                           consts_fence)
                                rowsRes[1].did_read(tba)
                                pbcR_[0].did_write(tba)
                                tbb = s.op("PE", mm(pbc1[:], ones1f[:], ro(0),
                                                    True, True),
                                           waits=[tb2] + pbcR_[1].wr())
                                rowsRes[0].did_read(tbb)
                                pbcR_[1].did_write(tbb)
                                tca = s.op("DVE", lambda e, d=abc[:], srd=pbc0[:]:
                                           e.tensor_copy(d, srd),
                                           waits=[tba] + abcR.wr())
                                pbcR_[0].did_read(tca)
                                abcR.did_write(tca)
                                tcb = s.op("DVE", lambda e, d=bbc[:], srd=pbc1[:]:
                                           e.tensor_copy(d, srd),
                                           waits=[tbb] + bbcR.wr())
                                pbcR_[1].did_read(tcb)
                                bbcR.did_write(tcb)
                            return f
                        opend.append(mk_bcast())
                        for mc in range(NDC):
                            def mk_apply(mc=mc):
                                def f():
                                    ti = mc % 2
                                    t1 = s.op("POOL",
                                              lambda e, d=tmpf[ti][:],
                                              a=res[:, mc * 512: mc * 512 + 512],
                                              b=abc[:]:
                                              e.tensor_tensor(d, a, b, ALU.mult),
                                              waits=resR[mc].rd() + abcR.rd() +
                                              tmpfR[ti].wr())
                                    resR[mc].did_read(t1)
                                    abcR.did_read(t1)
                                    tmpfR[ti].did_write(t1)
                                    t2 = s.op("DVE",
                                              lambda e,
                                              d=x2g[:, mc * 512: mc * 512 + 512],
                                              a=tmpf[ti][:], b=bbc[:]:
                                              e.tensor_tensor(d, a, b, ALU.add),
                                              waits=[t1] + bbcR.rd() +
                                              x2gR[mc].wr())
                                    tmpfR[ti].did_read(t2)
                                    bbcR.did_read(t2)
                                    x2gR[mc].did_write(t2)
                                return f
                            opend.append(mk_apply())
                            if mc % 4 == 3:
                                def mk_spill(qg=qg, g=mc // 4):
                                    def f():
                                        tw = s.op("ACT",
                                                  lambda e, qg=qg, g=g:
                                                  e.dma_start(
                                                      x2scr_r[
                                                          :, 4 * g:4 * g + 4,
                                                          qg * 512:
                                                          qg * 512 + 512],
                                                      x2g.ap().rearrange(
                                                          "p (c n) -> p c n",
                                                          c=32)[
                                                          :, 4 * g:4 * g + 4, :]),
                                                  waits=[tk for j in range(4)
                                                         for tk in
                                                         x2gR[4 * g + j].rd()],
                                                  dma="sca")
                                        for j in range(4):
                                            x2gR[4 * g + j].did_read(tw)
                                        x2scrR[qg].did_write(tw)
                                    return f
                                opend.append(mk_spill())
                    while opend:
                        opend.pop(0)()

        s.phase_barrier()
        if KPH < 5:
            s.finish(); s.run(); return nc
        with ExitStack() as stP:
            sbP = lambda n, sh, dt: stP.enter_context(nc.sbuf_tensor(n, sh, dt))
            x2h = sbP("x2h", [128, 32 * 1024], BF16)
            x2hR = [Res() for _ in range(32)]
            wq_bf = [sbP("wqf%d" % i, [128, 32 * 256], BF16) for i in range(2)]
            wqR = [[Res() for _ in range(32)] for _ in range(2)]
            wstg = [sbP("wstgf%d" % i, [128, 1024], F32) for i in range(2)]
            wstgR = [Res(), Res()]
            ar_bf = sbP("ar_bf", [128, 32 * 128], BF16)
            arR = Res()
            br_bf = sbP("br_bf", [128, 32 * 128], BF16)
            brR = Res()
            wr_bf = sbP("wr_bf", [128, 32 * 8], BF16)
            wrR = Res()
            hs = sbP("hs", [128, 1024], BF16)
            hsR = Res()
            ce = sbP("ce", [128, 1024], BF16)
            ceR = Res()
            lgt = sbP("lgt", [128, 64], F32)
            lgtR = Res()
            lg2 = sbP("lg2", [128, 64], F32)
            lg2R = Res()
            xpb = sbP("xpb", [128, 64], F32)
            xpbR = Res()
            prb = sbP("prb", [128, 64], F32)
            prbR = Res()
            mskb = sbP("mskb", [128, 64], F32)
            mskbR = Res()
            stt = sbP("stt", [128, 64], F32)
            sttR = Res()
            cmb = sbP("cmb", [128, 64], F32)
            cmbR = Res()
            cmT = sbP("cmT", [8, 1024], BF16)
            cmTR = Res()
            lg8 = sbP("lg8", [8, 1024], F32)
            lg8R = Res()
            scc = sbP("scc", [128, 8], F32)
            sccR = Res()
            res2c = [sbP("res2c%d" % i, [128, 1024], BF16) for i in range(2)]
            res2cR = [Res(), Res()]
            res2d = [sbP("res2d%d" % i, [128, 1024], BF16) for i in range(2)]
            res2dR = [Res(), Res()]
            sqc = [sbP("sqc%d" % i, [128, 1024], BF16) for i in range(2)]
            sqcR = [Res(), Res()]
            a2 = sbP("a2", [128, 1024], F32)
            a2R = Res()
            b2 = sbP("b2", [128, 1024], F32)
            b2R = Res()
            tmpo = [sbP("tmpo%d" % i, [128, 1024], F32) for i in range(2)]
            tmpoR = [Res(), Res()]
            outst = [sbP("outst%d" % i, [128, 1024], F32) for i in range(2)]
            outstR = [Res(), Res()]

            def small_load(dst, dstR, src3, n):
                per = max(1, 1024 // n)
                toks = []
                for p in range((32 + per - 1) // per):
                    c0 = p * per
                    cw = min(per, 32 - c0)
                    w = cw * n
                    st_i = p % 2
                    tl = s.op("SP",
                              lambda e, d=wstg[st_i].ap().rearrange(
                                  "q (c n) -> q c n", n=n)[:, 0:cw, :],
                              src=src3[:, c0:c0 + cw, :]: e.dma_start(d, src),
                              waits=wstgR[st_i].wr(), dma="ld")
                    wstgR[st_i].did_write(tl)
                    tc = s.op("POOL",
                              lambda e, d=dst[:, c0 * n:c0 * n + w],
                              srd=wstg[st_i][:, 0:w]: e.tensor_copy(d, srd),
                              waits=[tl])
                    wstgR[st_i].did_read(tc)
                    toks.append(tc)
                dstR.did_write(toks[-1])

            small_load(ar_bf, arR,
                       aallT.ap().rearrange("(c p) n -> p c n", p=128), 128)
            small_load(br_bf, brR,
                       ballT.ap().rearrange("p (c n) -> p c n", n=128), 128)
            small_load(wr_bf, wrR,
                       wrT.ap().rearrange("(c p) n -> p c n", p=128), 8)

            with nc.psum_tensor("pf0", [128, 1024], F32) as pf0, \
                 nc.psum_tensor("pf1", [128, 1024], F32) as pf1, \
                 nc.psum_tensor("pss", [1, 1024], F32) as pss, \
                 nc.psum_tensor("psq", [1, 1024], F32) as psq:
                pf = [pf0, pf1]
                pfR = [Res(), Res()]
                pssR = Res()
                psqR = Res()
                apply_pend = []

                def emit_apply(ath, mc, ta2, tb2c):
                    ri = mc % 2
                    trl = s.op("SP", lambda e, mc=mc, ath=ath, ri=ri:
                               e.dma_start(
                                   res2d[ri][:],
                                   r2scr_r[:, mc, ath * 1024: ath * 1024 + 1024]),
                               waits=res2dR[ri].wr() + r2scrR[ath][mc].rd(),
                               dma="sc")
                    res2dR[ri].did_write(trl)
                    r2scrR[ath][mc].did_read(trl)
                    t1 = s.op("POOL", lambda e, d=tmpo[ri][:], a=res2d[ri][:],
                              b=a2[:]: e.tensor_tensor(d, a, b, ALU.mult),
                              waits=[trl, ta2] + tmpoR[ri].wr())
                    res2dR[ri].did_read(t1)
                    a2R.did_read(t1)
                    tmpoR[ri].did_write(t1)
                    t2 = s.op("DVE", lambda e, d=outst[ri][:], a=tmpo[ri][:],
                              b=b2[:]: e.tensor_tensor(d, a, b, ALU.add),
                              waits=[t1, tb2c] + outstR[ri].wr())
                    tmpoR[ri].did_read(t2)
                    b2R.did_read(t2)
                    outstR[ri].did_write(t2)
                    two = s.op("ACT", lambda e, mc=mc, ath=ath, ri=ri:
                               e.dma_start(
                                   outT.ap()[mc * 128: mc * 128 + 128,
                                             ath * 1024: ath * 1024 + 1024],
                                   outst[ri][:]),
                               waits=[t2], dma="out")
                    outstR[ri].did_read(two)
                for th in range(2):
                    for g in range(8):
                        tl = s.op("SP", lambda e, th=th, g=g: e.dma_start(
                            x2h.ap().rearrange("p (c n) -> p c n", c=32)[
                                :, 4 * g:4 * g + 4, :],
                            x2scr_r[:, 4 * g:4 * g + 4,
                                    th * 1024: th * 1024 + 1024]),
                            waits=[tk for j in range(4)
                                   for tk in x2hR[4 * g + j].wr()] +
                            x2scrR[2 * th].rd() + x2scrR[2 * th + 1].rd(),
                            dma=("sc" if g % 2 else "ld"))
                        for j in range(4):
                            x2hR[4 * g + j].did_write(tl)
                    w0 = pfR[0].wr()
                    last = None
                    for dc in range(NDC):
                        for sub in range(2):
                            last = s.op("PE", mm(
                                pf[0].ap()[0:8, sub * 512: sub * 512 + 512],
                                wr_bf[:, dc * 8: dc * 8 + 8],
                                x2h[:, dc * 1024 + sub * 512:
                                    dc * 1024 + sub * 512 + 512],
                                dc == 0, dc == NDC - 1),
                                waits=(w0 if dc == 0 and sub == 0 else []) +
                                wrR.rd() + x2hR[dc].rd(),
                                tok=(dc == NDC - 1 and sub == 1))
                    pfR[0].did_write(last)
                    wrR.did_read(last)
                    for dc in range(NDC):
                        x2hR[dc].did_read(last)
                    tlg = s.op("DVE", lambda e, d=lg8[:],
                               srd=pf[0].ap()[0:8, 0:1024]:
                               e.tensor_copy(d, srd), waits=[last] + lg8R.wr())
                    pfR[0].did_read(tlg)
                    lg8R.did_write(tlg)
                    w0 = pfR[0].wr()
                    last = None
                    for dc in range(NDC):
                        for sub in range(2):
                            last = s.op("PE", mm(
                                pf[0][:, sub * 512: sub * 512 + 512],
                                ar_bf[:, dc * 128: dc * 128 + 128],
                                x2h[:, dc * 1024 + sub * 512:
                                    dc * 1024 + sub * 512 + 512],
                                dc == 0, dc == NDC - 1),
                                waits=(w0 if dc == 0 and sub == 0 else []) +
                                arR.rd() + x2hR[dc].rd(),
                                tok=(dc == NDC - 1 and sub == 1))
                    pfR[0].did_write(last)
                    arR.did_read(last)
                    for dc in range(NDC):
                        x2hR[dc].did_read(last)
                    t_loraA = last
                    for qc in range(8):
                        tt = s.op("PE",
                                  lambda e, o=pf[1].ap()[0:128, 0:8],
                                  i=lg8[:, qc * 128: qc * 128 + 128],
                                  idn=id32[0:8, 0:8]: e.transpose(o, i, idn),
                                  waits=lg8R.rd() + pfR[1].wr())
                        lg8R.did_read(tt)
                        pfR[1].did_write(tt)
                        tc = s.op("DVE", lambda e, d=lgt[:, qc * 8: qc * 8 + 8],
                                  srd=pf[1].ap()[0:128, 0:8]: e.tensor_copy(d, srd),
                                  waits=[tt] + (lgtR.wr() if qc == 0 else []))
                        pfR[1].did_read(tc)
                        lgtR.did_write(tc)
                    l3 = lgt.ap().rearrange("p (q x) -> p q x", x=8)
                    l23 = lg2.ap().rearrange("p (q x) -> p q x", x=8)
                    xp3 = xpb.ap().rearrange("p (q x) -> p q x", x=8)
                    pr3 = prb.ap().rearrange("p (q x) -> p q x", x=8)
                    mk3 = mskb.ap().rearrange("p (q x) -> p q x", x=8)
                    cm3 = cmb.ap().rearrange("p (q x) -> p q x", x=8)
                    stc = lambda i: stt[:, i * 8:(i + 1) * 8]
                    bc = lambda i: stt.ap().rearrange(
                        "p (c x) -> p c x", x=8)[:, i:i + 1, :].transpose(
                        [0, 2, 1]).to_broadcast([128, 8, 8])
                    t1 = s.op("DVE", lambda e: e.reduce_max(
                        stc(0), l3, AX.X, negate=True),
                        waits=lgtR.rd() + sttR.wr())
                    lgtR.did_read(t1)
                    sttR.did_write(t1)
                    t2 = s.op("DVE", lambda e: e.tensor_tensor(
                        l23, l3, bc(0), ALU.add),
                        waits=[t1] + lg2R.wr())
                    lgtR.did_read(t2)
                    sttR.did_read(t2)
                    lg2R.did_write(t2)
                    t3 = s.op("ACT", lambda e: e.activation(
                        xpb[:], lg2[:], AF.Exp), waits=[t2] + xpbR.wr())
                    lg2R.did_read(t3)
                    xpbR.did_write(t3)
                    t4 = s.op("DVE", lambda e: e.reduce_sum(
                        stc(1), xp3, AX.X), waits=[t3])
                    xpbR.did_read(t4)
                    sttR.did_write(t4)
                    t5 = s.op("DVE", lambda e: e.reciprocal(
                        stc(2), stc(1)), waits=[t4])
                    sttR.did_write(t5)
                    t6 = s.op("DVE", lambda e: e.tensor_tensor(
                        pr3, xp3, bc(2), ALU.mult), waits=[t5] + prbR.wr())
                    xpbR.did_read(t6)
                    sttR.did_read(t6)
                    prbR.did_write(t6)
                    t7 = s.op("DVE", lambda e: e.reduce_max(
                        stc(3), pr3, AX.X), waits=[t6])
                    prbR.did_read(t7)
                    sttR.did_write(t7)
                    t8 = s.op("DVE", lambda e: e.tensor_tensor(
                        mk3, pr3, bc(3), ALU.is_ge), waits=[t7] + mskbR.wr())
                    prbR.did_read(t8)
                    sttR.did_read(t8)
                    mskbR.did_write(t8)
                    t9 = s.op("DVE", lambda e: e.scalar_tensor_tensor(
                        l23, mskb[:], -1e30, prb[:], ALU.mult, ALU.add),
                        waits=[t8] + lg2R.wr())
                    mskbR.did_read(t9)
                    prbR.did_read(t9)
                    lg2R.did_write(t9)
                    t10 = s.op("DVE", lambda e: e.reduce_max(
                        stc(4), l23, AX.X), waits=[t9])
                    lg2R.did_read(t10)
                    sttR.did_write(t10)
                    t11 = s.op("DVE", lambda e: e.tensor_tensor(
                        mk3, pr3, bc(4), ALU.is_ge), waits=[t10] + mskbR.wr())
                    prbR.did_read(t11)
                    sttR.did_read(t11)
                    mskbR.did_write(t11)
                    t12 = s.op("DVE", lambda e: e.tensor_tensor(
                        stc(5), stc(3), stc(4), ALU.add), waits=[t10, t7])
                    sttR.did_write(t12)
                    t13 = s.op("DVE", lambda e: e.reciprocal(
                        stc(6), stc(5)), waits=[t12])
                    sttR.did_write(t13)
                    t14 = s.op("DVE", lambda e: e.tensor_scalar(
                        stc(5), stc(6), 2.0, None, ALU.mult), waits=[t13])
                    sttR.did_write(t14)
                    t15 = s.op("DVE", lambda e: e.tensor_tensor(
                        xp3, mk3, pr3, ALU.mult), waits=[t11] + xpbR.wr())
                    mskbR.did_read(t15)
                    prbR.did_read(t15)
                    xpbR.did_write(t15)
                    t16 = s.op("DVE", lambda e: e.tensor_tensor(
                        cm3, xp3, bc(5), ALU.mult),
                        waits=[t15, t14] + cmbR.wr())
                    xpbR.did_read(t16)
                    sttR.did_read(t16)
                    cmbR.did_write(t16)
                    for qc in range(8):
                        tt = s.op("PE", lambda e, o=pf[1].ap()[0:8, 0:128],
                                  i=cmb[:, qc * 8: qc * 8 + 8], idn=id32[:]:
                                  e.transpose(o, i, idn),
                                  waits=[t16] + pfR[1].wr())
                        cmbR.did_read(tt)
                        pfR[1].did_write(tt)
                        tc3 = s.op("DVE",
                                   lambda e, d=cmT[:, qc * 128: qc * 128 + 128],
                                   srd=pf[1].ap()[0:8, 0:128]:
                                   e.tensor_copy(d, srd),
                                   waits=[tt] + (cmTR.wr() if qc == 0 else []))
                        pfR[1].did_read(tc3)
                        cmTR.did_write(tc3)
                    tex = None
                    for sub in range(2):
                        tex = s.op("PE", mm(
                            pf[1].ap()[0:128, sub * 512: sub * 512 + 512],
                            e16[:], cmT[:, sub * 512: sub * 512 + 512],
                            True, True),
                            waits=cmTR.rd() +
                            (pfR[1].wr() if sub == 0 else []) + consts_fence,
                            tok=(sub == 1))
                    pfR[1].did_write(tex)
                    cmTR.did_read(tex)
                    tce = s.op("DVE", lambda e, d=ce[:],
                               srd=pf[1].ap()[0:128, 0:1024]:
                               e.tensor_copy(d, srd), waits=[tex] + ceR.wr())
                    pfR[1].did_read(tce)
                    ceR.did_write(tce)
                    ths = s.op("DVE", lambda e, d=hs[:], a=pf[0][:], b=ce[:]:
                               e.tensor_tensor(d, a, b, ALU.mult),
                               waits=[t_loraA] + ceR.rd() + hsR.wr())
                    pfR[0].did_read(ths)
                    ceR.did_read(ths)
                    hsR.did_write(ths)
                    wbT_r = wbT.ap().rearrange("(c p) n -> p c n", p=128)
                    for wq in range(16):
                        wb = wq % 2
                        for _ in range(2):
                            if apply_pend:
                                emit_apply(*apply_pend.pop(0))
                        for g4 in range(8):
                            st_i = g4 % 2
                            src = wbT_r[:, 4 * g4:4 * g4 + 4,
                                        wq * 256: wq * 256 + 256]
                            tl = s.op("SP",
                                      lambda e, d=wstg[st_i].ap().rearrange(
                                          "p (c n) -> p c n", n=256)[:, 0:4, :],
                                      src=src: e.dma_start(d, src),
                                      waits=wstgR[st_i].wr(), dma="ld")
                            wstgR[st_i].did_write(tl)
                            tc = s.op("POOL",
                                      lambda e,
                                      d=wq_bf[wb][:, g4 * 1024: g4 * 1024 + 1024],
                                      srd=wstg[st_i][:, 0:1024]:
                                      e.tensor_copy(d, srd),
                                      waits=[tl] +
                                      [tk for j in range(4)
                                       for tk in wqR[wb][4 * g4 + j].wr()])
                            wstgR[st_i].did_read(tc)
                            for j in range(4):
                                wqR[wb][4 * g4 + j].did_write(tc)
                        if wq == 0:
                            fpend = []
                            def emit_fstats(mc, te, ts, ri):
                                for sub in range(2):
                                    t_s = s.op("PE", mm(
                                        pss[0:1, sub * 512: sub * 512 + 512],
                                        onescol[:, 0:1],
                                        res2c[ri][:, sub * 512: sub * 512 + 512],
                                        mc == 0, mc == NDC - 1),
                                        waits=[te] +
                                        (pssR.wr() if mc == 0 and sub == 0 else []))
                                    t_q = s.op("PE", mm(
                                        psq[0:1, sub * 512: sub * 512 + 512],
                                        onescol[:, 1:2],
                                        sqc[ri][:, sub * 512: sub * 512 + 512],
                                        mc == 0, mc == NDC - 1),
                                        waits=[ts] +
                                        (psqR.wr() if mc == 0 and sub == 0 else []))
                                    res2cR[ri].did_read(t_s)
                                    sqcR[ri].did_read(t_q)
                                    if mc == NDC - 1 and sub == 1:
                                        pssR.did_write(t_s)
                                        psqR.did_write(t_q)
                        for m4 in range(2):
                            mc = wq * 2 + m4
                            pi = m4 % 2
                            w0 = pfR[pi].wr()
                            last = None
                            for sub in range(2):
                                for dc in range(NDC):
                                    s.op("PE", mm(
                                        pf[pi][:, sub * 512: sub * 512 + 512],
                                        wq_bf[wb][:, dc * 256 + m4 * 128:
                                              dc * 256 + m4 * 128 + 128],
                                        x2h[:, dc * 1024 + sub * 512:
                                            dc * 1024 + sub * 512 + 512],
                                        dc == 0, False),
                                        waits=(w0 if dc == 0 and sub == 0
                                               else []) +
                                        wqR[wb][dc].rd() + x2hR[dc].rd(),
                                        tok=False)
                                last = s.op("PE", mm(
                                    pf[pi][:, sub * 512: sub * 512 + 512],
                                    br_bf[:, mc * 128: mc * 128 + 128],
                                    hs[:, sub * 512: sub * 512 + 512],
                                    False, True),
                                    waits=brR.rd() + hsR.rd(),
                                    tok=(sub == 1))
                            pfR[pi].did_write(last)
                            for dc in range(NDC):
                                wqR[wb][dc].did_read(last)
                                x2hR[dc].did_read(last)
                            brR.did_read(last)
                            hsR.did_read(last)
                            ri = mc % 2
                            te = s.op("DVE",
                                      lambda e, d=res2c[ri][:], a=pf[pi][:],
                                      b=bB_sb[:, mc:mc + 1],
                                      c2=x2h[:, mc * 1024: mc * 1024 + 1024]:
                                      e.scalar_tensor_tensor(d, a, b, c2,
                                                             ALU.add, ALU.add),
                                      waits=[last] + x2hR[mc].rd() +
                                      res2cR[ri].wr())
                            pfR[pi].did_read(te)
                            x2hR[mc].did_read(te)
                            res2cR[ri].did_write(te)
                            twr = s.op("ACT", lambda e, mc=mc, th=th, ri=ri:
                                       e.dma_start(
                                           r2scr_r[:, mc,
                                                   th * 1024: th * 1024 + 1024],
                                           res2c[ri][:]),
                                       waits=[te] + r2scrR[th][mc].wr(),
                                       dma="sca")
                            res2cR[ri].did_read(twr)
                            r2scrR[th][mc].did_write(twr)
                            ts = s.op("ACT", lambda e, d=sqc[ri][:],
                                      a=res2c[ri][:]:
                                      e.activation(d, a, AF.Square),
                                      waits=[te] + sqcR[ri].wr())
                            res2cR[ri].did_read(ts)
                            sqcR[ri].did_write(ts)
                            fpend.append((mc, te, ts, ri))
                            if len(fpend) > 1:
                                emit_fstats(*fpend.pop(0))
                            if mc == NDC - 1:
                                while fpend:
                                    emit_fstats(*fpend.pop(0))
                    ro2, trs, tb2 = ln_rows(pss[0:1, :], psq[0:1, :],
                                            pssR, psqR, 1024, 0)
                    tba = None
                    for sub in range(2):
                        tba = s.op("PE", mm(
                            pf[0][:, sub * 512: sub * 512 + 512], ones1f[:],
                            rows[0:1, 1024 + sub * 512: 1024 + sub * 512 + 512],
                            True, True),
                            waits=[trs] + (pfR[0].wr() if sub == 0 else []),
                            tok=(sub == 1))
                    rowsRes[1].did_read(tba)
                    pfR[0].did_write(tba)
                    tbb = None
                    for sub in range(2):
                        tbb = s.op("PE", mm(
                            pf[1][:, sub * 512: sub * 512 + 512], ones1f[:],
                            rows[0:1, sub * 512: sub * 512 + 512],
                            True, True),
                            waits=[tb2] + (pfR[1].wr() if sub == 0 else []),
                            tok=(sub == 1))
                    rowsRes[0].did_read(tbb)
                    pfR[1].did_write(tbb)
                    tca = s.op("DVE", lambda e, d=a2[:], srd=pf[0][:]:
                               e.tensor_copy(d, srd), waits=[tba] + a2R.wr())
                    pfR[0].did_read(tca)
                    a2R.did_write(tca)
                    tcb = s.op("DVE", lambda e, d=b2[:], srd=pf[1][:]:
                               e.tensor_copy(d, srd), waits=[tbb] + b2R.wr())
                    pfR[1].did_read(tcb)
                    b2R.did_write(tcb)
                    ta2, tb2c = tca, tcb
                    for mc in range(NDC):
                        apply_pend.append((th, mc, ta2, tb2c))
                    if th == 1:
                        while apply_pend:
                            emit_apply(*apply_pend.pop(0))

        s.finish()
        s.run()
    return nc


_NC_CACHE = {}


def kernel(L0_attn_x, L0_attn_Wqk_w, L0_attn_Wqk_b, L0_attn_Wv_w, L0_attn_Wv_b,
           L0_attn_Wo_w, L0_attn_Wo_b, L0_ffn_W_base, L0_ffn_b_base,
           L0_ffn_W_router, L0_ffn_A_stack, L0_ffn_B_stack):
    x = np.asarray(L0_attn_x, np.float32)
    Wqk = np.asarray(L0_attn_Wqk_w, np.float32)
    bqk = np.asarray(L0_attn_Wqk_b, np.float32)
    Wv = np.asarray(L0_attn_Wv_w, np.float32)
    bv = np.asarray(L0_attn_Wv_b, np.float32)
    Wo = np.asarray(L0_attn_Wo_w, np.float32)
    bo = np.asarray(L0_attn_Wo_b, np.float32)
    WB = np.asarray(L0_ffn_W_base, np.float32)
    bB = np.asarray(L0_ffn_b_base, np.float32)
    Wr = np.asarray(L0_ffn_W_router, np.float32)
    A = np.asarray(L0_ffn_A_stack, np.float32)
    Bs = np.asarray(L0_ffn_B_stack, np.float32)

    C = np.ascontiguousarray
    Wqk_r = Wqk.reshape(H, E, 2, D)
    bqk_r = bqk.reshape(H, E, 2)
    consts = {
        "wqT": C(Wqk_r[:, :, 0, :].reshape(H * E, D).T),
        "wkT": C(Wqk_r[:, :, 1, :].reshape(H * E, D).T),
        "wvT": C(Wv.T),
        "woT": C(Wo.T),
        "wbT": C(WB.T),
        "wrT": C(Wr.T),
        "aallT": C(A.reshape(128, D).T),
        "ballT": C(Bs.transpose(0, 2, 1).reshape(128, D)),
        "c_bq": C(bqk_r[:, :, 0].reshape(4, 128).T),
        "c_bk": C(bqk_r[:, :, 1].reshape(4, 128).T),
        "c_bo": C(bo.reshape(32, 128).T),
        "c_bB": C(bB.reshape(32, 128).T),
        "c_bv": C(bv.reshape(1, 512)),
        "c_id32": np.eye(128, dtype=np.float32),
        "c_ones65": np.ones((65, 128), NPBF),
        "c_ones1f": np.ones((1, 128), np.float32),
        "c_o65f": np.ones((65, 64), np.float32),
        "c_onescol": np.ones((128, 2), NPBF),
        "c_e16": np.repeat(np.eye(8, dtype=np.float32), 16, axis=1).astype(NPBF),
        "c_eps": np.full((1, 1), 1e-5, np.float32),
    }

    in_maps = []
    for c in range(8):
        b, half = c // 2, c % 2
        xT = C(x[b].T)
        if half:
            xrot = C(np.concatenate([xT[:, 2048:], xT[:, :2048]], axis=1))
        else:
            xrot = xT
        m = dict(consts)
        m["xrot"] = xrot
        in_maps.append(m)

    if "prog" not in _NC_CACHE:
        _NC_CACHE["prog"] = build_program()
    nc = _NC_CACHE["prog"]
    res = run_bass_kernel_spmd(nc, in_maps, core_ids=list(range(8)))
    out = np.empty_like(x)
    for c in range(8):
        b, half = c // 2, c % 2
        outT_np = np.asarray(res.results[c]["outT"], np.float32)
        out[b, half * T:(half + 1) * T, :] = outT_np.T
    return out
